# version 12
# baseline (speedup 1.0000x reference)
import numpy as np
from contextlib import ExitStack

import concourse.bass as bass
import concourse.mybir as mybir
import concourse.bass_utils as _bass_utils
from concourse.bass_utils import run_bass_kernel_spmd

if not getattr(_bass_utils, "_ldw_opt_patched", False):
    _orig_run_command = _bass_utils.run_command

    def _run_command_ldw(argv, **kwargs):
        argv = ["--enable-ldw-opt=true" if a == "--enable-ldw-opt=false" else a
                for a in argv]
        return _orig_run_command(argv, **kwargs)

    _bass_utils.run_command = _run_command_ldw
    _bass_utils._ldw_opt_patched = True

F32 = mybir.dt.float32
BF16 = mybir.dt.bfloat16
AF = mybir.ActivationFunctionType
ALU = mybir.AluOpType
AX = mybir.AxisListType

S, M, D, H = 4096, 2048, 1024, 16
NE, NQ = 8, 4
EPS = 1e-5
MSA_SCALE = 0.125
MSA_B = 12.0
NCF = S // 128
NCO = M // 128


class Ctr:
    def __init__(self, nc, ctx, name):
        self.h = ctx.enter_context(nc.semaphore(name))
        self.n = 0

    def inc(self, instr, k=1):
        instr.then_inc(self.h, k)
        self.n += k
        return instr

    def dinc(self, instr):
        return self.inc(instr, 16)


def build_program(taps=(), until=None):
    import ml_dtypes
    nc = bass.Bass("TRN2", target_bir_lowering=False, detect_race_conditions=False)

    dx = nc.dram_tensor("x", [S, D], F32, kind="ExternalInput").ap()
    dwqT = nc.dram_tensor("wqT", [D, D], F32, kind="ExternalInput").ap()
    dwkT = nc.dram_tensor("wkT", [D, D], F32, kind="ExternalInput").ap()
    dwvT = nc.dram_tensor("wvT", [D, D], F32, kind="ExternalInput").ap()
    dwfqT = nc.dram_tensor("wfqT", [D, D], F32, kind="ExternalInput").ap()
    dwfkT = nc.dram_tensor("wfkT", [D, D], F32, kind="ExternalInput").ap()
    dwqkv = nc.dram_tensor("wqkv", [D, 3 * D], F32, kind="ExternalInput").ap()
    dwproj = nc.dram_tensor("wprojp", [128, 8, D], F32, kind="ExternalInput").ap()
    dgatew = nc.dram_tensor("gatew", [D, NE], F32, kind="ExternalInput").ap()
    dw1q = nc.dram_tensor("w1q", [NE, NQ, D, 1024], F32, kind="ExternalInput").ap()
    dw2q = nc.dram_tensor("w2q", [NE, NQ, 1024, D], F32, kind="ExternalInput").ap()
    dvecs = {nm: nc.dram_tensor(nm, [1, D], F32, kind="ExternalInput").ap()
             for nm in ("bq", "bk", "bv", "bfq", "bfk", "ln1w", "ln1b", "ln2w", "ln2b")}
    dgateb = nc.dram_tensor("gateb", [1, NE], F32, kind="ExternalInput").ap()
    dy = nc.dram_tensor("y", [M, D], F32, kind="ExternalOutput").ap()
    dtaps = {}
    for t in taps:
        shp, dt = {"x1fm": ([128, 8, S], BF16), "kv": ([128, NE, 132], BF16),
                   "ctxn": ([128, 8, M], BF16), "y1": ([128, NCO, D], F32),
                   "x2fm": ([128, 8, M], BF16), "masks": ([128, NCO, NE], F32),
                   "acc5": ([128, NCO, D], F32)}[t]
        dtaps[t] = nc.dram_tensor("t_" + t, shp, dt, kind="ExternalOutput").ap()

    ident_dram = nc.inline_tensor(
        np.eye(128, dtype=np.float32).astype(ml_dtypes.bfloat16), name="identc")
    ddbg = (nc.dram_tensor("dbg", [128, NE, 65], BF16, kind="ExternalOutput").ap()
            if until is not None else None)
    ddbg2 = {}
    if until == 1:
        for nm, sh, dt in (("xn_bf", [128, 2, D], BF16), ("khat", [128, 2, D], BF16),
                           ("kescr", [128, 2, D], BF16), ("ksum", [128, 2, 1], F32),
                           ("negmax", [128, 2, 1], F32), ("var", [128, 2, 1], F32),
                           ("rstd", [128, 2, 1], F32), ("kp1", [128, 2, D], BF16),
                           ("vsb", [128, 2, H, 65], BF16), ("kvacc", [128, NE, 65], F32),
                           ("xnfm", [128, 2, D], BF16), ("x1bf", [128, 2, D], BF16)):
            ddbg2[nm] = nc.dram_tensor("dbg_" + nm, sh, dt, kind="ExternalOutput").ap()

    with ExitStack() as top:
        SBL = lambda n, sh, dt=F32: top.enter_context(nc.sbuf_tensor(n, sh, dt, side="left"))
        ident = SBL("ident", [128, 128], BF16)
        ones2 = SBL("ones2", [128, 512], BF16)
        onesf = SBL("onesf", [1, 128], F32)
        vstage = SBL("vstage", [1, D], F32)
        bq_bf = SBL("bq_bf", [1, D], BF16)
        bk_bf = SBL("bk_bf", [1, D], BF16)
        bv_bf = SBL("bv_bf", [1, D], BF16)
        bfq_bf = SBL("bfq_bf", [1, D], BF16)
        bfk_bf = SBL("bfk_bf", [1, D], BF16)
        w1ln_bc = SBL("w1ln_bc", [128, D], F32)
        b1ln_bc = SBL("b1ln_bc", [128, D], F32)
        w2ln_bc = SBL("w2ln_bc", [128, D], F32)
        b2ln_bc = SBL("b2ln_bc", [128, D], F32)
        gb_bc = SBL("gb_bc", [128, NE], F32)
        kv_sb = SBL("kv_sb", [128, NE, 132], BF16)
        kvacc = SBL("kvacc", [128, NE, 65], F32)
        epsc = SBL("epsc", [128, 1], F32)
        eps2c = SBL("eps2c", [128, 1], F32)
        mnegb = SBL("mnegb", [128, 1], F32)
        masks = SBL("masks", [128, NCO, NE], F32)

        sy = Ctr(nc, top, "sy")
        gp = Ctr(nc, top, "gp")
        mm = Ctr(nc, top, "mm")
        ac = Ctr(nc, top, "ac")
        dv = Ctr(nc, top, "dv")
        smid = Ctr(nc, top, "smid")
        smx = [Ctr(nc, top, "smx0"), Ctr(nc, top, "smx1")]
        smy = [Ctr(nc, top, "smy0"), Ctr(nc, top, "smy1")]
        smg = [Ctr(nc, top, "smg0"), Ctr(nc, top, "smg1")]
        gpp = [Ctr(nc, top, "gpp0"), Ctr(nc, top, "gpp1")]
        gp1 = Ctr(nc, top, "gp1")
        gp2 = Ctr(nc, top, "gp2")
        vs = Ctr(nc, top, "vs")

        block = top.enter_context(nc.Block())

        with ExitStack() as p0:
            bcps = p0.enter_context(nc.psum_tensor("bcps", [128, 2, 512], F32))

            @block.sync
            def _(sync):
                smid.dinc(sync.dma_start(out=ident[:], in_=ident_dram.ap()))

            @block.vector
            def _(v):
                v.memset(ones2[:], 1.0)
                v.memset(epsc[:], EPS)
                v.memset(eps2c[:], EPS * EPS)
                v.memset(mnegb[:], -MSA_B)
                v.memset(kv_sb[:], 0.0)
                dv.inc(v.memset(onesf[:], 1.0))
            ones_dv = dv.n

            bc_list = [("ln1w", w1ln_bc, D), ("ln1b", b1ln_bc, D),
                       ("ln2w", w2ln_bc, D), ("ln2b", b2ln_bc, D),
                       ("gateb", gb_bc, NE)]
            nseg = lambda w: (w + 511) // 512
            gp_b0, mm_b0, dv_b0 = gp.n, mm.n, dv.n
            cum = [0]
            for _, _, w in bc_list:
                cum.append(cum[-1] + nseg(w))

            @block.gpsimd
            def _(g):
                for nm, dst in (("bq", bq_bf), ("bk", bk_bf), ("bv", bv_bf),
                                ("bfq", bfq_bf), ("bfk", bfk_bf)):
                    gp.dinc(g.dma_start(out=dst[:], in_=dvecs[nm]))
                    g.wait_ge(gp.h, gp.n)
                for i, (nm, dst, w) in enumerate(bc_list):
                    if i > 0:
                        g.wait_ge(mm.h, mm_b0 + cum[i])
                    gp.dinc(g.dma_start(out=vstage[:, 0:w], in_=(
                        dvecs[nm] if w == D else dgateb)))
                    g.wait_ge(gp.h, gp.n)

            @block.tensor
            def _(t):
                t.wait_ge(dv.h, ones_dv)
                for i, (nm, dst, w) in enumerate(bc_list):
                    t.wait_ge(gp.h, gp_b0 + 16 * (5 + i + 1))
                    if i > 0:
                        t.wait_ge(dv.h, dv_b0 + i)
                    for si, j in enumerate(range(0, w, 512)):
                        ws = min(512, w - j)
                        mm.inc(t.matmul(bcps[:, si % 2, 0:ws], onesf[:],
                                        vstage[:, j:j + ws], start=True, stop=True))

            @block.vector
            def _(v):
                for i, (nm, dst, w) in enumerate(bc_list):
                    v.wait_ge(mm.h, mm_b0 + cum[i + 1])
                    for si, j in enumerate(range(0, w, 512)):
                        ws = min(512, w - j)
                        ins = v.tensor_copy(dst[:, j:j + ws], bcps[:, si % 2, 0:ws])
                    dv.inc(ins)
            assert dv.n == dv_b0 + 5 and mm.n == mm_b0 + cum[-1]

        pctx = top.enter_context(ExitStack())
        with ExitStack() as px1:
            x1_fm = px1.enter_context(nc.sbuf_tensor("x1_fm", [128, 8, S], BF16, side="left"))

            with ExitStack() as p1:
                R = lambda n, sh, dt=F32: p1.enter_context(nc.sbuf_tensor(n, sh, dt, side="right"))
                wk_sb = R("wk_sb", [128, 8, D], BF16)
                wfk_sb = R("wfk_sb", [128, 8, D], BF16)
                wv_sb = R("wv_sb", [128, 8, D], BF16)
                xbuf = R("xbuf", [128, 2, D])
                sqd = R("sqd", [128, D], BF16)
                ss = R("ss", [128, 2, 1]); s_ = R("s_", [128, 2, 1])
                negm = R("negm", [128, 2, 1]); msq = R("msq", [128, 2, 1])
                t1_ = R("t1_", [128, 2, 1]); var = R("var", [128, 2, 1])
                lnv = R("lnv", [128, 2, 1]); rstd = R("rstd", [128, 2, 1])
                r01 = R("r01", [128, 2, 1])
                xn_tm = R("xn_tm", [128, 2, D])
                xn_bf = R("xn_bf", [128, 2, D], BF16)
                x1b = R("x1b", [128, 1, D])
                x1pre = R("x1pre", [128, 1, D])
                x1bf = R("x1bf", [128, 2, D], BF16)
                xnfm = R("xnfm", [128, 2, D], BF16)
                kp1 = R("kp1", [128, 2, D], BF16)
                negmax = R("negmax", [128, 2, 1])
                kescr = R("kescr", [128, 2, D], BF16)
                ksum = R("ksum", [128, 2, 1]); rk = R("rk", [128, 2, 1])
                khat = R("khat", [128, 2, D], BF16)
                vsb = R("vsb", [128, 2, H, 65], BF16)

                PP = lambda n, sh, dt=F32: p1.enter_context(nc.psum_tensor(n, sh, dt))
                kvA = PP("kvA", [128, 4, 128])
                kvB = PP("kvB", [128, 4, 128])
                xnT = PP("xnT", [128, D], BF16)
                x1T = PP("x1T", [128, D], BF16)
                psC = PP("psC", [128, D])
                psD = PP("psD", [128, D])

                @block.gpsimd
                def _(g):
                    for wsrc, wdst in ((dwkT, wk_sb), (dwfkT, wfk_sb), (dwvT, wv_sb)):
                        for fc in range(8):
                            gp.dinc(g.dma_start(out=wdst[:, fc, :],
                                                in_=wsrc[fc * 128:(fc + 1) * 128, :]))
                p1w_gp = gp.n

                @block.vector
                def _(v):
                    dv.inc(v.memset(vsb[:, :, :, 64:65], 1.0))

                mm0, ac0, dv0 = mm.n, ac.n, dv.n
                _cumv = lambda c: 0 if c == 0 else (5 if c == 1 else 14 + 10 * (c - 2))
                m_var = lambda c: dv0 + _cumv(c) + 1
                m_xnbf = lambda c: dv0 + _cumv(c) + 2
                m_x1bf = lambda c: dv0 + _cumv(c) + 3
                m_xnfm = lambda c: dv0 + _cumv(c) + (5 if c >= 2 else 4)
                m_x1fm = lambda c: dv0 + _cumv(c) + (6 if c >= 2 else 5)
                m_kp1e = lambda j: dv0 + (_cumv(NCF) + 1 if j == NCF - 1
                                          else _cumv(j + 1) + (7 if j >= 1 else 6))
                m_ngmx = lambda j: m_kp1e(j) + 1
                m_vevc = lambda j: m_kp1e(j) + 2
                m_khat = lambda j: m_kp1e(j) + 3
                m_kvac = lambda j: dv0 + (_cumv(j + 2) + 4 if j <= NCF - 3
                                          else _cumv(NCF) + (5 if j == NCF - 2 else 6))
                _cuma = lambda c: 0 if c == 0 else 5 + 6 * (c - 1)
                m_sq = lambda c: ac0 + _cuma(c) + 1
                m_rstd = lambda c: ac0 + _cuma(c) + 3
                m_r01 = lambda c: ac0 + _cuma(c) + 5
                m_kexp = lambda j: ac0 + (_cuma(j + 1) + 6 if j < NCF - 1 else 6 * NCF)
                xmark = {}
                mmk = {}

                @block.sync
                def _(sync):
                    for c in range(NCF):
                        if c >= 2:
                            sync.wait_ge(dv.h, m_x1bf(c - 2))
                            sync.wait_ge(ac.h, m_sq(c - 2))
                        smx[c % 2].dinc(sync.dma_start(out=xbuf[:, c % 2, :],
                                                       in_=dx[c * 128:(c + 1) * 128, :]))
                        xmark[c] = smx[c % 2].n

                @block.tensor
                def _(t):
                    t.wait_ge(gp.h, p1w_gp)
                    t.wait_ge(smid.h, 16)

                    def kv_mm(cc):
                        rr = cc % 2
                        t.wait_ge(dv.h, m_khat(cc))
                        if cc >= 1:
                            t.wait_ge(dv.h, m_kvac(cc - 1))
                        for p in range(8):
                            kvp = kvA if p < 4 else kvB
                            sl2 = p % 4
                            ha, hb = 2 * p, 2 * p + 1
                            t.matmul(kvp[0:64, sl2, 0:65],
                                     khat[:, rr, ha * 64:ha * 64 + 64],
                                     vsb[:, rr, ha, :], start=True, stop=True,
                                     tile_position=(0, 0))
                            ins = t.matmul(kvp[64:128, sl2, 0:65],
                                           khat[:, rr, hb * 64:hb * 64 + 64],
                                           vsb[:, rr, hb, :], start=True,
                                           stop=True, tile_position=(0, 64))
                        mm.inc(ins)
                        mmk[("kv", cc)] = mm.n

                    for c in range(NCF + 1):
                        if c < NCF:
                            r = c % 2
                            t.wait_ge(dv.h, m_xnbf(c))
                            if c >= 1:
                                t.wait_ge(dv.h, m_xnfm(c - 1))
                            for j in range(8):
                                ins = t.transpose(xnT[:, j * 128:(j + 1) * 128],
                                                  xn_bf[:, r, j * 128:(j + 1) * 128],
                                                  ident[:])
                            mm.inc(ins)
                            mmk[("xnT", c)] = mm.n
                        if c >= 1:
                            cc, rr = c - 1, (c - 1) % 2
                            t.wait_ge(dv.h, m_xnfm(cc))
                            if cc >= 1:
                                t.wait_ge(dv.h, m_vevc(cc - 1))
                            for jc in range(8):
                                sl = slice(jc * 128, (jc + 1) * 128)
                                for fc in range(8):
                                    t.matmul(psC[:, sl], wk_sb[:, fc, sl],
                                             xnfm[:, rr, fc * 128:(fc + 1) * 128],
                                             start=(fc == 0), stop=False)
                                ins = t.matmul(psC[:, sl], bk_bf[:, sl],
                                               ones2[0:1, 0:128], start=False, stop=True)
                            mm.inc(ins)
                            mmk[("kpre", cc)] = mm.n
                        if c < NCF:
                            r = c % 2
                            t.wait_ge(dv.h, m_x1bf(c))
                            if c >= 1:
                                t.wait_ge(dv.h, m_x1fm(c - 1))
                            for j in range(8):
                                ins = t.transpose(x1T[:, j * 128:(j + 1) * 128],
                                                  x1bf[:, r, j * 128:(j + 1) * 128],
                                                  ident[:])
                            mm.inc(ins)
                            mmk[("x1T", c)] = mm.n
                        if c >= 2:
                            kv_mm(c - 2)
                        if c >= 1:
                            cc, rr = c - 1, (c - 1) % 2
                            t.wait_ge(dv.h, m_kp1e(cc))
                            if cc >= 1:
                                t.wait_ge(ac.h, m_kexp(cc - 1))
                            for nt in range(2):
                                sl = slice(nt * 512, (nt + 1) * 512)
                                for jc in range(8):
                                    t.matmul(psD[:, sl],
                                             kp1[:, rr, jc * 128:(jc + 1) * 128],
                                             wfk_sb[:, jc, sl], start=(jc == 0),
                                             stop=False)
                                ins = t.matmul(psD[:, sl], ones2[0:1, 0:128],
                                               bfk_bf[:, sl], start=False, stop=True)
                            mm.inc(ins)
                            mmk[("kfk", cc)] = mm.n
                            for nt in range(2):
                                sl = slice(nt * 512, (nt + 1) * 512)
                                for fc in range(8):
                                    t.matmul(psC[:, sl],
                                             xnfm[:, rr, fc * 128:(fc + 1) * 128],
                                             wv_sb[:, fc, sl], start=(fc == 0),
                                             stop=False)
                                ins = t.matmul(psC[:, sl], ones2[0:1, 0:128],
                                               bv_bf[:, sl], start=False, stop=True)
                            mm.inc(ins)
                            mmk[("v", cc)] = mm.n
                    kv_mm(NCF - 1)
                assert mm.n == mm0 + 6 * NCF

                @block.scalar
                def _(sc):
                    def kexp_op(sc, j):
                        rr = j % 2
                        sc.wait_ge(mm.h, mmk[("kfk", j)])
                        sc.wait_ge(dv.h, m_ngmx(j))
                        ac.inc(sc.activation(out=kescr[:, rr, :], in_=psD[:],
                                             func=AF.Exp, bias=negmax[:, rr, :],
                                             accum_out=ksum[:, rr, :]))

                    for c in range(NCF):
                        r = c % 2
                        sc.wait_ge(smx[r].h, xmark[c])
                        ac.inc(sc.activation(out=sqd[:], in_=xbuf[:, r, :],
                                             func=AF.Square, accum_out=ss[:, r, :]))
                        sc.wait_ge(dv.h, m_var(c))
                        ac.inc(sc.activation(out=lnv[:, r, :], in_=var[:, r, :],
                                             func=AF.Ln, bias=epsc[:]))
                        ac.inc(sc.activation(out=rstd[:, r, :], in_=lnv[:, r, :],
                                             func=AF.Exp, scale=-0.5))
                        ac.inc(sc.activation(out=lnv[:, r, :], in_=var[:, r, :],
                                             func=AF.Ln, scale=1.0 + EPS, bias=eps2c[:]))
                        ac.inc(sc.activation(out=r01[:, r, :], in_=lnv[:, r, :],
                                             func=AF.Exp, scale=-0.5))
                        if c >= 1:
                            kexp_op(sc, c - 1)
                    kexp_op(sc, NCF - 1)
                assert ac.n == ac0 + 6 * NCF

                @block.vector
                def _(v):
                    def kvac_add(v, cc):
                        v.wait_ge(mm.h, mmk[("kv", cc)])
                        if cc >= 1:
                            v.wait_ge(dv.h, m_kvac(cc - 1))
                        if cc == 0:
                            v.tensor_copy(kvacc[:, 0:4, :], kvA[:, :, 0:65])
                            dv.inc(v.tensor_copy(kvacc[:, 4:8, :], kvB[:, :, 0:65]))
                        else:
                            v.tensor_tensor(kvacc[:, 0:4, :], kvA[:, :, 0:65],
                                            kvacc[:, 0:4, :], op=ALU.add)
                            dv.inc(v.tensor_tensor(kvacc[:, 4:8, :], kvB[:, :, 0:65],
                                                   kvacc[:, 4:8, :], op=ALU.add))

                    def ktail(v, j):
                        rr = j % 2
                        v.wait_ge(mm.h, mmk[("kpre", j)])
                        dv.inc(v.tensor_copy(kp1[:, rr, :], psC[:]))
                        v.wait_ge(mm.h, mmk[("kfk", j)])
                        dv.inc(v.reduce_max(negmax[:, rr, :], psD[:], axis=AX.X,
                                            negate=True))
                        v.wait_ge(mm.h, mmk[("v", j)])
                        dv.inc(v.tensor_copy(
                            vsb[:, rr, :, 0:64],
                            psC[:].rearrange("p (h d) -> p h d", h=H)))
                        v.wait_ge(ac.h, m_kexp(j))
                        vs.inc(v.reciprocal(rk[:, rr, :], ksum[:, rr, :]))
                        c_rk = vs.n
                        if j >= 2:
                            v.wait_ge(mm.h, mmk[("kv", j - 2)])
                        v.wait_ge(vs.h, c_rk)
                        dv.inc(v.tensor_scalar_mul(khat[:, rr, :], kescr[:, rr, :],
                                                   rk[:, rr, :]))

                    for c in range(NCF):
                        r = c % 2
                        v.wait_ge(smx[r].h, xmark[c])
                        vs.inc(v.reduce_sum(s_[:, r, :], xbuf[:, r, :], axis=AX.X))
                        c_s = vs.n
                        v.wait_ge(ac.h, m_sq(c))
                        vs.inc(v.tensor_scalar_mul(t1_[:, r, :], ss[:, r, :], 1.0 / D))
                        v.wait_ge(vs.h, c_s)
                        vs.inc(v.tensor_scalar_mul(negm[:, r, :], s_[:, r, :], -1.0 / D))
                        c_negm = vs.n
                        v.wait_ge(vs.h, c_negm)
                        vs.inc(v.tensor_mul(msq[:, r, :], negm[:, r, :], negm[:, r, :]))
                        v.wait_ge(vs.h, vs.n)
                        dv.inc(v.tensor_sub(var[:, r, :], t1_[:, r, :], msq[:, r, :]))
                        v.wait_ge(ac.h, m_rstd(c))
                        vs.inc(v.tensor_scalar(xn_tm[:, r, :], xbuf[:, r, :],
                                               negm[:, r, :], rstd[:, r, :],
                                               op0=ALU.add, op1=ALU.mult))
                        v.wait_ge(ac.h, m_r01(c))
                        vs.inc(v.tensor_scalar(x1pre[:, 0, :], xbuf[:, r, :],
                                               negm[:, r, :], r01[:, r, :],
                                               op0=ALU.add, op1=ALU.mult))
                        v.wait_ge(vs.h, vs.n)
                        vs.inc(v.tensor_mul(x1b[:, 0, :], x1pre[:, 0, :], w1ln_bc[:]))
                        c_x1b = vs.n
                        dv.inc(v.tensor_copy(xn_bf[:, r, :], xn_tm[:, r, :]))
                        v.wait_ge(vs.h, c_x1b)
                        dv.inc(v.tensor_tensor(x1bf[:, r, :], x1b[:, 0, :],
                                               b1ln_bc[:], op=ALU.add))
                        if c >= 2:
                            kvac_add(v, c - 2)
                        v.wait_ge(mm.h, mmk[("xnT", c)])
                        dv.inc(v.tensor_copy(xnfm[:, r, :], xnT[:]))
                        v.wait_ge(mm.h, mmk[("x1T", c)])
                        dv.inc(v.tensor_copy(
                            x1_fm[:, :, c * 128:(c + 1) * 128],
                            x1T[:].rearrange("p (a b) -> p a b", a=8)))
                        if c >= 1:
                            ktail(v, c - 1)
                    ktail(v, NCF - 1)
                    kvac_add(v, NCF - 2)
                    kvac_add(v, NCF - 1)
                    v.wait_ge(dv.h, m_kvac(NCF - 1))
                    v.tensor_copy(kv_sb[0:64, :, 0:65], kvacc[0:64, :, :])
                    dv.inc(v.tensor_copy(kv_sb[64:128, :, 65:130], kvacc[64:128, :, :]))
                assert dv.n == dv0 + 10 * NCF + 1
            p1_dv, p1_mm = dv.n, mm.n

            if until == 1:
                dbg_srcs = dict(xn_bf=xn_bf, khat=khat, kescr=kescr, ksum=ksum,
                                negmax=negmax, var=var, rstd=rstd, kp1=kp1, vsb=vsb,
                                kvacc=kvacc, xnfm=xnfm, x1bf=x1bf)

                @block.sync
                def _(sync):
                    sync.wait_ge(dv.h, dv.n)
                    sync.wait_ge(mm.h, mm.n)
                    sync.wait_ge(ac.h, ac.n)
                    sync.wait_ge(gp.h, gp.n)
                    sy.dinc(sync.dma_start(out=ddbg, in_=kv_sb[:]))
                    for nm, src in dbg_srcs.items():
                        sy.dinc(sync.dma_start(out=ddbg2[nm], in_=src[:]))
                    sync.wait_ge(sy.h, sy.n)
                return nc

            if "x1fm" in dtaps:
                @block.sync
                def _(sync):
                    sync.wait_ge(dv.h, p1_dv)
                    sy.dinc(sync.dma_start(out=dtaps["x1fm"], in_=x1_fm[:]))
                    sy.dinc(sync.dma_start(out=dtaps["kv"], in_=kv_sb[:]))

            ctxn = pctx.enter_context(nc.sbuf_tensor("ctxn", [128, 8, M], BF16, side="right"))

            with ExitStack() as p2:
                R2 = lambda n, sh, dt=F32: p2.enter_context(nc.sbuf_tensor(n, sh, dt, side="right"))
                wqp = R2("wqp", [128, 2, 8, 128], BF16)
                wkp = R2("wkp", [128, 2, 8, 128], BF16)
                wvp = R2("wvp", [128, 2, 8, 128], BF16)
                k1p = R2("k1p", [128, S], BF16)
                q1p = R2("q1p", [128, M], BF16)
                v1p = R2("v1p", [128, 32, 196], BF16)
                pab = R2("pab", [128, 3, 1024], BF16)
                rabf = R2("rabf", [65, 2048], BF16)
                rbc = R2("rbc", [64, 2, 1024])
                stgb = R2("stgb", [64, 2, 1024], BF16)

                ctxps_a = p2.enter_context(nc.psum_tensor("ctx_a", [128, 1024], F32))
                ctxps_b = p2.enter_context(nc.psum_tensor("ctx_b", [128, 1024], F32))
                sps0 = p2.enter_context(nc.psum_tensor("sps0", [128, 1024], F32))
                sps1 = p2.enter_context(nc.psum_tensor("sps1", [128, 1024], F32))
                spsl = [sps0, sps1]

                @block.vector
                def _(v):
                    v.wait_ge(dv.h, p1_dv)
                    v.memset(v1p[:], 0.0)
                    v.memset(v1p[:, :, 64:65], 1.0)
                    dv.inc(v.memset(v1p[:, :, 129:130], 1.0))

                mm2, ac2, dv2 = mm.n, ac.n, dv.n
                MMP, ACP, DVP = 304, 128, 52
                mk_k1 = lambda p, st: mmb(p) + st + 1
                mk_q1 = lambda p, st: mmb(p) + 8 + st + 1
                mk_v1 = lambda p, c: mmb(p) + 12 + c + 1
                att0 = lambda p, tqh: mmb(p) + 44 + tqh * 130
                mk_sc = lambda p, tqh, it: att0(p, tqh) + (it + 1 if it < 2 else 2 * it - 1)
                mk_pv = lambda p, tqh, j: att0(p, tqh) + (2 * j + 4 if j < 62 else 65 + j)
                mk_bpa = lambda p, tqh: att0(p, tqh) + 129
                mk_bpb = lambda p, tqh: att0(p, tqh) + 130
                mk_exp = lambda p, tqh, it: acb(p) + tqh * 64 + it + 1
                mmb = lambda p: mm2 + MMP * p
                acb = lambda p: ac2 + ACP * p
                dvb = lambda p: dv2 + DVP * p
                mk_k1e = lambda p, st: dvb(p) + st + 1
                mk_q1e = lambda p, st: dvb(p) + 8 + st + 1
                mk_v1e = lambda p, c: dvb(p) + 12 + c + 1
                mk_nr = lambda p, tqh, j: dvb(p) + 44 + 4 * tqh + j
                pslot = lambda p, tqh, j: ((p * 2 + tqh) * 64 + j) % 3
                pldmark = {}
                sgmark = {}

                @block.gpsimd
                def _(g):
                    g.wait_ge(mm.h, p1_mm)
                    g.wait_ge(dv.h, p1_dv)
                    for p in range(8):
                        if p >= 2:
                            g.wait_ge(mm.h, mk_v1(p - 2, 31))
                        r = p % 2
                        for fc in range(8):
                            fr = slice(fc * 128, (fc + 1) * 128)
                            gpp[r].dinc(g.dma_start(out=wqp[:, r, fc, :],
                                                    in_=dwqkv[fr, 128 * p:128 * p + 128]))
                            gpp[r].dinc(g.dma_start(out=wkp[:, r, fc, :],
                                                    in_=dwqkv[fr, D + 128 * p:D + 128 * p + 128]))
                            gpp[r].dinc(g.dma_start(
                                out=wvp[:, r, fc, :],
                                in_=dwqkv[fr, 2 * D + 128 * p:2 * D + 128 * p + 128]))
                        pldmark[p] = gpp[r].n

                @block.sync
                def _(sync):
                    for p in range(8):
                        for tqh in range(2):
                            qsl = slice(tqh * 1024, (tqh + 1) * 1024)
                            sync.wait_ge(dv.h, mk_nr(p, tqh, 4))
                            smg[tqh].dinc(sync.dma_start(out=ctxn[64:128, p, qsl],
                                                         in_=stgb[:, tqh, :]))
                            sgmark[(p, tqh)] = smg[tqh].n

                @block.tensor
                def _(t):
                    for p in range(8):
                        r = p % 2
                        t.wait_ge(gpp[p % 2].h, pldmark[p])
                        t.wait_ge(dv.h, dv2 if p == 0 else mk_nr(p - 1, 1, 2))
                        for st in range(8):
                            ps = sps0[:, 0:512] if st % 2 == 0 else sps1[:, 0:512]
                            if st >= 2:
                                t.wait_ge(dv.h, mk_k1e(p, st - 2))
                            for fc in range(8):
                                ins = t.matmul(ps, wkp[:, r, fc, :],
                                               x1_fm[:, fc, st * 512:(st + 1) * 512],
                                               start=(fc == 0), stop=(fc == 7))
                            mm.inc(ins)
                        for st in range(4):
                            ps = sps0[:, 0:512] if st % 2 == 0 else sps1[:, 0:512]
                            t.wait_ge(dv.h, mk_k1e(p, 6 + st) if st < 2
                                      else mk_q1e(p, st - 2))
                            for fc in range(8):
                                ins = t.matmul(ps, wqp[:, r, fc, :],
                                               x1_fm[:, fc, st * 512:(st + 1) * 512],
                                               start=(fc == 0), stop=(fc == 7))
                            mm.inc(ins)
                        for c in range(32):
                            ps = sps0[:, 0:128] if c % 2 == 0 else sps1[:, 0:128]
                            t.wait_ge(dv.h, mk_q1e(p, 2 + c) if c < 2
                                      else mk_v1e(p, c - 2))
                            for fc in range(8):
                                ins = t.matmul(ps,
                                               x1_fm[:, fc, c * 128:(c + 1) * 128],
                                               wvp[:, r, fc, :],
                                               start=(fc == 0), stop=(fc == 7))
                            mm.inc(ins)
                        t.wait_ge(dv.h, mk_v1e(p, 31))

                        def pv(t, p, tqh, j):
                            jt, jc = j % 2, j // 2
                            slot = pslot(p, tqh, j)
                            nsl = slice(jt * 512, (jt + 1) * 512)
                            ins = t.matmul(ctxps_a[:, nsl],
                                           v1p[:, jc, 0:128],
                                           pab[:, slot, 0:512],
                                           start=(jc == 0), stop=(jc == 31))
                            ins = t.matmul(ctxps_b[:, nsl],
                                           v1p[:, jc, 65:193],
                                           pab[:, slot, 512:1024],
                                           start=(jc == 0), stop=(jc == 31))
                            mm.inc(ins)

                        for tqh in range(2):
                            for it in range(64):
                                tile, c = it % 2, it // 2
                                ping = it % 2
                                if it >= 2:
                                    t.wait_ge(ac.h, mk_exp(p, tqh, it - 2))
                                elif tqh == 1:
                                    t.wait_ge(ac.h, mk_exp(p, 0, 62 + it))
                                    if it == 0:
                                        t.wait_ge(dv.h, mk_nr(p, 0, 2))
                                csl = slice(c * 128, (c + 1) * 128)
                                n0 = tqh * 1024 + tile * 512
                                t.matmul(spsl[ping][:, 0:512],
                                         k1p[0:64, csl], q1p[0:64, n0:n0 + 512],
                                         start=True, stop=True, tile_position=(0, 0))
                                ins = t.matmul(spsl[ping][:, 512:1024],
                                               k1p[64:128, csl],
                                               q1p[64:128, n0:n0 + 512],
                                               start=True, stop=True,
                                               tile_position=(64, 0))
                                mm.inc(ins)
                                if it >= 2:
                                    if it == 2 and tqh == 1:
                                        t.wait_ge(dv.h, mk_nr(p, 0, 4))
                                    pv(t, p, tqh, it - 2)
                            for j in (62, 63):
                                t.wait_ge(ac.h, mk_exp(p, tqh, j))
                                pv(t, p, tqh, j)
                            t.wait_ge(dv.h, mk_nr(p, tqh, 1))
                            for nt in range(2):
                                ins = t.matmul(sps0[0:64, nt * 512:(nt + 1) * 512],
                                               ones2[64:65, 0:64],
                                               rabf[64:65, nt * 512:(nt + 1) * 512],
                                               start=True, stop=True,
                                               tile_position=(64, 0))
                            mm.inc(ins)
                            for nt in range(2):
                                ins = t.matmul(sps1[0:64, nt * 512:(nt + 1) * 512],
                                               ones2[64:65, 0:64],
                                               rabf[64:65, 1024 + nt * 512:1024 + (nt + 1) * 512],
                                               start=True, stop=True,
                                               tile_position=(64, 0))
                            mm.inc(ins)
                assert mm.n == mm2 + 8 * MMP

                @block.scalar
                def _(sc):
                    for p in range(8):
                        for tqh in range(2):
                            for it in range(64):
                                g = (p * 2 + tqh) * 64 + it
                                ping = it % 2
                                sc.wait_ge(mm.h, mk_sc(p, tqh, it))
                                if g >= 3:
                                    gj = g - 3
                                    sc.wait_ge(mm.h, mk_pv(gj // 128,
                                                           (gj // 64) % 2, gj % 64))
                                ac.inc(sc.activation(out=pab[:, g % 3, :],
                                                     in_=spsl[ping][:],
                                                     func=AF.Exp, scale=MSA_SCALE,
                                                     bias=mnegb[:]))
                assert ac.n == ac2 + 8 * ACP

                @block.vector
                def _(v):
                    for p in range(8):
                        for st in range(8):
                            ps = sps0[:, 0:512] if st % 2 == 0 else sps1[:, 0:512]
                            v.wait_ge(mm.h, mk_k1(p, st))
                            dv.inc(v.tensor_copy(k1p[:, st * 512:(st + 1) * 512], ps))
                        for st in range(4):
                            ps = sps0[:, 0:512] if st % 2 == 0 else sps1[:, 0:512]
                            v.wait_ge(mm.h, mk_q1(p, st))
                            dv.inc(v.tensor_copy(q1p[:, st * 512:(st + 1) * 512], ps))
                        for c in range(32):
                            ps = sps0[:, 0:128] if c % 2 == 0 else sps1[:, 0:128]
                            v.wait_ge(mm.h, mk_v1(p, c))
                            v.tensor_copy(v1p[:, c, 0:64], ps[:, 0:64])
                            dv.inc(v.tensor_copy(v1p[:, c, 65:129], ps[:, 64:128]))
                        for tqh in range(2):
                            qsl = slice(tqh * 1024, (tqh + 1) * 1024)
                            v.wait_ge(mm.h, mk_pv(p, tqh, 63))
                            with nc.allow_low_precision(reason="recip rounded to bf16"):
                                v.reciprocal(rabf[64:65, 0:1024], ctxps_a[64:65, :])
                                dv.inc(v.reciprocal(rabf[64:65, 1024:2048],
                                                    ctxps_b[64:65, :]))
                            v.wait_ge(mm.h, mk_bpa(p, tqh))
                            v.tensor_copy(rbc[:, 0, :], sps0[0:64, :])
                            v.wait_ge(mm.h, mk_bpb(p, tqh))
                            if p >= 1:
                                v.wait_ge(smg[tqh].h, sgmark[(p - 1, tqh)])
                            dv.inc(v.tensor_copy(rbc[:, 1, :], sps1[0:64, :]))
                            v.wait_ge(dv.h, mk_nr(p, tqh, 2))
                            dv.inc(v.tensor_tensor(ctxn[0:64, p, qsl],
                                                   ctxps_a[0:64, :], rbc[:, 0, :], op=ALU.mult))
                            dv.inc(v.tensor_tensor(stgb[:, tqh, :],
                                                   ctxps_b[0:64, :], rbc[:, 1, :], op=ALU.mult))
                assert dv.n == dv2 + 8 * DVP
            p2_dv, p2_mm, p2_ac = dv.n, mm.n, ac.n

            if until == 2:
                @block.sync
                def _(sync):
                    sync.wait_ge(dv.h, dv.n)
                    sync.wait_ge(mm.h, mm.n)
                    sync.wait_ge(ac.h, ac.n)
                    sync.wait_ge(gp.h, gp.n)
                    sy.dinc(sync.dma_start(out=ddbg, in_=kv_sb[:]))
                    sync.wait_ge(sy.h, sy.n)
                return nc

        outacc = top.enter_context(nc.sbuf_tensor("outacc", [128, NCO, D], F32, side="left"))
        x2_fm = top.enter_context(nc.sbuf_tensor("x2fm", [128, 8, M], BF16, side="left"))

        if "ctxn" in dtaps:
            @block.sync
            def _(sync):
                sync.wait_ge(dv.h, p2_dv)
                sy.dinc(sync.dma_start(out=dtaps["ctxn"], in_=ctxn[:]))

        with ExitStack() as p34:
            R3 = lambda n, sh, dt=F32: p34.enter_context(nc.sbuf_tensor(n, sh, dt, side="right"))
            wp_sb = R3("wp_sb", [128, 8, D], BF16)
            gw_sb = R3("gw_sb", [128, 8, NE], BF16)
            xbuf2 = R3("xbuf2", [128, 2, D])
            sqd2 = R3("sqd2", [128, D], BF16)
            st2 = R3("st2", [128, 2, 16])
            xn2 = R3("xn2", [128, 2, D])
            x2bf = R3("x2bf", [128, 2, D], BF16)
            lgs = R3("lgs", [128, 2, NE])
            rmx2 = R3("rmx2", [128, 2, 1])

            msaps = p34.enter_context(nc.psum_tensor("msaps", [128, 2, D], F32))
            x2T = p34.enter_context(nc.psum_tensor("x2T", [128, D], BF16))
            lgps = p34.enter_context(nc.psum_tensor("lgps", [128, NE], F32))

            @block.gpsimd
            def _(g):
                g.wait_ge(mm.h, p2_mm)
                g.wait_ge(dv.h, p2_dv)
                g.wait_ge(ac.h, p2_ac)
                g.wait_ge(smg[0].h, sgmark[(7, 0)])
                g.wait_ge(smg[1].h, sgmark[(7, 1)])
                for p in range(8):
                    gp.dinc(g.dma_start(out=wp_sb[:, p, :], in_=dwproj[:, p, :]))
                for fc in range(8):
                    gp.dinc(g.dma_start(out=gw_sb[:, fc, :],
                                        in_=dgatew[fc * 128:(fc + 1) * 128, :]))
            p34_gp = gp.n

            mm3, ac3, dv3 = mm.n, ac.n, dv.n
            xmark3 = {}
            n_var2 = lambda c: dv3 + 6 * c + 1
            n_y1 = lambda c: dv3 + 6 * c + 2
            n_var3 = lambda c: dv3 + 6 * c + 3
            n_x2bf = lambda c: dv3 + 6 * c + 4
            n_x2fm = lambda c: dv3 + 6 * c + 5
            n_mask = lambda c: dv3 + 6 * c + 6
            n_sq2 = lambda c: ac3 + 6 * c + 1
            n_rstd2 = lambda c: ac3 + 6 * c + 3
            n_sqy = lambda c: ac3 + 6 * c + 4
            n_rstd3 = lambda c: ac3 + 6 * c + 6
            n_proj = lambda c: mm3 + 3 * c + 1
            n_x2T = lambda c: mm3 + 3 * c + 2
            n_lg = lambda c: mm3 + 3 * c + 3

            @block.sync
            def _(sync):
                sync.wait_ge(mm.h, p2_mm)
                sync.wait_ge(dv.h, p2_dv)
                sync.wait_ge(ac.h, p2_ac)
                for c in range(NCO):
                    if c >= 2:
                        sync.wait_ge(dv.h, n_y1(c - 2))
                        sync.wait_ge(ac.h, n_sq2(c - 2))
                    smx[c % 2].dinc(sync.dma_start(out=xbuf2[:, c % 2, :],
                                                   in_=dx[c * 128:(c + 1) * 128, :]))
                    xmark3[c] = smx[c % 2].n

            @block.scalar
            def _(sc):
                for c in range(NCO):
                    r = c % 2
                    sc.wait_ge(smx[r].h, xmark3[c])
                    ac.inc(sc.activation(out=sqd2[:], in_=xbuf2[:, r, :],
                                         func=AF.Square, accum_out=st2[:, r, 0:1]))
                    sc.wait_ge(dv.h, n_var2(c))
                    ac.inc(sc.activation(out=st2[:, r, 6:7], in_=st2[:, r, 5:6],
                                         func=AF.Ln, bias=epsc[:]))
                    ac.inc(sc.activation(out=st2[:, r, 7:8], in_=st2[:, r, 6:7],
                                         func=AF.Exp, scale=-0.5))
                    sc.wait_ge(dv.h, n_y1(c))
                    ac.inc(sc.activation(out=sqd2[:], in_=outacc[:, c, :],
                                         func=AF.Square, accum_out=st2[:, r, 8:9]))
                    sc.wait_ge(dv.h, n_var3(c))
                    ac.inc(sc.activation(out=st2[:, r, 14:15], in_=st2[:, r, 13:14],
                                         func=AF.Ln, bias=epsc[:]))
                    ac.inc(sc.activation(out=st2[:, r, 15:16], in_=st2[:, r, 14:15],
                                         func=AF.Exp, scale=-0.5))
            assert ac.n == ac3 + 6 * NCO

            @block.tensor
            def _(t):
                t.wait_ge(gp.h, p34_gp)
                t.wait_ge(dv.h, p2_dv)
                t.wait_ge(smg[0].h, sgmark[(7, 0)])
                t.wait_ge(smg[1].h, sgmark[(7, 1)])
                for c in range(NCO):
                    r = c % 2
                    if c >= 2:
                        t.wait_ge(dv.h, n_y1(c - 2))
                    csl = slice(c * 128, (c + 1) * 128)
                    for nt in range(2):
                        nsl = slice(nt * 512, (nt + 1) * 512)
                        for p in range(8):
                            ins = t.matmul(msaps[:, r, nsl],
                                           ctxn[:, p, csl],
                                           wp_sb[:, p, nsl],
                                           start=(p == 0), stop=(p == 7))
                    mm.inc(ins)
                    t.wait_ge(dv.h, n_x2bf(c))
                    if c >= 1:
                        t.wait_ge(dv.h, n_x2fm(c - 1))
                    for j in range(8):
                        ins = t.transpose(x2T[:, j * 128:(j + 1) * 128],
                                          x2bf[:, r, j * 128:(j + 1) * 128], ident[:])
                    mm.inc(ins)
                    t.wait_ge(dv.h, n_x2fm(c))
                    if c >= 1:
                        t.wait_ge(dv.h, n_mask(c - 1))
                    for fc in range(8):
                        ins = t.matmul(lgps[:], x2_fm[:, fc, csl],
                                       gw_sb[:, fc, :], start=(fc == 0), stop=(fc == 7))
                    mm.inc(ins)
            assert mm.n == mm3 + 3 * NCO

            @block.vector
            def _(v):
                v.wait_ge(dv.h, p2_dv)
                for c in range(NCO):
                    r = c % 2
                    v.wait_ge(smx[r].h, xmark3[c])
                    vs.inc(v.reduce_sum(st2[:, r, 1:2], xbuf2[:, r, :], axis=AX.X))
                    c_s = vs.n
                    v.wait_ge(ac.h, n_sq2(c))
                    vs.inc(v.tensor_scalar_mul(st2[:, r, 4:5], st2[:, r, 0:1], 1.0 / D))
                    v.wait_ge(vs.h, c_s)
                    vs.inc(v.tensor_scalar_mul(st2[:, r, 2:3], st2[:, r, 1:2], -1.0 / D))
                    v.wait_ge(vs.h, vs.n)
                    vs.inc(v.tensor_mul(st2[:, r, 3:4], st2[:, r, 2:3], st2[:, r, 2:3]))
                    v.wait_ge(vs.h, vs.n)
                    dv.inc(v.tensor_sub(st2[:, r, 5:6], st2[:, r, 4:5], st2[:, r, 3:4]))
                    v.wait_ge(ac.h, n_rstd2(c))
                    vs.inc(v.tensor_scalar(xn2[:, r, :], xbuf2[:, r, :], st2[:, r, 2:3],
                                           st2[:, r, 7:8], op0=ALU.add, op1=ALU.mult))
                    v.wait_ge(mm.h, n_proj(c))
                    v.wait_ge(vs.h, vs.n)
                    dv.inc(v.tensor_add(outacc[:, c, :], xn2[:, r, :], msaps[:, r, :]))
                    v.wait_ge(dv.h, n_y1(c))
                    vs.inc(v.reduce_sum(st2[:, r, 9:10], outacc[:, c, :], axis=AX.X))
                    c_s2 = vs.n
                    v.wait_ge(ac.h, n_sqy(c))
                    vs.inc(v.tensor_scalar_mul(st2[:, r, 12:13], st2[:, r, 8:9], 1.0 / D))
                    v.wait_ge(vs.h, c_s2)
                    vs.inc(v.tensor_scalar_mul(st2[:, r, 10:11], st2[:, r, 9:10], -1.0 / D))
                    v.wait_ge(vs.h, vs.n)
                    vs.inc(v.tensor_mul(st2[:, r, 11:12], st2[:, r, 10:11], st2[:, r, 10:11]))
                    v.wait_ge(vs.h, vs.n)
                    dv.inc(v.tensor_sub(st2[:, r, 13:14], st2[:, r, 12:13],
                                        st2[:, r, 11:12]))
                    v.wait_ge(ac.h, n_rstd3(c))
                    vs.inc(v.tensor_scalar(xn2[:, r, :], outacc[:, c, :], st2[:, r, 10:11],
                                           st2[:, r, 15:16], op0=ALU.add, op1=ALU.mult))
                    v.wait_ge(vs.h, vs.n)
                    vs.inc(v.tensor_mul(xn2[:, r, :], xn2[:, r, :], w2ln_bc[:]))
                    v.wait_ge(vs.h, vs.n)
                    dv.inc(v.tensor_tensor(x2bf[:, r, :], xn2[:, r, :], b2ln_bc[:],
                                           op=ALU.add))
                    v.wait_ge(mm.h, n_x2T(c))
                    dv.inc(v.tensor_copy(
                        x2_fm[:, :, c * 128:(c + 1) * 128],
                        x2T[:].rearrange("p (a b) -> p a b", a=8)))
                    v.wait_ge(mm.h, n_lg(c))
                    vs.inc(v.tensor_add(lgs[:, r, :], lgps[:], gb_bc[:]))
                    v.wait_ge(vs.h, vs.n)
                    vs.inc(v.reduce_max(rmx2[:, r, :], lgs[:, r, :], axis=AX.X))
                    v.wait_ge(vs.h, vs.n)
                    dv.inc(v.tensor_single_scalar(masks[:, c, :], lgs[:, r, :],
                                                  rmx2[:, r, :], op=ALU.is_equal))
            assert dv.n == dv3 + 6 * NCO
        p34_dv, p34_mm, p34_ac = dv.n, mm.n, ac.n

        if until == 3:
            @block.sync
            def _(sync):
                sync.wait_ge(dv.h, dv.n)
                sync.wait_ge(mm.h, mm.n)
                sync.wait_ge(ac.h, ac.n)
                sync.wait_ge(gp.h, gp.n)
                sy.dinc(sync.dma_start(out=ddbg, in_=kv_sb[:]))
                sync.wait_ge(sy.h, sy.n)
            return nc

        if "y1" in dtaps:
            @block.sync
            def _(sync):
                sync.wait_ge(dv.h, p34_dv)
                sy.dinc(sync.dma_start(out=dtaps["y1"], in_=outacc[:]))
                sy.dinc(sync.dma_start(out=dtaps["x2fm"], in_=x2_fm[:]))
                sy.dinc(sync.dma_start(out=dtaps["masks"], in_=masks[:]))

        pctx.close()
        p5_sy_taps = sy.n
        with ExitStack() as p5:
            w1sb = p5.enter_context(nc.sbuf_tensor("w1sb", [128, 8, 1024], BF16, side="right"))
            w2sb = p5.enter_context(nc.sbuf_tensor("w2sb", [128, 8, 1024], BF16, side="right"))
            h_sb = p5.enter_context(nc.sbuf_tensor("h_sb", [128, 8, M], BF16, side="right"))

            mm5, ac5, dv5 = mm.n, ac.n, dv.n
            MM5, AC5, DV5 = 8 + 16, 8, 16
            i_of = lambda e, q: e * NQ + q
            w1mark, w2mark = {}, {}
            w1mk = lambda i, hc: mm5 + MM5 * i + hc + 1
            w2mk = lambda i, tt: mm5 + MM5 * i + 8 + tt + 1
            gemk = lambda i, hc: ac5 + AC5 * i + hc + 1
            mamk = lambda i, tt: dv5 + DV5 * i + tt + 1

            @block.gpsimd
            def _(g):
                g.wait_ge(dv.h, p34_dv)
                for e in range(NE):
                    for q in range(NQ):
                        i = i_of(e, q)
                        if i >= 1:
                            g.wait_ge(mm.h, w1mk(i - 1, 7))
                        for fc in range(8):
                            gp1.dinc(g.dma_start(
                                out=w1sb[:, fc, :],
                                in_=dw1q[e, q, fc * 128:(fc + 1) * 128, :]))
                        w1mark[i] = gp1.n
                        if i >= 1:
                            g.wait_ge(mm.h, w2mk(i - 1, 15))
                        for hc in range(8):
                            gp2.dinc(g.dma_start(
                                out=w2sb[:, hc, :],
                                in_=dw2q[e, q, hc * 128:(hc + 1) * 128, :]))
                        w2mark[i] = gp2.n

            with ExitStack() as p5p:
                hps = p5p.enter_context(nc.psum_tensor("hps", [128, 2, M], F32))

                @block.tensor
                def _(t):
                    t.wait_ge(dv.h, p34_dv)
                    for e in range(NE):
                        for q in range(NQ):
                            i = i_of(e, q)
                            t.wait_ge(gp1.h, w1mark[i])
                            if i >= 1:
                                t.wait_ge(dv.h, mamk(i - 1, 15))
                            for hc in range(8):
                                if hc >= 2:
                                    t.wait_ge(ac.h, gemk(i, hc - 2))
                                for nt in range(4):
                                    nsl = slice(nt * 512, (nt + 1) * 512)
                                    for fc in range(8):
                                        ins = t.matmul(
                                            hps[:, hc % 2, nsl],
                                            w1sb[:, fc, hc * 128:(hc + 1) * 128],
                                            x2_fm[:, fc, nsl],
                                            start=(fc == 0), stop=(fc == 7))
                                mm.inc(ins)
                            t.wait_ge(gp2.h, w2mark[i])
                            t.wait_ge(ac.h, gemk(i, 7))
                            for tt in range(NCO):
                                ysl = (tt % 2)
                                if tt >= 2:
                                    t.wait_ge(dv.h, mamk(i, tt - 2))
                                for nt in range(2):
                                    nsl = slice(ysl * 1024 + nt * 512,
                                                ysl * 1024 + (nt + 1) * 512)
                                    for hc in range(8):
                                        ins = t.matmul(
                                            hps[:, 0, nsl],
                                            h_sb[:, hc, tt * 128:(tt + 1) * 128],
                                            w2sb[:, hc, nt * 512:(nt + 1) * 512],
                                            start=(hc == 0), stop=(hc == 7))
                                mm.inc(ins)
                assert mm.n == mm5 + MM5 * NE * NQ

                @block.scalar
                def _(sc):
                    for e in range(NE):
                        for q in range(NQ):
                            i = i_of(e, q)
                            for hc in range(8):
                                sc.wait_ge(mm.h, w1mk(i, hc))
                                if i >= 1:
                                    sc.wait_ge(mm.h, w2mk(i - 1, 15))
                                ac.inc(sc.activation(out=h_sb[:, hc, :],
                                                     in_=hps[:, hc % 2, :],
                                                     func=AF.Gelu_apprx_tanh))
                assert ac.n == ac5 + AC5 * NE * NQ

                @block.vector
                def _(v):
                    v.wait_ge(dv.h, p34_dv)
                    for e in range(NE):
                        for q in range(NQ):
                            i = i_of(e, q)
                            for tt in range(NCO):
                                ysl = (tt % 2)
                                v.wait_ge(mm.h, w2mk(i, tt))
                                if i >= 1:
                                    v.wait_ge(dv.h, mamk(i - 1, tt))
                                dv.inc(v.scalar_tensor_tensor(
                                    outacc[:, tt, :],
                                    hps[:, 0, ysl * 1024:(ysl + 1) * 1024],
                                    masks[:, tt, e:e + 1],
                                    outacc[:, tt, :],
                                    op0=ALU.mult, op1=ALU.add))
                assert dv.n == dv5 + DV5 * NE * NQ
        p5_dv, p5_mm, p5_ac = dv.n, mm.n, ac.n

        if until == 5:
            @block.sync
            def _(sync):
                sync.wait_ge(dv.h, dv.n)
                sync.wait_ge(mm.h, mm.n)
                sync.wait_ge(ac.h, ac.n)
                sync.wait_ge(gp.h, gp.n)
                sy.dinc(sync.dma_start(out=ddbg, in_=kv_sb[:]))
                sync.wait_ge(sy.h, sy.n)
            return nc

        if "acc5" in dtaps:
            @block.sync
            def _(sync):
                sync.wait_ge(dv.h, p5_dv)
                sy.dinc(sync.dma_start(out=dtaps["acc5"], in_=outacc[:]))

        with ExitStack() as p6:
            R6 = lambda n, sh, dt=F32: p6.enter_context(nc.sbuf_tensor(n, sh, dt, side="right"))
            wq6 = R6("wq6", [128, 8, D], BF16)
            wfq6 = R6("wfq6", [128, 8, D], BF16)
            xbuf3 = R6("xbuf3", [128, 2, D])
            sqd3 = R6("sqd3", [128, D], BF16)
            st3 = R6("st3", [128, 2, 8])
            xnbf3 = R6("xnbf3", [128, 2, D], BF16)
            xnfm3 = R6("xnfm3", [128, 2, D], BF16)
            qp1b = R6("qp1b", [128, 2, D], BF16)
            nmx6 = R6("nmx6", [128, 2, 1])
            qtbf = R6("qtbf", [128, 2, D], BF16)
            qfm = R6("qfm", [128, 2, D], BF16)
            rr6 = R6("rr6", [128, H])
            oc = R6("oc", [128, 2, D])

            xnT6 = p6.enter_context(nc.psum_tensor("xnT6", [128, D], BF16))
            qT6 = p6.enter_context(nc.psum_tensor("qT6", [128, D], BF16))
            psA = p6.enter_context(nc.psum_tensor("psA", [128, D], F32))
            ctx6 = [p6.enter_context(nc.psum_tensor(f"ctx6_{j}", [128, 2, 256], F32))
                    for j in range(4)]

            @block.gpsimd
            def _(g):
                g.wait_ge(mm.h, p5_mm)
                g.wait_ge(dv.h, p5_dv)
                for wsrc, wdst in ((dwqT, wq6), (dwfqT, wfq6)):
                    for fc in range(8):
                        gp.dinc(g.dma_start(out=wdst[:, fc, :],
                                            in_=wsrc[fc * 128:(fc + 1) * 128, :]))
            p6_gp = gp.n

            mm6, ac6, dv6 = mm.n, ac.n, dv.n
            o_var = lambda c: dv6 + (1 if c == 0 else 7 * c - 1)
            o_xnbf = lambda c: dv6 + (2 if c == 0 else 7 * c)
            o_xnfm = lambda c: dv6 + (3 if c == 0 else 7 * c + 3)
            o_qp1e = lambda c: dv6 + (4 if c == 0 else 7 * c + 4)
            o_nmx = lambda c: dv6 + (5 if c == 0 else 7 * c + 5)
            o_qfme = lambda c: dv6 + (7 * c + 8 if c < NCO - 1 else 7 * NCO - 1)
            o_asm = lambda c: dv6 + (7 * c + 9 if c < NCO - 1 else 7 * NCO)
            o_sq = lambda c: ac6 + 4 * c + 1
            o_rstd = lambda c: ac6 + 4 * c + 3
            o_qexp = lambda c: ac6 + 4 * c + 4
            o_xnT = lambda c: mm6 + (1 if c == 0 else 5 * c - 1)
            o_qp1 = lambda c: mm6 + (2 if c == 0 else 5 * c + 2)
            o_qp2 = lambda c: mm6 + (3 if c == 0 else 5 * c + 3)
            o_qT = lambda c: mm6 + (5 * c + 5 if c < NCO - 1 else 5 * NCO - 1)
            o_ctx = lambda c: mm6 + (5 * c + 6 if c < NCO - 1 else 5 * NCO)
            sy_marks = {}

            @block.sync
            def _(sync):
                sync.wait_ge(dv.h, p5_dv)
                for c in range(NCO):
                    if c >= 2:
                        sync.wait_ge(dv.h, o_asm(c - 2))
                        sync.wait_ge(ac.h, o_sq(c - 2))
                    smx[c % 2].dinc(sync.dma_start(out=xbuf3[:, c % 2, :],
                                                   in_=dx[c * 128:(c + 1) * 128, :]))
                    sy_marks[("x", c)] = smx[c % 2].n
                    if c >= 1:
                        sync.wait_ge(dv.h, o_asm(c - 1))
                        smy[(c - 1) % 2].dinc(sync.dma_start(
                            out=dy[(c - 1) * 128:c * 128, :], in_=oc[:, (c - 1) % 2, :]))
                        sy_marks[("y", c - 1)] = smy[(c - 1) % 2].n
                sync.wait_ge(dv.h, o_asm(NCO - 1))
                smy[(NCO - 1) % 2].dinc(sync.dma_start(out=dy[(NCO - 1) * 128:NCO * 128, :],
                                                       in_=oc[:, (NCO - 1) % 2, :]))
                sy_marks[("y", NCO - 1)] = smy[(NCO - 1) % 2].n
                sync.wait_ge(smy[0].h, smy[0].n)
                sync.wait_ge(smy[1].h, smy[1].n)
                if sy.n:
                    sync.wait_ge(sy.h, sy.n)

            @block.scalar
            def _(sc):
                sc.wait_ge(gp.h, p6_gp)
                for c in range(NCO):
                    r = c % 2
                    sc.wait_ge(smx[r].h, sy_marks[("x", c)])
                    ac.inc(sc.activation(out=sqd3[:], in_=xbuf3[:, r, :],
                                         func=AF.Square, accum_out=st3[:, r, 0:1]))
                    sc.wait_ge(dv.h, o_var(c))
                    ac.inc(sc.activation(out=st3[:, r, 6:7], in_=st3[:, r, 5:6],
                                         func=AF.Ln, bias=epsc[:]))
                    ac.inc(sc.activation(out=st3[:, r, 7:8], in_=st3[:, r, 6:7],
                                         func=AF.Exp, scale=-0.5))
                    sc.wait_ge(mm.h, o_qp2(c))
                    sc.wait_ge(dv.h, o_nmx(c))
                    ac.inc(sc.activation(out=qtbf[:, r, :], in_=psA[:],
                                         func=AF.Exp, bias=nmx6[:, r, :]))
            assert ac.n == ac6 + 4 * NCO

            @block.tensor
            def _(t):
                t.wait_ge(gp.h, p6_gp)

                def qt_ctx(cc):
                    rr = cc % 2
                    t.wait_ge(ac.h, o_qexp(cc))
                    if cc >= 1:
                        t.wait_ge(dv.h, o_qfme(cc - 1))
                    for j in range(8):
                        ins = t.transpose(qT6[:, j * 128:(j + 1) * 128],
                                          qtbf[:, rr, j * 128:(j + 1) * 128], ident[:])
                    mm.inc(ins)
                    t.wait_ge(dv.h, o_qfme(cc))
                    if cc >= 1:
                        t.wait_ge(dv.h, o_asm(cc - 1))
                    for p in range(8):
                        ins = t.matmul(ctx6[p // 2][:, p % 2, 0:130],
                                       qfm[:, rr, p * 128:(p + 1) * 128],
                                       kv_sb[:, p, 0:130],
                                       start=True, stop=True)
                    mm.inc(ins)

                for c in range(NCO):
                    r = c % 2
                    t.wait_ge(dv.h, o_xnbf(c))
                    if c >= 1:
                        t.wait_ge(dv.h, o_xnfm(c - 1))
                    for j in range(8):
                        ins = t.transpose(xnT6[:, j * 128:(j + 1) * 128],
                                          xnbf3[:, r, j * 128:(j + 1) * 128], ident[:])
                    mm.inc(ins)
                    if c >= 1:
                        qt_ctx(c - 1)
                    t.wait_ge(dv.h, o_xnfm(c))
                    if c >= 1:
                        t.wait_ge(ac.h, o_qexp(c - 1))
                    for jc in range(8):
                        sl = slice(jc * 128, (jc + 1) * 128)
                        for fc in range(8):
                            t.matmul(psA[:, sl], wq6[:, fc, sl],
                                     xnfm3[:, r, fc * 128:(fc + 1) * 128],
                                     start=(fc == 0), stop=False)
                        ins = t.matmul(psA[:, sl], bq_bf[:, sl],
                                       ones2[0:1, 0:128], start=False, stop=True)
                    mm.inc(ins)
                    t.wait_ge(dv.h, o_qp1e(c))
                    for nt in range(2):
                        sl = slice(nt * 512, (nt + 1) * 512)
                        for jc in range(8):
                            t.matmul(psA[:, sl], qp1b[:, r, jc * 128:(jc + 1) * 128],
                                     wfq6[:, jc, sl], start=(jc == 0), stop=False)
                        ins = t.matmul(psA[:, sl], ones2[0:1, 0:128],
                                       bfq_bf[:, sl], start=False, stop=True)
                    mm.inc(ins)
                qt_ctx(NCO - 1)
            assert mm.n == mm6 + 5 * NCO

            @block.vector
            def _(v):
                v.wait_ge(dv.h, p5_dv)

                def qfm_asm(v, cc):
                    rr = cc % 2
                    v.wait_ge(mm.h, o_qT(cc))
                    dv.inc(v.tensor_copy(qfm[:, rr, :], qT6[:]))
                    if cc >= 2:
                        v.wait_ge(smy[rr].h, sy_marks[("y", cc - 2)])
                    vs.inc(v.tensor_add(oc[:, rr, :], xbuf3[:, rr, :],
                                        outacc[:, cc, :]))
                    v.wait_ge(mm.h, o_ctx(cc))
                    for h in range(H):
                        p, side = h // 2, h % 2
                        cps = ctx6[p // 2]
                        ins = v.reciprocal(rr6[:, h:h + 1],
                                           cps[:, p % 2, 65 * side + 64:65 * side + 65])
                    vs.inc(ins)
                    v.wait_ge(vs.h, vs.n)
                    for h in range(H):
                        p, side = h // 2, h % 2
                        cps = ctx6[p // 2]
                        ins = v.scalar_tensor_tensor(
                            oc[:, rr, h * 64:(h + 1) * 64],
                            cps[:, p % 2, 65 * side:65 * side + 64],
                            rr6[:, h:h + 1], oc[:, rr, h * 64:(h + 1) * 64],
                            op0=ALU.mult, op1=ALU.add)
                    dv.inc(ins)

                for c in range(NCO):
                    r = c % 2
                    v.wait_ge(smx[r].h, sy_marks[("x", c)])
                    vs.inc(v.reduce_sum(st3[:, r, 1:2], xbuf3[:, r, :], axis=AX.X))
                    c_s = vs.n
                    v.wait_ge(ac.h, o_sq(c))
                    vs.inc(v.tensor_scalar_mul(st3[:, r, 4:5], st3[:, r, 0:1], 1.0 / D))
                    v.wait_ge(vs.h, c_s)
                    vs.inc(v.tensor_scalar_mul(st3[:, r, 2:3], st3[:, r, 1:2], -1.0 / D))
                    v.wait_ge(vs.h, vs.n)
                    vs.inc(v.tensor_mul(st3[:, r, 3:4], st3[:, r, 2:3], st3[:, r, 2:3]))
                    v.wait_ge(vs.h, vs.n)
                    dv.inc(v.tensor_sub(st3[:, r, 5:6], st3[:, r, 4:5], st3[:, r, 3:4]))
                    v.wait_ge(ac.h, o_rstd(c))
                    dv.inc(v.tensor_scalar(xnbf3[:, r, :], xbuf3[:, r, :], st3[:, r, 2:3],
                                           st3[:, r, 7:8], op0=ALU.add, op1=ALU.mult))
                    if c >= 1:
                        qfm_asm(v, c - 1)
                    v.wait_ge(mm.h, o_xnT(c))
                    dv.inc(v.tensor_copy(xnfm3[:, r, :], xnT6[:]))
                    v.wait_ge(mm.h, o_qp1(c))
                    dv.inc(v.tensor_copy(qp1b[:, r, :], psA[:]))
                    v.wait_ge(mm.h, o_qp2(c))
                    dv.inc(v.reduce_max(nmx6[:, r, :], psA[:], axis=AX.X, negate=True))
                qfm_asm(v, NCO - 1)
            assert dv.n == dv6 + 7 * NCO

    return nc


def _stage(inputs):
    g = lambda k: np.asarray(inputs["L0_" + k])
    x = g("attn_hidden_states")
    shared = {
        "wqT": np.ascontiguousarray(g("attn_Wq").T),
        "wkT": np.ascontiguousarray(g("attn_Wk").T),
        "wvT": np.ascontiguousarray(g("attn_Wv").T),
        "wfqT": np.ascontiguousarray(g("attn_Wfq").T),
        "wfkT": np.ascontiguousarray(g("attn_Wfk").T),
        "wqkv": np.ascontiguousarray(g("ffn_w_qkv")),
        "wprojp": np.ascontiguousarray(g("ffn_w_proj").reshape(8, 128, D).transpose(1, 0, 2)),
        "gatew": np.ascontiguousarray(g("ffn_gate_w")),
        "w1q": np.ascontiguousarray(g("ffn_w1").reshape(NE, D, NQ, 1024).transpose(0, 2, 1, 3)),
        "w2q": np.ascontiguousarray(g("ffn_w2").reshape(NE, NQ, 1024, D)),
        "bq": g("attn_bq").reshape(1, D), "bk": g("attn_bk").reshape(1, D),
        "bv": g("attn_bv").reshape(1, D), "bfq": g("attn_bfq").reshape(1, D),
        "bfk": g("attn_bfk").reshape(1, D),
        "ln1w": g("ffn_ln1_w").reshape(1, D), "ln1b": g("ffn_ln1_b").reshape(1, D),
        "ln2w": g("ffn_ln2_w").reshape(1, D), "ln2b": g("ffn_ln2_b").reshape(1, D),
        "gateb": g("ffn_gate_b").reshape(1, NE),
    }
    shared = {k: np.ascontiguousarray(v, dtype=np.float32) for k, v in shared.items()}
    in_maps = []
    for c in range(8):
        b, hf = c // 2, c % 2
        xb = x[b]
        xc = np.ascontiguousarray(np.concatenate(
            [xb[hf * M:(hf + 1) * M], xb[(1 - hf) * M:(2 - hf) * M]], axis=0),
            dtype=np.float32)
        im = dict(shared)
        im["x"] = xc
        in_maps.append(im)
    return in_maps


def kernel(**inputs):
    in_maps = _stage(inputs)
    nc = build_program()
    res = run_bass_kernel_spmd(nc, in_maps, core_ids=list(range(8)))
    out = np.empty((4, S, D), dtype=np.float32)
    for c in range(8):
        b, hf = c // 2, c % 2
        out[b, hf * M:(hf + 1) * M, :] = res.results[c]["y"]
    return out


# version 14
# speedup vs baseline: 1.0411x; 1.0411x over previous
import numpy as np
from contextlib import ExitStack

import concourse.bass as bass
import concourse.mybir as mybir
import concourse.bass_utils as _bass_utils
from concourse.bass_utils import run_bass_kernel_spmd

if not getattr(_bass_utils, "_ldw_opt_patched", False):
    _orig_run_command = _bass_utils.run_command

    def _run_command_ldw(argv, **kwargs):
        argv = ["--enable-ldw-opt=true" if a == "--enable-ldw-opt=false" else a
                for a in argv]
        return _orig_run_command(argv, **kwargs)

    _bass_utils.run_command = _run_command_ldw
    _bass_utils._ldw_opt_patched = True

F32 = mybir.dt.float32
BF16 = mybir.dt.bfloat16
AF = mybir.ActivationFunctionType
ALU = mybir.AluOpType
AX = mybir.AxisListType

S, M, D, H = 4096, 2048, 1024, 16
NE, NQ = 8, 4
EPS = 1e-5
MSA_SCALE = 0.125
MSA_B = 12.0
KQ_B = 8.0
NCF = S // 128
NCO = M // 128


class Ctr:
    def __init__(self, nc, ctx, name):
        self.h = ctx.enter_context(nc.semaphore(name))
        self.n = 0

    def inc(self, instr, k=1):
        instr.then_inc(self.h, k)
        self.n += k
        return instr

    def dinc(self, instr):
        return self.inc(instr, 16)


def build_program(taps=(), until=None):
    import ml_dtypes
    nc = bass.Bass("TRN2", target_bir_lowering=False, detect_race_conditions=False)

    dx = nc.dram_tensor("x", [S, D], F32, kind="ExternalInput").ap()
    dwqT = nc.dram_tensor("wqT", [D, D], F32, kind="ExternalInput").ap()
    dwkT = nc.dram_tensor("wkT", [D, D], F32, kind="ExternalInput").ap()
    dwvT = nc.dram_tensor("wvT", [D, D], F32, kind="ExternalInput").ap()
    dwfqT = nc.dram_tensor("wfqT", [D, D], F32, kind="ExternalInput").ap()
    dwfkT = nc.dram_tensor("wfkT", [D, D], F32, kind="ExternalInput").ap()
    dwqkv = nc.dram_tensor("wqkv", [D, 3 * D], F32, kind="ExternalInput").ap()
    dwproj = nc.dram_tensor("wprojp", [128, 8, D], F32, kind="ExternalInput").ap()
    dgatew = nc.dram_tensor("gatew", [D, NE], F32, kind="ExternalInput").ap()
    dw1q = nc.dram_tensor("w1q", [NE, NQ, D, 1024], F32, kind="ExternalInput").ap()
    dw2q = nc.dram_tensor("w2q", [NE, NQ, 1024, D], F32, kind="ExternalInput").ap()
    dvecs = {nm: nc.dram_tensor(nm, [1, D], F32, kind="ExternalInput").ap()
             for nm in ("bq", "bk", "bv", "bfq", "bfk", "ln1w", "ln1b", "ln2w", "ln2b")}
    dgateb = nc.dram_tensor("gateb", [1, NE], F32, kind="ExternalInput").ap()
    dy = nc.dram_tensor("y", [M, D], F32, kind="ExternalOutput").ap()
    dtaps = {}
    for t in taps:
        shp, dt = {"x1fm": ([128, 8, S], BF16), "kv": ([128, NE, 132], BF16),
                   "ctxn": ([128, 8, M], BF16), "y1": ([128, NCO, D], F32),
                   "x2fm": ([128, 8, M], BF16), "masks": ([128, NCO, NE], F32),
                   "acc5": ([128, NCO, D], F32)}[t]
        dtaps[t] = nc.dram_tensor("t_" + t, shp, dt, kind="ExternalOutput").ap()

    ident_dram = nc.inline_tensor(
        np.eye(128, dtype=np.float32).astype(ml_dtypes.bfloat16), name="identc")
    ddbg = (nc.dram_tensor("dbg", [128, NE, 65], BF16, kind="ExternalOutput").ap()
            if until is not None else None)
    ddbg2 = {}
    if until == 1:
        for nm, sh, dt in (("xn_bf", [128, 2, D], BF16), ("khat", [128, 2, D], BF16),
                           ("kescr", [128, 2, D], BF16), ("ksum", [128, 2, 1], F32),
                           ("negmax", [128, 2, 1], F32), ("var", [128, 2, 1], F32),
                           ("rstd", [128, 2, 1], F32), ("kp1", [128, 2, D], BF16),
                           ("vsb", [128, 2, H, 65], BF16), ("kvacc", [128, NE, 65], F32),
                           ("xnfm", [128, 2, D], BF16), ("x1bf", [128, 2, D], BF16)):
            ddbg2[nm] = nc.dram_tensor("dbg_" + nm, sh, dt, kind="ExternalOutput").ap()

    with ExitStack() as top:
        SBL = lambda n, sh, dt=F32: top.enter_context(nc.sbuf_tensor(n, sh, dt, side="left"))
        ident = SBL("ident", [128, 128], BF16)
        ones2 = SBL("ones2", [128, 512], BF16)
        onesf = SBL("onesf", [1, 128], F32)
        vstage = SBL("vstage", [1, D], F32)
        bq_bf = SBL("bq_bf", [1, D], BF16)
        bk_bf = SBL("bk_bf", [1, D], BF16)
        bv_bf = SBL("bv_bf", [1, D], BF16)
        bfq_bf = SBL("bfq_bf", [1, D], BF16)
        bfk_bf = SBL("bfk_bf", [1, D], BF16)
        w1ln_bc = SBL("w1ln_bc", [128, D], F32)
        b1ln_bc = SBL("b1ln_bc", [128, D], F32)
        w2ln_bc = SBL("w2ln_bc", [128, D], F32)
        b2ln_bc = SBL("b2ln_bc", [128, D], F32)
        gb_bc = SBL("gb_bc", [128, NE], F32)
        kv_sb = SBL("kv_sb", [128, NE, 132], BF16)
        kvacc = SBL("kvacc", [128, NE, 65], F32)
        epsc = SBL("epsc", [128, 1], F32)
        eps2c = SBL("eps2c", [128, 1], F32)
        mnegb = SBL("mnegb", [128, 1], F32)
        kqb = SBL("kqb", [128, 1], F32)
        masks = SBL("masks", [128, NCO, NE], F32)

        sy = Ctr(nc, top, "sy")
        gp = Ctr(nc, top, "gp")
        mm = Ctr(nc, top, "mm")
        ac = Ctr(nc, top, "ac")
        dv = Ctr(nc, top, "dv")
        smid = Ctr(nc, top, "smid")
        smx = [Ctr(nc, top, "smx0"), Ctr(nc, top, "smx1")]
        smy = [Ctr(nc, top, "smy0"), Ctr(nc, top, "smy1")]
        smg = [Ctr(nc, top, "smg0"), Ctr(nc, top, "smg1")]
        gpp = [Ctr(nc, top, "gpp0"), Ctr(nc, top, "gpp1")]
        gp1 = Ctr(nc, top, "gp1")
        gp2 = Ctr(nc, top, "gp2")
        vs = Ctr(nc, top, "vs")

        block = top.enter_context(nc.Block())

        with ExitStack() as p0:
            bcps = p0.enter_context(nc.psum_tensor("bcps", [128, 2, 512], F32))

            @block.sync
            def _(sync):
                smid.dinc(sync.dma_start(out=ident[:], in_=ident_dram.ap()))

            @block.vector
            def _(v):
                v.memset(ones2[:], 1.0)
                v.memset(epsc[:], EPS)
                v.memset(eps2c[:], EPS * EPS)
                v.memset(mnegb[:], -MSA_B)
                v.memset(kqb[:], -KQ_B)
                v.memset(kv_sb[:], 0.0)
                dv.inc(v.memset(onesf[:], 1.0))
            ones_dv = dv.n

            bc_list = [("ln1w", w1ln_bc, D), ("ln1b", b1ln_bc, D),
                       ("ln2w", w2ln_bc, D), ("ln2b", b2ln_bc, D),
                       ("gateb", gb_bc, NE)]
            nseg = lambda w: (w + 511) // 512
            gp_b0, mm_b0, dv_b0 = gp.n, mm.n, dv.n
            cum = [0]
            for _, _, w in bc_list:
                cum.append(cum[-1] + nseg(w))

            @block.gpsimd
            def _(g):
                for nm, dst in (("bq", bq_bf), ("bk", bk_bf), ("bv", bv_bf),
                                ("bfq", bfq_bf), ("bfk", bfk_bf)):
                    gp.dinc(g.dma_start(out=dst[:], in_=dvecs[nm]))
                    g.wait_ge(gp.h, gp.n)
                for i, (nm, dst, w) in enumerate(bc_list):
                    if i > 0:
                        g.wait_ge(mm.h, mm_b0 + cum[i])
                    gp.dinc(g.dma_start(out=vstage[:, 0:w], in_=(
                        dvecs[nm] if w == D else dgateb)))
                    g.wait_ge(gp.h, gp.n)

            @block.tensor
            def _(t):
                t.wait_ge(dv.h, ones_dv)
                for i, (nm, dst, w) in enumerate(bc_list):
                    t.wait_ge(gp.h, gp_b0 + 16 * (5 + i + 1))
                    if i > 0:
                        t.wait_ge(dv.h, dv_b0 + i)
                    for si, j in enumerate(range(0, w, 512)):
                        ws = min(512, w - j)
                        mm.inc(t.matmul(bcps[:, si % 2, 0:ws], onesf[:],
                                        vstage[:, j:j + ws], start=True, stop=True))

            @block.vector
            def _(v):
                for i, (nm, dst, w) in enumerate(bc_list):
                    v.wait_ge(mm.h, mm_b0 + cum[i + 1])
                    for si, j in enumerate(range(0, w, 512)):
                        ws = min(512, w - j)
                        ins = v.tensor_copy(dst[:, j:j + ws], bcps[:, si % 2, 0:ws])
                    dv.inc(ins)
            assert dv.n == dv_b0 + 5 and mm.n == mm_b0 + cum[-1]

        pctx = top.enter_context(ExitStack())
        with ExitStack() as px1:
            x1_fm = px1.enter_context(nc.sbuf_tensor("x1_fm", [128, 8, S], BF16, side="left"))

            with ExitStack() as p1:
                R = lambda n, sh, dt=F32: p1.enter_context(nc.sbuf_tensor(n, sh, dt, side="right"))
                wk_sb = R("wk_sb", [128, 8, D], BF16)
                wfk_sb = R("wfk_sb", [128, 8, D], BF16)
                wv_sb = R("wv_sb", [128, 8, D], BF16)
                xbuf = R("xbuf", [128, 2, D])
                sqd = R("sqd", [128, D], BF16)
                ss = R("ss", [128, 2, 1]); s_ = R("s_", [128, 2, 1])
                negm = R("negm", [128, 2, 1]); msq = R("msq", [128, 2, 1])
                t1_ = R("t1_", [128, 2, 1]); var = R("var", [128, 2, 1])
                lnv = R("lnv", [128, 2, 1]); rstd = R("rstd", [128, 2, 1])
                r01 = R("r01", [128, 2, 1])
                xn_tm = R("xn_tm", [128, 2, D])
                xn_bf = R("xn_bf", [128, 2, D], BF16)
                x1b = R("x1b", [128, 1, D])
                x1pre = R("x1pre", [128, 1, D])
                x1bf = R("x1bf", [128, 2, D], BF16)
                xnfm = R("xnfm", [128, 2, D], BF16)
                kp1 = R("kp1", [128, 2, D], BF16)
                negmax = R("negmax", [128, 2, 1])
                kescr = R("kescr", [128, 2, D], BF16)
                ksum = R("ksum", [128, 2, 1]); rk = R("rk", [128, 2, 1])
                khat = R("khat", [128, 2, D], BF16)
                vsb = R("vsb", [128, 2, H, 65], BF16)

                PP = lambda n, sh, dt=F32: p1.enter_context(nc.psum_tensor(n, sh, dt))
                kvA = PP("kvA", [128, 4, 128])
                kvB = PP("kvB", [128, 4, 128])
                xnT = PP("xnT", [128, D], BF16)
                x1T = PP("x1T", [128, D], BF16)
                psC = PP("psC", [128, D])
                psD = PP("psD", [128, D])

                @block.gpsimd
                def _(g):
                    for wsrc, wdst in ((dwkT, wk_sb), (dwfkT, wfk_sb), (dwvT, wv_sb)):
                        for fc in range(8):
                            gp.dinc(g.dma_start(out=wdst[:, fc, :],
                                                in_=wsrc[fc * 128:(fc + 1) * 128, :]))
                p1w_gp = gp.n

                @block.vector
                def _(v):
                    dv.inc(v.memset(vsb[:, :, :, 64:65], 1.0))

                mm0, ac0, dv0 = mm.n, ac.n, dv.n
                dvm = {}
                _n = [dv0]

                def _rec(*keys):
                    for k in keys:
                        _n[0] += 1
                        dvm[k] = _n[0]

                for _c in range(NCF):
                    if _c >= 1:
                        _rec(("kp1e", _c - 1))
                    for _j in ((0, 1) if _c == 0 else
                               (_c + 1,) if _c + 1 < NCF else ()):
                        _rec(("var", _j), ("xnbf", _j), ("x1bf", _j))
                    if _c >= 2:
                        _rec(("kvac", _c - 2))
                    _rec(("xnfm", _c), ("x1fm", _c))
                    if _c >= 1:
                        _rec(("vevc", _c - 1), ("khat", _c - 1))
                _rec(("kp1e", NCF - 1), ("vevc", NCF - 1), ("khat", NCF - 1))
                _rec(("kvac", NCF - 2), ("kvac", NCF - 1), ("kvsb", 0))
                m_var = lambda c: dvm[("var", c)]
                m_xnbf = lambda c: dvm[("xnbf", c)]
                m_x1bf = lambda c: dvm[("x1bf", c)]
                m_xnfm = lambda c: dvm[("xnfm", c)]
                m_x1fm = lambda c: dvm[("x1fm", c)]
                m_kp1e = lambda j: dvm[("kp1e", j)]
                m_vevc = lambda j: dvm[("vevc", j)]
                m_khat = lambda j: dvm[("khat", j)]
                m_kvac = lambda j: dvm[("kvac", j)]
                acm = {}
                _na = [ac0]

                def _reca(*keys):
                    for k in keys:
                        _na[0] += 1
                        acm[k] = _na[0]

                for _c in range(NCF):
                    for _j in ((0, 1) if _c == 0 else
                               (_c + 1,) if _c + 1 < NCF else ()):
                        _reca(("sq", _j))
                    _reca(("ln", _c), ("rstd", _c), ("lnv2", _c), ("r01", _c))
                    if _c >= 1:
                        _reca(("kexp", _c - 1))
                _reca(("kexp", NCF - 1))
                m_sq = lambda c: acm[("sq", c)]
                m_rstd = lambda c: acm[("rstd", c)]
                m_r01 = lambda c: acm[("r01", c)]
                m_kexp = lambda j: acm[("kexp", j)]
                xmark = {}
                mmk = {}

                @block.sync
                def _(sync):
                    for c in range(NCF):
                        if c >= 2:
                            sync.wait_ge(dv.h, m_x1bf(c - 2))
                            sync.wait_ge(ac.h, m_sq(c - 2))
                        smx[c % 2].dinc(sync.dma_start(out=xbuf[:, c % 2, :],
                                                       in_=dx[c * 128:(c + 1) * 128, :]))
                        xmark[c] = smx[c % 2].n

                @block.tensor
                def _(t):
                    t.wait_ge(gp.h, p1w_gp)
                    t.wait_ge(smid.h, 16)

                    def kv_mm(cc):
                        rr = cc % 2
                        t.wait_ge(dv.h, m_khat(cc))
                        if cc >= 1:
                            t.wait_ge(dv.h, m_kvac(cc - 1))
                        for p in range(8):
                            kvp = kvA if p < 4 else kvB
                            sl2 = p % 4
                            ha, hb = 2 * p, 2 * p + 1
                            t.matmul(kvp[0:64, sl2, 0:65],
                                     khat[:, rr, ha * 64:ha * 64 + 64],
                                     vsb[:, rr, ha, :], start=True, stop=True,
                                     tile_position=(0, 0))
                            ins = t.matmul(kvp[64:128, sl2, 0:65],
                                           khat[:, rr, hb * 64:hb * 64 + 64],
                                           vsb[:, rr, hb, :], start=True,
                                           stop=True, tile_position=(0, 64))
                        mm.inc(ins)
                        mmk[("kv", cc)] = mm.n

                    for c in range(NCF + 1):
                        if c < NCF:
                            r = c % 2
                            t.wait_ge(dv.h, m_xnbf(c))
                            if c >= 1:
                                t.wait_ge(dv.h, m_xnfm(c - 1))
                            for j in range(8):
                                ins = t.transpose(xnT[:, j * 128:(j + 1) * 128],
                                                  xn_bf[:, r, j * 128:(j + 1) * 128],
                                                  ident[:])
                            mm.inc(ins)
                            mmk[("xnT", c)] = mm.n
                        if c >= 1:
                            cc, rr = c - 1, (c - 1) % 2
                            t.wait_ge(dv.h, m_xnfm(cc))
                            if cc >= 1:
                                t.wait_ge(dv.h, m_vevc(cc - 1))
                            for jc in range(8):
                                sl = slice(jc * 128, (jc + 1) * 128)
                                for fc in range(8):
                                    t.matmul(psC[:, sl], wk_sb[:, fc, sl],
                                             xnfm[:, rr, fc * 128:(fc + 1) * 128],
                                             start=(fc == 0), stop=False)
                                ins = t.matmul(psC[:, sl], bk_bf[:, sl],
                                               ones2[0:1, 0:128], start=False, stop=True)
                            mm.inc(ins)
                            mmk[("kpre", cc)] = mm.n
                        if c < NCF:
                            r = c % 2
                            t.wait_ge(dv.h, m_x1bf(c))
                            if c >= 1:
                                t.wait_ge(dv.h, m_x1fm(c - 1))
                            for j in range(8):
                                ins = t.transpose(x1T[:, j * 128:(j + 1) * 128],
                                                  x1bf[:, r, j * 128:(j + 1) * 128],
                                                  ident[:])
                            mm.inc(ins)
                            mmk[("x1T", c)] = mm.n
                        if c >= 2:
                            kv_mm(c - 2)
                        if c >= 1:
                            cc, rr = c - 1, (c - 1) % 2
                            t.wait_ge(dv.h, m_kp1e(cc))
                            if cc >= 1:
                                t.wait_ge(ac.h, m_kexp(cc - 1))
                            for nt in range(2):
                                sl = slice(nt * 512, (nt + 1) * 512)
                                for jc in range(8):
                                    t.matmul(psD[:, sl],
                                             kp1[:, rr, jc * 128:(jc + 1) * 128],
                                             wfk_sb[:, jc, sl], start=(jc == 0),
                                             stop=False)
                                ins = t.matmul(psD[:, sl], ones2[0:1, 0:128],
                                               bfk_bf[:, sl], start=False, stop=True)
                            mm.inc(ins)
                            mmk[("kfk", cc)] = mm.n
                            for nt in range(2):
                                sl = slice(nt * 512, (nt + 1) * 512)
                                for fc in range(8):
                                    t.matmul(psC[:, sl],
                                             xnfm[:, rr, fc * 128:(fc + 1) * 128],
                                             wv_sb[:, fc, sl], start=(fc == 0),
                                             stop=False)
                                ins = t.matmul(psC[:, sl], ones2[0:1, 0:128],
                                               bv_bf[:, sl], start=False, stop=True)
                            mm.inc(ins)
                            mmk[("v", cc)] = mm.n
                    kv_mm(NCF - 1)
                assert mm.n == mm0 + 6 * NCF

                @block.scalar
                def _(sc):
                    def kexp_op(sc, j):
                        rr = j % 2
                        sc.wait_ge(mm.h, mmk[("kfk", j)])
                        ac.inc(sc.activation(out=kescr[:, rr, :], in_=psD[:],
                                             func=AF.Exp, bias=kqb[:],
                                             accum_out=ksum[:, rr, :]))

                    def sq_op(sc, j):
                        rj = j % 2
                        sc.wait_ge(smx[rj].h, xmark[j])
                        ac.inc(sc.activation(out=sqd[:], in_=xbuf[:, rj, :],
                                             func=AF.Square, accum_out=ss[:, rj, :]))

                    for c in range(NCF):
                        r = c % 2
                        if c == 0:
                            sq_op(sc, 0)
                            sq_op(sc, 1)
                        elif c + 1 < NCF:
                            sq_op(sc, c + 1)
                        sc.wait_ge(dv.h, m_var(c))
                        ac.inc(sc.activation(out=lnv[:, r, :], in_=var[:, r, :],
                                             func=AF.Ln, bias=epsc[:]))
                        ac.inc(sc.activation(out=rstd[:, r, :], in_=lnv[:, r, :],
                                             func=AF.Exp, scale=-0.5))
                        ac.inc(sc.activation(out=lnv[:, r, :], in_=var[:, r, :],
                                             func=AF.Ln, scale=1.0 + EPS, bias=eps2c[:]))
                        ac.inc(sc.activation(out=r01[:, r, :], in_=lnv[:, r, :],
                                             func=AF.Exp, scale=-0.5))
                        if c >= 1:
                            kexp_op(sc, c - 1)
                    kexp_op(sc, NCF - 1)
                assert ac.n == ac0 + 6 * NCF

                @block.vector
                def _(v):
                    def kvac_add(v, cc):
                        v.wait_ge(mm.h, mmk[("kv", cc)])
                        if cc >= 1:
                            v.wait_ge(dv.h, m_kvac(cc - 1))
                        if cc == 0:
                            v.tensor_copy(kvacc[:, 0:4, :], kvA[:, :, 0:65])
                            dv.inc(v.tensor_copy(kvacc[:, 4:8, :], kvB[:, :, 0:65]))
                        else:
                            v.tensor_tensor(kvacc[:, 0:4, :], kvA[:, :, 0:65],
                                            kvacc[:, 0:4, :], op=ALU.add)
                            dv.inc(v.tensor_tensor(kvacc[:, 4:8, :], kvB[:, :, 0:65],
                                                   kvacc[:, 4:8, :], op=ALU.add))

                    def kp1_copy(v, j):
                        v.wait_ge(mm.h, mmk[("kpre", j)])
                        dv.inc(v.tensor_copy(kp1[:, j % 2, :], psC[:]))

                    def ktail2(v, j):
                        rr = j % 2
                        v.wait_ge(mm.h, mmk[("v", j)])
                        dv.inc(v.tensor_copy(
                            vsb[:, rr, :, 0:64],
                            psC[:].rearrange("p (h d) -> p h d", h=H)))
                        v.wait_ge(ac.h, m_kexp(j))
                        vs.inc(v.reciprocal(rk[:, rr, :], ksum[:, rr, :]))
                        c_rk = vs.n
                        if j >= 2:
                            v.wait_ge(mm.h, mmk[("kv", j - 2)])
                        v.wait_ge(vs.h, c_rk)
                        dv.inc(v.tensor_scalar_mul(khat[:, rr, :], kescr[:, rr, :],
                                                   rk[:, rr, :]))

                    def ln_chain(v, j):
                        rj = j % 2
                        v.wait_ge(smx[rj].h, xmark[j])
                        if j >= 2:
                            v.wait_ge(mm.h, mmk[("xnT", j - 2)])
                            v.wait_ge(mm.h, mmk[("x1T", j - 2)])
                        vs.inc(v.reduce_sum(s_[:, rj, :], xbuf[:, rj, :], axis=AX.X))
                        c_s = vs.n
                        v.wait_ge(ac.h, m_sq(j))
                        vs.inc(v.tensor_scalar_mul(t1_[:, rj, :], ss[:, rj, :], 1.0 / D))
                        v.wait_ge(vs.h, c_s)
                        vs.inc(v.tensor_scalar_mul(negm[:, rj, :], s_[:, rj, :],
                                                   -1.0 / D))
                        c_negm = vs.n
                        v.wait_ge(vs.h, c_negm)
                        vs.inc(v.tensor_mul(msq[:, rj, :], negm[:, rj, :],
                                            negm[:, rj, :]))
                        v.wait_ge(vs.h, vs.n)
                        dv.inc(v.tensor_sub(var[:, rj, :], t1_[:, rj, :],
                                            msq[:, rj, :]))
                        v.wait_ge(ac.h, m_rstd(j))
                        vs.inc(v.tensor_scalar(xn_tm[:, rj, :], xbuf[:, rj, :],
                                               negm[:, rj, :], rstd[:, rj, :],
                                               op0=ALU.add, op1=ALU.mult))
                        v.wait_ge(ac.h, m_r01(j))
                        vs.inc(v.tensor_scalar(x1pre[:, 0, :], xbuf[:, rj, :],
                                               negm[:, rj, :], r01[:, rj, :],
                                               op0=ALU.add, op1=ALU.mult))
                        v.wait_ge(vs.h, vs.n)
                        vs.inc(v.tensor_mul(x1b[:, 0, :], x1pre[:, 0, :], w1ln_bc[:]))
                        c_x1b = vs.n
                        dv.inc(v.tensor_copy(xn_bf[:, rj, :], xn_tm[:, rj, :]))
                        v.wait_ge(vs.h, c_x1b)
                        dv.inc(v.tensor_tensor(x1bf[:, rj, :], x1b[:, 0, :],
                                               b1ln_bc[:], op=ALU.add))

                    for c in range(NCF):
                        r = c % 2
                        if c >= 1:
                            kp1_copy(v, c - 1)
                        if c == 0:
                            ln_chain(v, 0)
                            ln_chain(v, 1)
                        elif c + 1 < NCF:
                            ln_chain(v, c + 1)
                        if c >= 2:
                            kvac_add(v, c - 2)
                        v.wait_ge(mm.h, mmk[("xnT", c)])
                        dv.inc(v.tensor_copy(xnfm[:, r, :], xnT[:]))
                        v.wait_ge(mm.h, mmk[("x1T", c)])
                        dv.inc(v.tensor_copy(
                            x1_fm[:, :, c * 128:(c + 1) * 128],
                            x1T[:].rearrange("p (a b) -> p a b", a=8)))
                        if c >= 1:
                            ktail2(v, c - 1)
                    kp1_copy(v, NCF - 1)
                    ktail2(v, NCF - 1)
                    kvac_add(v, NCF - 2)
                    kvac_add(v, NCF - 1)
                    v.wait_ge(dv.h, m_kvac(NCF - 1))
                    v.tensor_copy(kv_sb[0:64, :, 0:65], kvacc[0:64, :, :])
                    dv.inc(v.tensor_copy(kv_sb[64:128, :, 65:130], kvacc[64:128, :, :]))
                assert dv.n == dvm[("kvsb", 0)]
            p1_dv, p1_mm = dv.n, mm.n

            if until == 1:
                dbg_srcs = dict(xn_bf=xn_bf, khat=khat, kescr=kescr, ksum=ksum,
                                negmax=negmax, var=var, rstd=rstd, kp1=kp1, vsb=vsb,
                                kvacc=kvacc, xnfm=xnfm, x1bf=x1bf)

                @block.sync
                def _(sync):
                    sync.wait_ge(dv.h, dv.n)
                    sync.wait_ge(mm.h, mm.n)
                    sync.wait_ge(ac.h, ac.n)
                    sync.wait_ge(gp.h, gp.n)
                    sy.dinc(sync.dma_start(out=ddbg, in_=kv_sb[:]))
                    for nm, src in dbg_srcs.items():
                        sy.dinc(sync.dma_start(out=ddbg2[nm], in_=src[:]))
                    sync.wait_ge(sy.h, sy.n)
                return nc

            if "x1fm" in dtaps:
                @block.sync
                def _(sync):
                    sync.wait_ge(dv.h, p1_dv)
                    sy.dinc(sync.dma_start(out=dtaps["x1fm"], in_=x1_fm[:]))
                    sy.dinc(sync.dma_start(out=dtaps["kv"], in_=kv_sb[:]))

            ctxn = pctx.enter_context(nc.sbuf_tensor("ctxn", [128, 8, M], BF16, side="right"))

            with ExitStack() as p2:
                R2 = lambda n, sh, dt=F32: p2.enter_context(nc.sbuf_tensor(n, sh, dt, side="right"))
                wqp = R2("wqp", [128, 2, 8, 128], BF16)
                wkp = R2("wkp", [128, 2, 8, 128], BF16)
                wvp = R2("wvp", [128, 2, 8, 128], BF16)
                k1p = R2("k1p", [128, S], BF16)
                q1p = R2("q1p", [128, M], BF16)
                v1p = R2("v1p", [128, 32, 196], BF16)
                pab = R2("pab", [128, 3, 1024], BF16)
                rabf = R2("rabf", [65, 2048], BF16)
                rbc = R2("rbc", [64, 2, 1024])
                stgb = R2("stgb", [64, 2, 1024], BF16)

                ctxps_a = p2.enter_context(nc.psum_tensor("ctx_a", [128, 1024], F32))
                ctxps_b = p2.enter_context(nc.psum_tensor("ctx_b", [128, 1024], F32))
                sps0 = p2.enter_context(nc.psum_tensor("sps0", [128, 1024], F32))
                sps1 = p2.enter_context(nc.psum_tensor("sps1", [128, 1024], F32))
                spsl = [sps0, sps1]

                @block.vector
                def _(v):
                    v.wait_ge(dv.h, p1_dv)
                    v.memset(v1p[:], 0.0)
                    v.memset(v1p[:, :, 64:65], 1.0)
                    dv.inc(v.memset(v1p[:, :, 129:130], 1.0))

                mm2, ac2, dv2 = mm.n, ac.n, dv.n
                MMP, ACP, DVP = 304, 128, 52
                mk_k1 = lambda p, st: mmb(p) + st + 1
                mk_q1 = lambda p, st: mmb(p) + 8 + st + 1
                mk_v1 = lambda p, c: mmb(p) + 12 + c + 1
                att0 = lambda p, tqh: mmb(p) + 44 + tqh * 130
                mk_sc = lambda p, tqh, it: att0(p, tqh) + (it + 1 if it < 2 else 2 * it - 1)
                mk_pv = lambda p, tqh, j: att0(p, tqh) + (2 * j + 4 if j < 62 else 65 + j)
                mk_bpa = lambda p, tqh: att0(p, tqh) + 129
                mk_bpb = lambda p, tqh: att0(p, tqh) + 130
                mk_exp = lambda p, tqh, it: acb(p) + tqh * 64 + it + 1
                mmb = lambda p: mm2 + MMP * p
                acb = lambda p: ac2 + ACP * p
                dvb = lambda p: dv2 + DVP * p
                mk_k1e = lambda p, st: dvb(p) + st + 1
                mk_q1e = lambda p, st: dvb(p) + 8 + st + 1
                mk_v1e = lambda p, c: dvb(p) + 12 + c + 1
                mk_nr = lambda p, tqh, j: dvb(p) + 44 + 4 * tqh + j
                pslot = lambda p, tqh, j: ((p * 2 + tqh) * 64 + j) % 3
                pldmark = {}
                sgmark = {}

                @block.gpsimd
                def _(g):
                    g.wait_ge(mm.h, p1_mm)
                    g.wait_ge(dv.h, p1_dv)
                    for p in range(8):
                        if p >= 2:
                            g.wait_ge(mm.h, mk_v1(p - 2, 31))
                        r = p % 2
                        for fc in range(8):
                            fr = slice(fc * 128, (fc + 1) * 128)
                            gpp[r].dinc(g.dma_start(out=wqp[:, r, fc, :],
                                                    in_=dwqkv[fr, 128 * p:128 * p + 128]))
                            gpp[r].dinc(g.dma_start(out=wkp[:, r, fc, :],
                                                    in_=dwqkv[fr, D + 128 * p:D + 128 * p + 128]))
                            gpp[r].dinc(g.dma_start(
                                out=wvp[:, r, fc, :],
                                in_=dwqkv[fr, 2 * D + 128 * p:2 * D + 128 * p + 128]))
                        pldmark[p] = gpp[r].n

                @block.sync
                def _(sync):
                    for p in range(8):
                        for tqh in range(2):
                            qsl = slice(tqh * 1024, (tqh + 1) * 1024)
                            sync.wait_ge(dv.h, mk_nr(p, tqh, 4))
                            smg[tqh].dinc(sync.dma_start(out=ctxn[64:128, p, qsl],
                                                         in_=stgb[:, tqh, :]))
                            sgmark[(p, tqh)] = smg[tqh].n

                @block.tensor
                def _(t):
                    for p in range(8):
                        r = p % 2
                        t.wait_ge(gpp[p % 2].h, pldmark[p])
                        t.wait_ge(dv.h, dv2 if p == 0 else mk_nr(p - 1, 1, 2))
                        for st in range(8):
                            ps = sps0[:, 0:512] if st % 2 == 0 else sps1[:, 0:512]
                            if st >= 2:
                                t.wait_ge(dv.h, mk_k1e(p, st - 2))
                            for fc in range(8):
                                ins = t.matmul(ps, wkp[:, r, fc, :],
                                               x1_fm[:, fc, st * 512:(st + 1) * 512],
                                               start=(fc == 0), stop=(fc == 7))
                            mm.inc(ins)
                        for st in range(4):
                            ps = sps0[:, 0:512] if st % 2 == 0 else sps1[:, 0:512]
                            t.wait_ge(dv.h, mk_k1e(p, 6 + st) if st < 2
                                      else mk_q1e(p, st - 2))
                            for fc in range(8):
                                ins = t.matmul(ps, wqp[:, r, fc, :],
                                               x1_fm[:, fc, st * 512:(st + 1) * 512],
                                               start=(fc == 0), stop=(fc == 7))
                            mm.inc(ins)
                        for c in range(32):
                            ps = sps0[:, 0:128] if c % 2 == 0 else sps1[:, 0:128]
                            t.wait_ge(dv.h, mk_q1e(p, 2 + c) if c < 2
                                      else mk_v1e(p, c - 2))
                            for fc in range(8):
                                ins = t.matmul(ps,
                                               x1_fm[:, fc, c * 128:(c + 1) * 128],
                                               wvp[:, r, fc, :],
                                               start=(fc == 0), stop=(fc == 7))
                            mm.inc(ins)
                        t.wait_ge(dv.h, mk_v1e(p, 31))

                        def pv(t, p, tqh, j):
                            jt, jc = j % 2, j // 2
                            slot = pslot(p, tqh, j)
                            nsl = slice(jt * 512, (jt + 1) * 512)
                            ins = t.matmul(ctxps_a[:, nsl],
                                           v1p[:, jc, 0:128],
                                           pab[:, slot, 0:512],
                                           start=(jc == 0), stop=(jc == 31))
                            ins = t.matmul(ctxps_b[:, nsl],
                                           v1p[:, jc, 65:193],
                                           pab[:, slot, 512:1024],
                                           start=(jc == 0), stop=(jc == 31))
                            mm.inc(ins)

                        for tqh in range(2):
                            for it in range(64):
                                tile, c = it % 2, it // 2
                                ping = it % 2
                                if it >= 2:
                                    t.wait_ge(ac.h, mk_exp(p, tqh, it - 2))
                                elif tqh == 1:
                                    t.wait_ge(ac.h, mk_exp(p, 0, 62 + it))
                                    if it == 0:
                                        t.wait_ge(dv.h, mk_nr(p, 0, 2))
                                csl = slice(c * 128, (c + 1) * 128)
                                n0 = tqh * 1024 + tile * 512
                                t.matmul(spsl[ping][:, 0:512],
                                         k1p[0:64, csl], q1p[0:64, n0:n0 + 512],
                                         start=True, stop=True, tile_position=(0, 0))
                                ins = t.matmul(spsl[ping][:, 512:1024],
                                               k1p[64:128, csl],
                                               q1p[64:128, n0:n0 + 512],
                                               start=True, stop=True,
                                               tile_position=(64, 0))
                                mm.inc(ins)
                                if it >= 2:
                                    if it == 2 and tqh == 1:
                                        t.wait_ge(dv.h, mk_nr(p, 0, 4))
                                    pv(t, p, tqh, it - 2)
                            for j in (62, 63):
                                t.wait_ge(ac.h, mk_exp(p, tqh, j))
                                pv(t, p, tqh, j)
                            t.wait_ge(dv.h, mk_nr(p, tqh, 1))
                            for nt in range(2):
                                ins = t.matmul(sps0[0:64, nt * 512:(nt + 1) * 512],
                                               ones2[64:65, 0:64],
                                               rabf[64:65, nt * 512:(nt + 1) * 512],
                                               start=True, stop=True,
                                               tile_position=(64, 0))
                            mm.inc(ins)
                            for nt in range(2):
                                ins = t.matmul(sps1[0:64, nt * 512:(nt + 1) * 512],
                                               ones2[64:65, 0:64],
                                               rabf[64:65, 1024 + nt * 512:1024 + (nt + 1) * 512],
                                               start=True, stop=True,
                                               tile_position=(64, 0))
                            mm.inc(ins)
                assert mm.n == mm2 + 8 * MMP

                @block.scalar
                def _(sc):
                    for p in range(8):
                        for tqh in range(2):
                            for it in range(64):
                                g = (p * 2 + tqh) * 64 + it
                                ping = it % 2
                                sc.wait_ge(mm.h, mk_sc(p, tqh, it))
                                if g >= 3:
                                    gj = g - 3
                                    sc.wait_ge(mm.h, mk_pv(gj // 128,
                                                           (gj // 64) % 2, gj % 64))
                                ac.inc(sc.activation(out=pab[:, g % 3, :],
                                                     in_=spsl[ping][:],
                                                     func=AF.Exp, scale=MSA_SCALE,
                                                     bias=mnegb[:]))
                assert ac.n == ac2 + 8 * ACP

                @block.vector
                def _(v):
                    for p in range(8):
                        for st in range(8):
                            ps = sps0[:, 0:512] if st % 2 == 0 else sps1[:, 0:512]
                            v.wait_ge(mm.h, mk_k1(p, st))
                            dv.inc(v.tensor_copy(k1p[:, st * 512:(st + 1) * 512], ps))
                        for st in range(4):
                            ps = sps0[:, 0:512] if st % 2 == 0 else sps1[:, 0:512]
                            v.wait_ge(mm.h, mk_q1(p, st))
                            dv.inc(v.tensor_copy(q1p[:, st * 512:(st + 1) * 512], ps))
                        for c in range(32):
                            ps = sps0[:, 0:128] if c % 2 == 0 else sps1[:, 0:128]
                            v.wait_ge(mm.h, mk_v1(p, c))
                            v.tensor_copy(v1p[:, c, 0:64], ps[:, 0:64])
                            dv.inc(v.tensor_copy(v1p[:, c, 65:129], ps[:, 64:128]))
                        for tqh in range(2):
                            qsl = slice(tqh * 1024, (tqh + 1) * 1024)
                            v.wait_ge(mm.h, mk_pv(p, tqh, 63))
                            with nc.allow_low_precision(reason="recip rounded to bf16"):
                                v.reciprocal(rabf[64:65, 0:1024], ctxps_a[64:65, :])
                                dv.inc(v.reciprocal(rabf[64:65, 1024:2048],
                                                    ctxps_b[64:65, :]))
                            v.wait_ge(mm.h, mk_bpa(p, tqh))
                            v.tensor_copy(rbc[:, 0, :], sps0[0:64, :])
                            v.wait_ge(mm.h, mk_bpb(p, tqh))
                            if p >= 1:
                                v.wait_ge(smg[tqh].h, sgmark[(p - 1, tqh)])
                            dv.inc(v.tensor_copy(rbc[:, 1, :], sps1[0:64, :]))
                            v.wait_ge(dv.h, mk_nr(p, tqh, 2))
                            dv.inc(v.tensor_tensor(ctxn[0:64, p, qsl],
                                                   ctxps_a[0:64, :], rbc[:, 0, :], op=ALU.mult))
                            dv.inc(v.tensor_tensor(stgb[:, tqh, :],
                                                   ctxps_b[0:64, :], rbc[:, 1, :], op=ALU.mult))
                assert dv.n == dv2 + 8 * DVP
            p2_dv, p2_mm, p2_ac = dv.n, mm.n, ac.n

            if until == 2:
                @block.sync
                def _(sync):
                    sync.wait_ge(dv.h, dv.n)
                    sync.wait_ge(mm.h, mm.n)
                    sync.wait_ge(ac.h, ac.n)
                    sync.wait_ge(gp.h, gp.n)
                    sy.dinc(sync.dma_start(out=ddbg, in_=kv_sb[:]))
                    sync.wait_ge(sy.h, sy.n)
                return nc

        outacc = top.enter_context(nc.sbuf_tensor("outacc", [128, NCO, D], F32, side="left"))
        x2_fm = top.enter_context(nc.sbuf_tensor("x2fm", [128, 8, M], BF16, side="left"))

        if "ctxn" in dtaps:
            @block.sync
            def _(sync):
                sync.wait_ge(dv.h, p2_dv)
                sy.dinc(sync.dma_start(out=dtaps["ctxn"], in_=ctxn[:]))

        with ExitStack() as p34:
            R3 = lambda n, sh, dt=F32: p34.enter_context(nc.sbuf_tensor(n, sh, dt, side="right"))
            wp_sb = R3("wp_sb", [128, 8, D], BF16)
            gw_sb = R3("gw_sb", [128, 8, NE], BF16)
            xbuf2 = R3("xbuf2", [128, 2, D])
            sqd2 = R3("sqd2", [128, D], BF16)
            st2 = R3("st2", [128, 2, 16])
            xn2 = R3("xn2", [128, 2, D])
            x2bf = R3("x2bf", [128, 2, D], BF16)
            lgs = R3("lgs", [128, 2, NE])
            rmx2 = R3("rmx2", [128, 2, 1])

            msaps = p34.enter_context(nc.psum_tensor("msaps", [128, 2, D], F32))
            x2T = p34.enter_context(nc.psum_tensor("x2T", [128, D], BF16))
            lgps = p34.enter_context(nc.psum_tensor("lgps", [128, NE], F32))

            @block.gpsimd
            def _(g):
                g.wait_ge(mm.h, p2_mm)
                g.wait_ge(dv.h, p2_dv)
                g.wait_ge(ac.h, p2_ac)
                g.wait_ge(smg[0].h, sgmark[(7, 0)])
                g.wait_ge(smg[1].h, sgmark[(7, 1)])
                for p in range(8):
                    gp.dinc(g.dma_start(out=wp_sb[:, p, :], in_=dwproj[:, p, :]))
                for fc in range(8):
                    gp.dinc(g.dma_start(out=gw_sb[:, fc, :],
                                        in_=dgatew[fc * 128:(fc + 1) * 128, :]))
            p34_gp = gp.n

            mm3, ac3, dv3 = mm.n, ac.n, dv.n
            xmark3 = {}
            n_var2 = lambda c: dv3 + 6 * c + 1
            n_y1 = lambda c: dv3 + 6 * c + 2
            n_var3 = lambda c: dv3 + 6 * c + 3
            n_x2bf = lambda c: dv3 + 6 * c + 4
            n_x2fm = lambda c: dv3 + 6 * c + 5
            n_mask = lambda c: dv3 + 6 * c + 6
            n_sq2 = lambda c: ac3 + 6 * c + 1
            n_rstd2 = lambda c: ac3 + 6 * c + 3
            n_sqy = lambda c: ac3 + 6 * c + 4
            n_rstd3 = lambda c: ac3 + 6 * c + 6
            n_proj = lambda c: mm3 + 3 * c + 1
            n_x2T = lambda c: mm3 + 3 * c + 2
            n_lg = lambda c: mm3 + 3 * c + 3

            @block.sync
            def _(sync):
                sync.wait_ge(mm.h, p2_mm)
                sync.wait_ge(dv.h, p2_dv)
                sync.wait_ge(ac.h, p2_ac)
                for c in range(NCO):
                    if c >= 2:
                        sync.wait_ge(dv.h, n_y1(c - 2))
                        sync.wait_ge(ac.h, n_sq2(c - 2))
                    smx[c % 2].dinc(sync.dma_start(out=xbuf2[:, c % 2, :],
                                                   in_=dx[c * 128:(c + 1) * 128, :]))
                    xmark3[c] = smx[c % 2].n

            @block.scalar
            def _(sc):
                for c in range(NCO):
                    r = c % 2
                    sc.wait_ge(smx[r].h, xmark3[c])
                    ac.inc(sc.activation(out=sqd2[:], in_=xbuf2[:, r, :],
                                         func=AF.Square, accum_out=st2[:, r, 0:1]))
                    sc.wait_ge(dv.h, n_var2(c))
                    ac.inc(sc.activation(out=st2[:, r, 6:7], in_=st2[:, r, 5:6],
                                         func=AF.Ln, bias=epsc[:]))
                    ac.inc(sc.activation(out=st2[:, r, 7:8], in_=st2[:, r, 6:7],
                                         func=AF.Exp, scale=-0.5))
                    sc.wait_ge(dv.h, n_y1(c))
                    ac.inc(sc.activation(out=sqd2[:], in_=outacc[:, c, :],
                                         func=AF.Square, accum_out=st2[:, r, 8:9]))
                    sc.wait_ge(dv.h, n_var3(c))
                    ac.inc(sc.activation(out=st2[:, r, 14:15], in_=st2[:, r, 13:14],
                                         func=AF.Ln, bias=epsc[:]))
                    ac.inc(sc.activation(out=st2[:, r, 15:16], in_=st2[:, r, 14:15],
                                         func=AF.Exp, scale=-0.5))
            assert ac.n == ac3 + 6 * NCO

            @block.tensor
            def _(t):
                t.wait_ge(gp.h, p34_gp)
                t.wait_ge(dv.h, p2_dv)
                t.wait_ge(smg[0].h, sgmark[(7, 0)])
                t.wait_ge(smg[1].h, sgmark[(7, 1)])
                for c in range(NCO):
                    r = c % 2
                    if c >= 2:
                        t.wait_ge(dv.h, n_y1(c - 2))
                    csl = slice(c * 128, (c + 1) * 128)
                    for nt in range(2):
                        nsl = slice(nt * 512, (nt + 1) * 512)
                        for p in range(8):
                            ins = t.matmul(msaps[:, r, nsl],
                                           ctxn[:, p, csl],
                                           wp_sb[:, p, nsl],
                                           start=(p == 0), stop=(p == 7))
                    mm.inc(ins)
                    t.wait_ge(dv.h, n_x2bf(c))
                    if c >= 1:
                        t.wait_ge(dv.h, n_x2fm(c - 1))
                    for j in range(8):
                        ins = t.transpose(x2T[:, j * 128:(j + 1) * 128],
                                          x2bf[:, r, j * 128:(j + 1) * 128], ident[:])
                    mm.inc(ins)
                    t.wait_ge(dv.h, n_x2fm(c))
                    if c >= 1:
                        t.wait_ge(dv.h, n_mask(c - 1))
                    for fc in range(8):
                        ins = t.matmul(lgps[:], x2_fm[:, fc, csl],
                                       gw_sb[:, fc, :], start=(fc == 0), stop=(fc == 7))
                    mm.inc(ins)
            assert mm.n == mm3 + 3 * NCO

            @block.vector
            def _(v):
                v.wait_ge(dv.h, p2_dv)
                for c in range(NCO):
                    r = c % 2
                    v.wait_ge(smx[r].h, xmark3[c])
                    vs.inc(v.reduce_sum(st2[:, r, 1:2], xbuf2[:, r, :], axis=AX.X))
                    c_s = vs.n
                    v.wait_ge(ac.h, n_sq2(c))
                    vs.inc(v.tensor_scalar_mul(st2[:, r, 4:5], st2[:, r, 0:1], 1.0 / D))
                    v.wait_ge(vs.h, c_s)
                    vs.inc(v.tensor_scalar_mul(st2[:, r, 2:3], st2[:, r, 1:2], -1.0 / D))
                    v.wait_ge(vs.h, vs.n)
                    vs.inc(v.tensor_mul(st2[:, r, 3:4], st2[:, r, 2:3], st2[:, r, 2:3]))
                    v.wait_ge(vs.h, vs.n)
                    dv.inc(v.tensor_sub(st2[:, r, 5:6], st2[:, r, 4:5], st2[:, r, 3:4]))
                    v.wait_ge(ac.h, n_rstd2(c))
                    vs.inc(v.tensor_scalar(xn2[:, r, :], xbuf2[:, r, :], st2[:, r, 2:3],
                                           st2[:, r, 7:8], op0=ALU.add, op1=ALU.mult))
                    v.wait_ge(mm.h, n_proj(c))
                    v.wait_ge(vs.h, vs.n)
                    dv.inc(v.tensor_add(outacc[:, c, :], xn2[:, r, :], msaps[:, r, :]))
                    v.wait_ge(dv.h, n_y1(c))
                    vs.inc(v.reduce_sum(st2[:, r, 9:10], outacc[:, c, :], axis=AX.X))
                    c_s2 = vs.n
                    v.wait_ge(ac.h, n_sqy(c))
                    vs.inc(v.tensor_scalar_mul(st2[:, r, 12:13], st2[:, r, 8:9], 1.0 / D))
                    v.wait_ge(vs.h, c_s2)
                    vs.inc(v.tensor_scalar_mul(st2[:, r, 10:11], st2[:, r, 9:10], -1.0 / D))
                    v.wait_ge(vs.h, vs.n)
                    vs.inc(v.tensor_mul(st2[:, r, 11:12], st2[:, r, 10:11], st2[:, r, 10:11]))
                    v.wait_ge(vs.h, vs.n)
                    dv.inc(v.tensor_sub(st2[:, r, 13:14], st2[:, r, 12:13],
                                        st2[:, r, 11:12]))
                    v.wait_ge(ac.h, n_rstd3(c))
                    vs.inc(v.tensor_scalar(xn2[:, r, :], outacc[:, c, :], st2[:, r, 10:11],
                                           st2[:, r, 15:16], op0=ALU.add, op1=ALU.mult))
                    v.wait_ge(vs.h, vs.n)
                    vs.inc(v.tensor_mul(xn2[:, r, :], xn2[:, r, :], w2ln_bc[:]))
                    v.wait_ge(vs.h, vs.n)
                    dv.inc(v.tensor_tensor(x2bf[:, r, :], xn2[:, r, :], b2ln_bc[:],
                                           op=ALU.add))
                    v.wait_ge(mm.h, n_x2T(c))
                    dv.inc(v.tensor_copy(
                        x2_fm[:, :, c * 128:(c + 1) * 128],
                        x2T[:].rearrange("p (a b) -> p a b", a=8)))
                    v.wait_ge(mm.h, n_lg(c))
                    vs.inc(v.tensor_add(lgs[:, r, :], lgps[:], gb_bc[:]))
                    v.wait_ge(vs.h, vs.n)
                    vs.inc(v.reduce_max(rmx2[:, r, :], lgs[:, r, :], axis=AX.X))
                    v.wait_ge(vs.h, vs.n)
                    dv.inc(v.tensor_single_scalar(masks[:, c, :], lgs[:, r, :],
                                                  rmx2[:, r, :], op=ALU.is_equal))
            assert dv.n == dv3 + 6 * NCO
        p34_dv, p34_mm, p34_ac = dv.n, mm.n, ac.n

        if until == 3:
            @block.sync
            def _(sync):
                sync.wait_ge(dv.h, dv.n)
                sync.wait_ge(mm.h, mm.n)
                sync.wait_ge(ac.h, ac.n)
                sync.wait_ge(gp.h, gp.n)
                sy.dinc(sync.dma_start(out=ddbg, in_=kv_sb[:]))
                sync.wait_ge(sy.h, sy.n)
            return nc

        if "y1" in dtaps:
            @block.sync
            def _(sync):
                sync.wait_ge(dv.h, p34_dv)
                sy.dinc(sync.dma_start(out=dtaps["y1"], in_=outacc[:]))
                sy.dinc(sync.dma_start(out=dtaps["x2fm"], in_=x2_fm[:]))
                sy.dinc(sync.dma_start(out=dtaps["masks"], in_=masks[:]))

        pctx.close()
        p5_sy_taps = sy.n
        with ExitStack() as p5:
            w1sb = p5.enter_context(nc.sbuf_tensor("w1sb", [128, 8, 1024], BF16, side="right"))
            w2sb = p5.enter_context(nc.sbuf_tensor("w2sb", [128, 8, 1024], BF16, side="right"))
            h_sb = p5.enter_context(nc.sbuf_tensor("h_sb", [128, 8, M], BF16, side="right"))

            mm5, ac5, dv5 = mm.n, ac.n, dv.n
            MM5, AC5, DV5 = 8 + 16, 8, 16
            i_of = lambda e, q: e * NQ + q
            w1mark, w2mark = {}, {}
            w1mk = lambda i, hc: mm5 + MM5 * i + hc + 1
            w2mk = lambda i, tt: mm5 + MM5 * i + 8 + tt + 1
            gemk = lambda i, hc: ac5 + AC5 * i + hc + 1
            mamk = lambda i, tt: dv5 + DV5 * i + tt + 1

            @block.gpsimd
            def _(g):
                g.wait_ge(dv.h, p34_dv)
                for e in range(NE):
                    for q in range(NQ):
                        i = i_of(e, q)
                        if i >= 1:
                            g.wait_ge(mm.h, w1mk(i - 1, 7))
                        for fc in range(8):
                            gp1.dinc(g.dma_start(
                                out=w1sb[:, fc, :],
                                in_=dw1q[e, q, fc * 128:(fc + 1) * 128, :]))
                        w1mark[i] = gp1.n
                        if i >= 1:
                            g.wait_ge(mm.h, w2mk(i - 1, 15))
                        for hc in range(8):
                            gp2.dinc(g.dma_start(
                                out=w2sb[:, hc, :],
                                in_=dw2q[e, q, hc * 128:(hc + 1) * 128, :]))
                        w2mark[i] = gp2.n

            with ExitStack() as p5p:
                hps = p5p.enter_context(nc.psum_tensor("hps", [128, 2, M], F32))

                @block.tensor
                def _(t):
                    t.wait_ge(dv.h, p34_dv)
                    for e in range(NE):
                        for q in range(NQ):
                            i = i_of(e, q)
                            t.wait_ge(gp1.h, w1mark[i])
                            if i >= 1:
                                t.wait_ge(dv.h, mamk(i - 1, 15))
                            for hc in range(8):
                                if hc >= 2:
                                    t.wait_ge(ac.h, gemk(i, hc - 2))
                                for nt in range(4):
                                    nsl = slice(nt * 512, (nt + 1) * 512)
                                    for fc in range(8):
                                        ins = t.matmul(
                                            hps[:, hc % 2, nsl],
                                            w1sb[:, fc, hc * 128:(hc + 1) * 128],
                                            x2_fm[:, fc, nsl],
                                            start=(fc == 0), stop=(fc == 7))
                                mm.inc(ins)
                            t.wait_ge(gp2.h, w2mark[i])
                            t.wait_ge(ac.h, gemk(i, 7))
                            for tt in range(NCO):
                                ysl = (tt % 2)
                                if tt >= 2:
                                    t.wait_ge(dv.h, mamk(i, tt - 2))
                                for nt in range(2):
                                    nsl = slice(ysl * 1024 + nt * 512,
                                                ysl * 1024 + (nt + 1) * 512)
                                    for hc in range(8):
                                        ins = t.matmul(
                                            hps[:, 0, nsl],
                                            h_sb[:, hc, tt * 128:(tt + 1) * 128],
                                            w2sb[:, hc, nt * 512:(nt + 1) * 512],
                                            start=(hc == 0), stop=(hc == 7))
                                mm.inc(ins)
                assert mm.n == mm5 + MM5 * NE * NQ

                @block.scalar
                def _(sc):
                    for e in range(NE):
                        for q in range(NQ):
                            i = i_of(e, q)
                            for hc in range(8):
                                sc.wait_ge(mm.h, w1mk(i, hc))
                                if i >= 1:
                                    sc.wait_ge(mm.h, w2mk(i - 1, 15))
                                ac.inc(sc.activation(out=h_sb[:, hc, :],
                                                     in_=hps[:, hc % 2, :],
                                                     func=AF.Gelu_apprx_tanh))
                assert ac.n == ac5 + AC5 * NE * NQ

                @block.vector
                def _(v):
                    v.wait_ge(dv.h, p34_dv)
                    for e in range(NE):
                        for q in range(NQ):
                            i = i_of(e, q)
                            for tt in range(NCO):
                                ysl = (tt % 2)
                                v.wait_ge(mm.h, w2mk(i, tt))
                                if i >= 1:
                                    v.wait_ge(dv.h, mamk(i - 1, tt))
                                dv.inc(v.scalar_tensor_tensor(
                                    outacc[:, tt, :],
                                    hps[:, 0, ysl * 1024:(ysl + 1) * 1024],
                                    masks[:, tt, e:e + 1],
                                    outacc[:, tt, :],
                                    op0=ALU.mult, op1=ALU.add))
                assert dv.n == dv5 + DV5 * NE * NQ
        p5_dv, p5_mm, p5_ac = dv.n, mm.n, ac.n

        if until == 5:
            @block.sync
            def _(sync):
                sync.wait_ge(dv.h, dv.n)
                sync.wait_ge(mm.h, mm.n)
                sync.wait_ge(ac.h, ac.n)
                sync.wait_ge(gp.h, gp.n)
                sy.dinc(sync.dma_start(out=ddbg, in_=kv_sb[:]))
                sync.wait_ge(sy.h, sy.n)
            return nc

        if "acc5" in dtaps:
            @block.sync
            def _(sync):
                sync.wait_ge(dv.h, p5_dv)
                sy.dinc(sync.dma_start(out=dtaps["acc5"], in_=outacc[:]))

        with ExitStack() as p6:
            R6 = lambda n, sh, dt=F32: p6.enter_context(nc.sbuf_tensor(n, sh, dt, side="right"))
            wq6 = R6("wq6", [128, 8, D], BF16)
            wfq6 = R6("wfq6", [128, 8, D], BF16)
            xbuf3 = R6("xbuf3", [128, 2, D])
            sqd3 = R6("sqd3", [128, D], BF16)
            st3 = R6("st3", [128, 2, 8])
            xnbf3 = R6("xnbf3", [128, 2, D], BF16)
            xnfm3 = R6("xnfm3", [128, 2, D], BF16)
            qp1b = R6("qp1b", [128, 2, D], BF16)
            nmx6 = R6("nmx6", [128, 2, 1])
            qtbf = R6("qtbf", [128, 2, D], BF16)
            qfm = R6("qfm", [128, 2, D], BF16)
            rr6 = R6("rr6", [128, H])
            oc = R6("oc", [128, 2, D])

            xnT6 = p6.enter_context(nc.psum_tensor("xnT6", [128, D], BF16))
            qT6 = p6.enter_context(nc.psum_tensor("qT6", [128, D], BF16))
            psA = p6.enter_context(nc.psum_tensor("psA", [128, D], F32))
            ctx6 = [p6.enter_context(nc.psum_tensor(f"ctx6_{j}", [128, 2, 256], F32))
                    for j in range(4)]

            @block.gpsimd
            def _(g):
                g.wait_ge(mm.h, p5_mm)
                g.wait_ge(dv.h, p5_dv)
                for wsrc, wdst in ((dwqT, wq6), (dwfqT, wfq6)):
                    for fc in range(8):
                        gp.dinc(g.dma_start(out=wdst[:, fc, :],
                                            in_=wsrc[fc * 128:(fc + 1) * 128, :]))
            p6_gp = gp.n

            mm6, ac6, dv6 = mm.n, ac.n, dv.n
            o_var = lambda c: dv6 + (1 if c == 0 else 6 * c - 1)
            o_xnbf = lambda c: dv6 + (2 if c == 0 else 6 * c)
            o_xnfm = lambda c: dv6 + (3 if c == 0 else 6 * c + 3)
            o_qp1e = lambda c: dv6 + (4 if c == 0 else 6 * c + 4)
            o_qfme = lambda c: dv6 + (6 * c + 7 if c < NCO - 1 else 6 * NCO - 1)
            o_asm = lambda c: dv6 + (6 * c + 8 if c < NCO - 1 else 6 * NCO)
            o_sq = lambda c: ac6 + 4 * c + 1
            o_rstd = lambda c: ac6 + 4 * c + 3
            o_qexp = lambda c: ac6 + 4 * c + 4
            o_xnT = lambda c: mm6 + (1 if c == 0 else 5 * c - 1)
            o_qp1 = lambda c: mm6 + (2 if c == 0 else 5 * c + 2)
            o_qp2 = lambda c: mm6 + (3 if c == 0 else 5 * c + 3)
            o_qT = lambda c: mm6 + (5 * c + 5 if c < NCO - 1 else 5 * NCO - 1)
            o_ctx = lambda c: mm6 + (5 * c + 6 if c < NCO - 1 else 5 * NCO)
            sy_marks = {}

            @block.sync
            def _(sync):
                sync.wait_ge(dv.h, p5_dv)
                for c in range(NCO):
                    if c >= 2:
                        sync.wait_ge(dv.h, o_asm(c - 2))
                        sync.wait_ge(ac.h, o_sq(c - 2))
                    smx[c % 2].dinc(sync.dma_start(out=xbuf3[:, c % 2, :],
                                                   in_=dx[c * 128:(c + 1) * 128, :]))
                    sy_marks[("x", c)] = smx[c % 2].n
                    if c >= 1:
                        sync.wait_ge(dv.h, o_asm(c - 1))
                        smy[(c - 1) % 2].dinc(sync.dma_start(
                            out=dy[(c - 1) * 128:c * 128, :], in_=oc[:, (c - 1) % 2, :]))
                        sy_marks[("y", c - 1)] = smy[(c - 1) % 2].n
                sync.wait_ge(dv.h, o_asm(NCO - 1))
                smy[(NCO - 1) % 2].dinc(sync.dma_start(out=dy[(NCO - 1) * 128:NCO * 128, :],
                                                       in_=oc[:, (NCO - 1) % 2, :]))
                sy_marks[("y", NCO - 1)] = smy[(NCO - 1) % 2].n
                sync.wait_ge(smy[0].h, smy[0].n)
                sync.wait_ge(smy[1].h, smy[1].n)
                if sy.n:
                    sync.wait_ge(sy.h, sy.n)

            @block.scalar
            def _(sc):
                sc.wait_ge(gp.h, p6_gp)
                for c in range(NCO):
                    r = c % 2
                    sc.wait_ge(smx[r].h, sy_marks[("x", c)])
                    ac.inc(sc.activation(out=sqd3[:], in_=xbuf3[:, r, :],
                                         func=AF.Square, accum_out=st3[:, r, 0:1]))
                    sc.wait_ge(dv.h, o_var(c))
                    ac.inc(sc.activation(out=st3[:, r, 6:7], in_=st3[:, r, 5:6],
                                         func=AF.Ln, bias=epsc[:]))
                    ac.inc(sc.activation(out=st3[:, r, 7:8], in_=st3[:, r, 6:7],
                                         func=AF.Exp, scale=-0.5))
                    sc.wait_ge(mm.h, o_qp2(c))
                    ac.inc(sc.activation(out=qtbf[:, r, :], in_=psA[:],
                                         func=AF.Exp, bias=kqb[:]))
            assert ac.n == ac6 + 4 * NCO

            @block.tensor
            def _(t):
                t.wait_ge(gp.h, p6_gp)

                def qt_ctx(cc):
                    rr = cc % 2
                    t.wait_ge(ac.h, o_qexp(cc))
                    if cc >= 1:
                        t.wait_ge(dv.h, o_qfme(cc - 1))
                    for j in range(8):
                        ins = t.transpose(qT6[:, j * 128:(j + 1) * 128],
                                          qtbf[:, rr, j * 128:(j + 1) * 128], ident[:])
                    mm.inc(ins)
                    t.wait_ge(dv.h, o_qfme(cc))
                    if cc >= 1:
                        t.wait_ge(dv.h, o_asm(cc - 1))
                    for p in range(8):
                        ins = t.matmul(ctx6[p // 2][:, p % 2, 0:130],
                                       qfm[:, rr, p * 128:(p + 1) * 128],
                                       kv_sb[:, p, 0:130],
                                       start=True, stop=True)
                    mm.inc(ins)

                for c in range(NCO):
                    r = c % 2
                    t.wait_ge(dv.h, o_xnbf(c))
                    if c >= 1:
                        t.wait_ge(dv.h, o_xnfm(c - 1))
                    for j in range(8):
                        ins = t.transpose(xnT6[:, j * 128:(j + 1) * 128],
                                          xnbf3[:, r, j * 128:(j + 1) * 128], ident[:])
                    mm.inc(ins)
                    if c >= 1:
                        qt_ctx(c - 1)
                    t.wait_ge(dv.h, o_xnfm(c))
                    if c >= 1:
                        t.wait_ge(ac.h, o_qexp(c - 1))
                    for jc in range(8):
                        sl = slice(jc * 128, (jc + 1) * 128)
                        for fc in range(8):
                            t.matmul(psA[:, sl], wq6[:, fc, sl],
                                     xnfm3[:, r, fc * 128:(fc + 1) * 128],
                                     start=(fc == 0), stop=False)
                        ins = t.matmul(psA[:, sl], bq_bf[:, sl],
                                       ones2[0:1, 0:128], start=False, stop=True)
                    mm.inc(ins)
                    t.wait_ge(dv.h, o_qp1e(c))
                    for nt in range(2):
                        sl = slice(nt * 512, (nt + 1) * 512)
                        for jc in range(8):
                            t.matmul(psA[:, sl], qp1b[:, r, jc * 128:(jc + 1) * 128],
                                     wfq6[:, jc, sl], start=(jc == 0), stop=False)
                        ins = t.matmul(psA[:, sl], ones2[0:1, 0:128],
                                       bfq_bf[:, sl], start=False, stop=True)
                    mm.inc(ins)
                qt_ctx(NCO - 1)
            assert mm.n == mm6 + 5 * NCO

            @block.vector
            def _(v):
                v.wait_ge(dv.h, p5_dv)

                def qfm_asm(v, cc):
                    rr = cc % 2
                    v.wait_ge(mm.h, o_qT(cc))
                    dv.inc(v.tensor_copy(qfm[:, rr, :], qT6[:]))
                    if cc >= 2:
                        v.wait_ge(smy[rr].h, sy_marks[("y", cc - 2)])
                    vs.inc(v.tensor_add(oc[:, rr, :], xbuf3[:, rr, :],
                                        outacc[:, cc, :]))
                    v.wait_ge(mm.h, o_ctx(cc))
                    for h in range(H):
                        p, side = h // 2, h % 2
                        cps = ctx6[p // 2]
                        ins = v.reciprocal(rr6[:, h:h + 1],
                                           cps[:, p % 2, 65 * side + 64:65 * side + 65])
                    vs.inc(ins)
                    v.wait_ge(vs.h, vs.n)
                    for h in range(H):
                        p, side = h // 2, h % 2
                        cps = ctx6[p // 2]
                        ins = v.scalar_tensor_tensor(
                            oc[:, rr, h * 64:(h + 1) * 64],
                            cps[:, p % 2, 65 * side:65 * side + 64],
                            rr6[:, h:h + 1], oc[:, rr, h * 64:(h + 1) * 64],
                            op0=ALU.mult, op1=ALU.add)
                    dv.inc(ins)

                for c in range(NCO):
                    r = c % 2
                    v.wait_ge(smx[r].h, sy_marks[("x", c)])
                    vs.inc(v.reduce_sum(st3[:, r, 1:2], xbuf3[:, r, :], axis=AX.X))
                    c_s = vs.n
                    v.wait_ge(ac.h, o_sq(c))
                    vs.inc(v.tensor_scalar_mul(st3[:, r, 4:5], st3[:, r, 0:1], 1.0 / D))
                    v.wait_ge(vs.h, c_s)
                    vs.inc(v.tensor_scalar_mul(st3[:, r, 2:3], st3[:, r, 1:2], -1.0 / D))
                    v.wait_ge(vs.h, vs.n)
                    vs.inc(v.tensor_mul(st3[:, r, 3:4], st3[:, r, 2:3], st3[:, r, 2:3]))
                    v.wait_ge(vs.h, vs.n)
                    dv.inc(v.tensor_sub(st3[:, r, 5:6], st3[:, r, 4:5], st3[:, r, 3:4]))
                    v.wait_ge(ac.h, o_rstd(c))
                    dv.inc(v.tensor_scalar(xnbf3[:, r, :], xbuf3[:, r, :], st3[:, r, 2:3],
                                           st3[:, r, 7:8], op0=ALU.add, op1=ALU.mult))
                    if c >= 1:
                        qfm_asm(v, c - 1)
                    v.wait_ge(mm.h, o_xnT(c))
                    dv.inc(v.tensor_copy(xnfm3[:, r, :], xnT6[:]))
                    v.wait_ge(mm.h, o_qp1(c))
                    dv.inc(v.tensor_copy(qp1b[:, r, :], psA[:]))
                qfm_asm(v, NCO - 1)
            assert dv.n == dv6 + 6 * NCO

    return nc


def _stage(inputs):
    g = lambda k: np.asarray(inputs["L0_" + k])
    x = g("attn_hidden_states")
    shared = {
        "wqT": np.ascontiguousarray(g("attn_Wq").T),
        "wkT": np.ascontiguousarray(g("attn_Wk").T),
        "wvT": np.ascontiguousarray(g("attn_Wv").T),
        "wfqT": np.ascontiguousarray(g("attn_Wfq").T),
        "wfkT": np.ascontiguousarray(g("attn_Wfk").T),
        "wqkv": np.ascontiguousarray(g("ffn_w_qkv")),
        "wprojp": np.ascontiguousarray(g("ffn_w_proj").reshape(8, 128, D).transpose(1, 0, 2)),
        "gatew": np.ascontiguousarray(g("ffn_gate_w")),
        "w1q": np.ascontiguousarray(g("ffn_w1").reshape(NE, D, NQ, 1024).transpose(0, 2, 1, 3)),
        "w2q": np.ascontiguousarray(g("ffn_w2").reshape(NE, NQ, 1024, D)),
        "bq": g("attn_bq").reshape(1, D), "bk": g("attn_bk").reshape(1, D),
        "bv": g("attn_bv").reshape(1, D), "bfq": g("attn_bfq").reshape(1, D),
        "bfk": g("attn_bfk").reshape(1, D),
        "ln1w": g("ffn_ln1_w").reshape(1, D), "ln1b": g("ffn_ln1_b").reshape(1, D),
        "ln2w": g("ffn_ln2_w").reshape(1, D), "ln2b": g("ffn_ln2_b").reshape(1, D),
        "gateb": g("ffn_gate_b").reshape(1, NE),
    }
    shared = {k: np.ascontiguousarray(v, dtype=np.float32) for k, v in shared.items()}
    in_maps = []
    for c in range(8):
        b, hf = c // 2, c % 2
        xb = x[b]
        xc = np.ascontiguousarray(np.concatenate(
            [xb[hf * M:(hf + 1) * M], xb[(1 - hf) * M:(2 - hf) * M]], axis=0),
            dtype=np.float32)
        im = dict(shared)
        im["x"] = xc
        in_maps.append(im)
    return in_maps


def kernel(**inputs):
    in_maps = _stage(inputs)
    nc = build_program()
    res = run_bass_kernel_spmd(nc, in_maps, core_ids=list(range(8)))
    out = np.empty((4, S, D), dtype=np.float32)
    for c in range(8):
        b, hf = c // 2, c % 2
        out[b, hf * M:(hf + 1) * M, :] = res.results[c]["y"]
    return out


# version 15 (final)
# speedup vs baseline: 1.0819x; 1.0391x over previous
import numpy as np
from contextlib import ExitStack

import concourse.bass as bass
import concourse.mybir as mybir
import concourse.bass_utils as _bass_utils
from concourse.bass_utils import run_bass_kernel_spmd

if not getattr(_bass_utils, "_ldw_opt_patched", False):
    _orig_run_command = _bass_utils.run_command

    def _run_command_ldw(argv, **kwargs):
        argv = ["--enable-ldw-opt=true" if a == "--enable-ldw-opt=false" else a
                for a in argv]
        return _orig_run_command(argv, **kwargs)

    _bass_utils.run_command = _run_command_ldw
    _bass_utils._ldw_opt_patched = True

F32 = mybir.dt.float32
BF16 = mybir.dt.bfloat16
AF = mybir.ActivationFunctionType
ALU = mybir.AluOpType
AX = mybir.AxisListType

S, M, D, H = 4096, 2048, 1024, 16
NE, NQ = 8, 4
EPS = 1e-5
MSA_SCALE = 0.125
MSA_B = 12.0
KQ_B = 8.0
NCF = S // 128
NCO = M // 128


class Ctr:
    def __init__(self, nc, ctx, name):
        self.h = ctx.enter_context(nc.semaphore(name))
        self.n = 0

    def inc(self, instr, k=1):
        instr.then_inc(self.h, k)
        self.n += k
        return instr

    def dinc(self, instr):
        return self.inc(instr, 16)


def build_program(taps=(), until=None):
    import ml_dtypes
    nc = bass.Bass("TRN2", target_bir_lowering=False, detect_race_conditions=False)

    dx = nc.dram_tensor("x", [S, D], F32, kind="ExternalInput").ap()
    dwqT = nc.dram_tensor("wqT", [D, D], F32, kind="ExternalInput").ap()
    dwkT = nc.dram_tensor("wkT", [D, D], F32, kind="ExternalInput").ap()
    dwvT = nc.dram_tensor("wvT", [D, D], F32, kind="ExternalInput").ap()
    dwfqT = nc.dram_tensor("wfqT", [D, D], F32, kind="ExternalInput").ap()
    dwfkT = nc.dram_tensor("wfkT", [D, D], F32, kind="ExternalInput").ap()
    dwqkv = nc.dram_tensor("wqkv", [D, 3 * D], F32, kind="ExternalInput").ap()
    dwproj = nc.dram_tensor("wprojp", [128, 8, D], F32, kind="ExternalInput").ap()
    dgatew = nc.dram_tensor("gatew", [D, NE], F32, kind="ExternalInput").ap()
    dw1q = nc.dram_tensor("w1q", [NE, NQ, D, 1024], F32, kind="ExternalInput").ap()
    dw2q = nc.dram_tensor("w2q", [NE, NQ, 1024, D], F32, kind="ExternalInput").ap()
    dvecs = {nm: nc.dram_tensor(nm, [1, D], F32, kind="ExternalInput").ap()
             for nm in ("bq", "bk", "bv", "bfq", "bfk", "ln1w", "ln1b", "ln2w", "ln2b")}
    dgateb = nc.dram_tensor("gateb", [1, NE], F32, kind="ExternalInput").ap()
    dy = nc.dram_tensor("y", [M, D], F32, kind="ExternalOutput").ap()
    dtaps = {}
    for t in taps:
        shp, dt = {"x1fm": ([128, 8, S], BF16), "kv": ([128, NE, 132], BF16),
                   "ctxn": ([128, 8, M], BF16), "y1": ([128, NCO, D], F32),
                   "x2fm": ([128, 8, M], BF16), "masks": ([128, NCO, NE], F32),
                   "acc5": ([128, NCO, D], F32)}[t]
        dtaps[t] = nc.dram_tensor("t_" + t, shp, dt, kind="ExternalOutput").ap()

    ident_dram = nc.inline_tensor(
        np.eye(128, dtype=np.float32).astype(ml_dtypes.bfloat16), name="identc")
    ddbg = (nc.dram_tensor("dbg", [128, NE, 65], BF16, kind="ExternalOutput").ap()
            if until is not None else None)
    ddbg2 = {}
    if until == 1:
        for nm, sh, dt in (("xn_bf", [128, 2, D], BF16), ("khat", [128, 2, D], BF16),
                           ("kescr", [128, 2, D], BF16), ("ksum", [128, 2, 1], F32),
                           ("negmax", [128, 2, 1], F32), ("var", [128, 2, 1], F32),
                           ("rstd", [128, 2, 1], F32), ("kp1", [128, 2, D], BF16),
                           ("vsb", [128, 2, H, 65], BF16), ("kvacc", [128, NE, 65], F32),
                           ("xnfm", [128, 2, D], BF16), ("x1bf", [128, 2, D], BF16)):
            ddbg2[nm] = nc.dram_tensor("dbg_" + nm, sh, dt, kind="ExternalOutput").ap()

    with ExitStack() as top:
        SBL = lambda n, sh, dt=F32: top.enter_context(nc.sbuf_tensor(n, sh, dt, side="left"))
        ident = SBL("ident", [128, 128], BF16)
        ones2 = SBL("ones2", [128, 512], BF16)
        onesf = SBL("onesf", [1, 128], F32)
        vstage = SBL("vstage", [1, D], F32)
        bq_bf = SBL("bq_bf", [1, D], BF16)
        bk_bf = SBL("bk_bf", [1, D], BF16)
        bv_bf = SBL("bv_bf", [1, D], BF16)
        bfq_bf = SBL("bfq_bf", [1, D], BF16)
        bfk_bf = SBL("bfk_bf", [1, D], BF16)
        w1ln_bc = SBL("w1ln_bc", [128, D], F32)
        b1ln_bc = SBL("b1ln_bc", [128, D], F32)
        w2ln_bc = SBL("w2ln_bc", [128, D], F32)
        b2ln_bc = SBL("b2ln_bc", [128, D], F32)
        gb_bc = SBL("gb_bc", [128, NE], F32)
        kv_sb = SBL("kv_sb", [128, NE, 132], BF16)
        kvacc = SBL("kvacc", [128, NE, 65], F32)
        epsc = SBL("epsc", [128, 1], F32)
        eps2c = SBL("eps2c", [128, 1], F32)
        mnegb = SBL("mnegb", [128, 1], F32)
        kqb = SBL("kqb", [128, 1], F32)
        masks = SBL("masks", [128, NCO, NE], F32)

        sy = Ctr(nc, top, "sy")
        gp = Ctr(nc, top, "gp")
        mm = Ctr(nc, top, "mm")
        ac = Ctr(nc, top, "ac")
        dv = Ctr(nc, top, "dv")
        smid = Ctr(nc, top, "smid")
        smx = [Ctr(nc, top, "smx0"), Ctr(nc, top, "smx1")]
        smy = [Ctr(nc, top, "smy0"), Ctr(nc, top, "smy1")]
        smg = [Ctr(nc, top, "smg0"), Ctr(nc, top, "smg1")]
        gpp = [Ctr(nc, top, "gpp0"), Ctr(nc, top, "gpp1")]
        gp1 = Ctr(nc, top, "gp1")
        gp2 = Ctr(nc, top, "gp2")
        vs = Ctr(nc, top, "vs")

        block = top.enter_context(nc.Block())

        with ExitStack() as p0:
            bcps = p0.enter_context(nc.psum_tensor("bcps", [128, 2, 512], F32))

            @block.sync
            def _(sync):
                smid.dinc(sync.dma_start(out=ident[:], in_=ident_dram.ap()))

            @block.vector
            def _(v):
                v.memset(ones2[:], 1.0)
                v.memset(epsc[:], EPS)
                v.memset(eps2c[:], EPS * EPS)
                v.memset(mnegb[:], -MSA_B)
                v.memset(kqb[:], -KQ_B)
                v.memset(kv_sb[:], 0.0)
                dv.inc(v.memset(onesf[:], 1.0))
            ones_dv = dv.n

            bc_list = [("ln1w", w1ln_bc, D), ("ln1b", b1ln_bc, D),
                       ("ln2w", w2ln_bc, D), ("ln2b", b2ln_bc, D),
                       ("gateb", gb_bc, NE)]
            nseg = lambda w: (w + 511) // 512
            gp_b0, mm_b0, dv_b0 = gp.n, mm.n, dv.n
            cum = [0]
            for _, _, w in bc_list:
                cum.append(cum[-1] + nseg(w))

            @block.gpsimd
            def _(g):
                for nm, dst in (("bq", bq_bf), ("bk", bk_bf), ("bv", bv_bf),
                                ("bfq", bfq_bf), ("bfk", bfk_bf)):
                    gp.dinc(g.dma_start(out=dst[:], in_=dvecs[nm]))
                g.wait_ge(gp.h, gp.n)
                for i, (nm, dst, w) in enumerate(bc_list):
                    if i > 0:
                        g.wait_ge(mm.h, mm_b0 + cum[i])
                    gp.dinc(g.dma_start(out=vstage[:, 0:w], in_=(
                        dvecs[nm] if w == D else dgateb)))
                    g.wait_ge(gp.h, gp.n)

            @block.tensor
            def _(t):
                t.wait_ge(dv.h, ones_dv)
                for i, (nm, dst, w) in enumerate(bc_list):
                    t.wait_ge(gp.h, gp_b0 + 16 * (5 + i + 1))
                    if i > 0:
                        t.wait_ge(dv.h, dv_b0 + i)
                    for si, j in enumerate(range(0, w, 512)):
                        ws = min(512, w - j)
                        mm.inc(t.matmul(bcps[:, si % 2, 0:ws], onesf[:],
                                        vstage[:, j:j + ws], start=True, stop=True))

            @block.vector
            def _(v):
                for i, (nm, dst, w) in enumerate(bc_list):
                    v.wait_ge(mm.h, mm_b0 + cum[i + 1])
                    for si, j in enumerate(range(0, w, 512)):
                        ws = min(512, w - j)
                        ins = v.tensor_copy(dst[:, j:j + ws], bcps[:, si % 2, 0:ws])
                    dv.inc(ins)
            assert dv.n == dv_b0 + 5 and mm.n == mm_b0 + cum[-1]

        pctx = top.enter_context(ExitStack())
        with ExitStack() as px1:
            x1_fm = px1.enter_context(nc.sbuf_tensor("x1_fm", [128, 8, S], BF16, side="left"))

            with ExitStack() as p1:
                R = lambda n, sh, dt=F32: p1.enter_context(nc.sbuf_tensor(n, sh, dt, side="right"))
                wk_sb = R("wk_sb", [128, 8, D], BF16)
                wfk_sb = R("wfk_sb", [128, 8, D], BF16)
                wv_sb = R("wv_sb", [128, 8, D], BF16)
                xbuf = R("xbuf", [128, 2, D])
                sqd = R("sqd", [128, D], BF16)
                ss = R("ss", [128, 2, 1]); s_ = R("s_", [128, 2, 1])
                negm = R("negm", [128, 2, 1]); msq = R("msq", [128, 2, 1])
                t1_ = R("t1_", [128, 2, 1]); var = R("var", [128, 2, 1])
                lnv = R("lnv", [128, 2, 1]); rstd = R("rstd", [128, 2, 1])
                r01 = R("r01", [128, 2, 1])
                xn_tm = R("xn_tm", [128, 2, D])
                xn_bf = R("xn_bf", [128, 2, D], BF16)
                x1b = R("x1b", [128, 1, D])
                x1pre = R("x1pre", [128, 1, D])
                x1bf = R("x1bf", [128, 2, D], BF16)
                xnfm = R("xnfm", [128, 2, D], BF16)
                kp1 = R("kp1", [128, 2, D], BF16)
                negmax = R("negmax", [128, 2, 1])
                kescr = R("kescr", [128, 2, D], BF16)
                ksum = R("ksum", [128, 2, 1]); rk = R("rk", [128, 2, 1])
                khat = R("khat", [128, 2, D], BF16)
                vsb = R("vsb", [128, 2, H, 65], BF16)

                PP = lambda n, sh, dt=F32: p1.enter_context(nc.psum_tensor(n, sh, dt))
                kvA = PP("kvA", [128, 4, 128])
                kvB = PP("kvB", [128, 4, 128])
                xnT = PP("xnT", [128, D], BF16)
                x1T = PP("x1T", [128, D], BF16)
                psC = PP("psC", [128, D])
                psD = PP("psD", [128, D])

                p1w = {}

                @block.gpsimd
                def _(g):
                    for wn, wsrc, wdst in (("wk", dwkT, wk_sb), ("wfk", dwfkT, wfk_sb),
                                           ("wv", dwvT, wv_sb)):
                        for fc in range(8):
                            gp.dinc(g.dma_start(out=wdst[:, fc, :],
                                                in_=wsrc[fc * 128:(fc + 1) * 128, :]))
                        p1w[wn] = gp.n
                p1w_gp = gp.n

                @block.vector
                def _(v):
                    dv.inc(v.memset(vsb[:, :, :, 64:65], 1.0))

                mm0, ac0, dv0 = mm.n, ac.n, dv.n
                dvm = {}
                _n = [dv0]

                def _rec(*keys):
                    for k in keys:
                        _n[0] += 1
                        dvm[k] = _n[0]

                for _c in range(NCF):
                    if _c >= 1:
                        _rec(("kp1e", _c - 1))
                    for _j in ((0, 1) if _c == 0 else
                               (_c + 1,) if _c + 1 < NCF else ()):
                        _rec(("var", _j), ("xnbf", _j), ("x1bf", _j))
                    if _c >= 2:
                        _rec(("kvac", _c - 2))
                    _rec(("xnfm", _c), ("x1fm", _c))
                    if _c >= 1:
                        _rec(("vevc", _c - 1), ("khat", _c - 1))
                _rec(("kp1e", NCF - 1), ("vevc", NCF - 1), ("khat", NCF - 1))
                _rec(("kvac", NCF - 2), ("kvac", NCF - 1), ("kvsb", 0))
                m_var = lambda c: dvm[("var", c)]
                m_xnbf = lambda c: dvm[("xnbf", c)]
                m_x1bf = lambda c: dvm[("x1bf", c)]
                m_xnfm = lambda c: dvm[("xnfm", c)]
                m_x1fm = lambda c: dvm[("x1fm", c)]
                m_kp1e = lambda j: dvm[("kp1e", j)]
                m_vevc = lambda j: dvm[("vevc", j)]
                m_khat = lambda j: dvm[("khat", j)]
                m_kvac = lambda j: dvm[("kvac", j)]
                acm = {}
                _na = [ac0]

                def _reca(*keys):
                    for k in keys:
                        _na[0] += 1
                        acm[k] = _na[0]

                for _c in range(NCF):
                    for _j in ((0, 1) if _c == 0 else
                               (_c + 1,) if _c + 1 < NCF else ()):
                        _reca(("sq", _j))
                    _reca(("ln", _c), ("rstd", _c), ("lnv2", _c), ("r01", _c))
                    if _c >= 1:
                        _reca(("kexp", _c - 1))
                _reca(("kexp", NCF - 1))
                m_sq = lambda c: acm[("sq", c)]
                m_rstd = lambda c: acm[("rstd", c)]
                m_r01 = lambda c: acm[("r01", c)]
                m_kexp = lambda j: acm[("kexp", j)]
                xmark = {}
                mmk = {}

                @block.sync
                def _(sync):
                    for c in range(NCF):
                        if c >= 2:
                            sync.wait_ge(dv.h, m_x1bf(c - 2))
                            sync.wait_ge(ac.h, m_sq(c - 2))
                        smx[c % 2].dinc(sync.dma_start(out=xbuf[:, c % 2, :],
                                                       in_=dx[c * 128:(c + 1) * 128, :]))
                        xmark[c] = smx[c % 2].n

                @block.tensor
                def _(t):
                    t.wait_ge(smid.h, 16)

                    def kv_mm(cc):
                        rr = cc % 2
                        t.wait_ge(dv.h, m_khat(cc))
                        if cc >= 1:
                            t.wait_ge(dv.h, m_kvac(cc - 1))
                        for p in range(8):
                            kvp = kvA if p < 4 else kvB
                            sl2 = p % 4
                            ha, hb = 2 * p, 2 * p + 1
                            t.matmul(kvp[0:64, sl2, 0:65],
                                     khat[:, rr, ha * 64:ha * 64 + 64],
                                     vsb[:, rr, ha, :], start=True, stop=True,
                                     tile_position=(0, 0))
                            ins = t.matmul(kvp[64:128, sl2, 0:65],
                                           khat[:, rr, hb * 64:hb * 64 + 64],
                                           vsb[:, rr, hb, :], start=True,
                                           stop=True, tile_position=(0, 64))
                        mm.inc(ins)
                        mmk[("kv", cc)] = mm.n

                    for c in range(NCF + 1):
                        if c < NCF:
                            r = c % 2
                            t.wait_ge(dv.h, m_xnbf(c))
                            if c >= 1:
                                t.wait_ge(dv.h, m_xnfm(c - 1))
                            for j in range(8):
                                ins = t.transpose(xnT[:, j * 128:(j + 1) * 128],
                                                  xn_bf[:, r, j * 128:(j + 1) * 128],
                                                  ident[:])
                            mm.inc(ins)
                            mmk[("xnT", c)] = mm.n
                        if c >= 1:
                            cc, rr = c - 1, (c - 1) % 2
                            if cc == 0:
                                t.wait_ge(gp.h, p1w["wk"])
                            t.wait_ge(dv.h, m_xnfm(cc))
                            if cc >= 1:
                                t.wait_ge(dv.h, m_vevc(cc - 1))
                            for jc in range(8):
                                sl = slice(jc * 128, (jc + 1) * 128)
                                for fc in range(8):
                                    t.matmul(psC[:, sl], wk_sb[:, fc, sl],
                                             xnfm[:, rr, fc * 128:(fc + 1) * 128],
                                             start=(fc == 0), stop=False)
                                ins = t.matmul(psC[:, sl], bk_bf[:, sl],
                                               ones2[0:1, 0:128], start=False, stop=True)
                            mm.inc(ins)
                            mmk[("kpre", cc)] = mm.n
                        if c < NCF:
                            r = c % 2
                            t.wait_ge(dv.h, m_x1bf(c))
                            if c >= 1:
                                t.wait_ge(dv.h, m_x1fm(c - 1))
                            for j in range(8):
                                ins = t.transpose(x1T[:, j * 128:(j + 1) * 128],
                                                  x1bf[:, r, j * 128:(j + 1) * 128],
                                                  ident[:])
                            mm.inc(ins)
                            mmk[("x1T", c)] = mm.n
                        if c >= 2:
                            kv_mm(c - 2)
                        if c >= 1:
                            cc, rr = c - 1, (c - 1) % 2
                            if cc == 0:
                                t.wait_ge(gp.h, p1w_gp)
                            t.wait_ge(dv.h, m_kp1e(cc))
                            if cc >= 1:
                                t.wait_ge(ac.h, m_kexp(cc - 1))
                            for nt in range(2):
                                sl = slice(nt * 512, (nt + 1) * 512)
                                for jc in range(8):
                                    t.matmul(psD[:, sl],
                                             kp1[:, rr, jc * 128:(jc + 1) * 128],
                                             wfk_sb[:, jc, sl], start=(jc == 0),
                                             stop=False)
                                ins = t.matmul(psD[:, sl], ones2[0:1, 0:128],
                                               bfk_bf[:, sl], start=False, stop=True)
                            mm.inc(ins)
                            mmk[("kfk", cc)] = mm.n
                            for nt in range(2):
                                sl = slice(nt * 512, (nt + 1) * 512)
                                for fc in range(8):
                                    t.matmul(psC[:, sl],
                                             xnfm[:, rr, fc * 128:(fc + 1) * 128],
                                             wv_sb[:, fc, sl], start=(fc == 0),
                                             stop=False)
                                ins = t.matmul(psC[:, sl], ones2[0:1, 0:128],
                                               bv_bf[:, sl], start=False, stop=True)
                            mm.inc(ins)
                            mmk[("v", cc)] = mm.n
                    kv_mm(NCF - 1)
                assert mm.n == mm0 + 6 * NCF

                @block.scalar
                def _(sc):
                    def kexp_op(sc, j):
                        rr = j % 2
                        sc.wait_ge(mm.h, mmk[("kfk", j)])
                        ac.inc(sc.activation(out=kescr[:, rr, :], in_=psD[:],
                                             func=AF.Exp, bias=kqb[:],
                                             accum_out=ksum[:, rr, :]))

                    def sq_op(sc, j):
                        rj = j % 2
                        sc.wait_ge(smx[rj].h, xmark[j])
                        ac.inc(sc.activation(out=sqd[:], in_=xbuf[:, rj, :],
                                             func=AF.Square, accum_out=ss[:, rj, :]))

                    for c in range(NCF):
                        r = c % 2
                        if c == 0:
                            sq_op(sc, 0)
                            sq_op(sc, 1)
                        elif c + 1 < NCF:
                            sq_op(sc, c + 1)
                        sc.wait_ge(dv.h, m_var(c))
                        ac.inc(sc.activation(out=lnv[:, r, :], in_=var[:, r, :],
                                             func=AF.Ln, bias=epsc[:]))
                        ac.inc(sc.activation(out=rstd[:, r, :], in_=lnv[:, r, :],
                                             func=AF.Exp, scale=-0.5))
                        ac.inc(sc.activation(out=lnv[:, r, :], in_=var[:, r, :],
                                             func=AF.Ln, scale=1.0 + EPS, bias=eps2c[:]))
                        ac.inc(sc.activation(out=r01[:, r, :], in_=lnv[:, r, :],
                                             func=AF.Exp, scale=-0.5))
                        if c >= 1:
                            kexp_op(sc, c - 1)
                    kexp_op(sc, NCF - 1)
                assert ac.n == ac0 + 6 * NCF

                @block.vector
                def _(v):
                    def kvac_add(v, cc):
                        v.wait_ge(mm.h, mmk[("kv", cc)])
                        if cc >= 1:
                            v.wait_ge(dv.h, m_kvac(cc - 1))
                        if cc == 0:
                            v.tensor_copy(kvacc[:, 0:4, :], kvA[:, :, 0:65])
                            dv.inc(v.tensor_copy(kvacc[:, 4:8, :], kvB[:, :, 0:65]))
                        else:
                            v.tensor_tensor(kvacc[:, 0:4, :], kvA[:, :, 0:65],
                                            kvacc[:, 0:4, :], op=ALU.add)
                            dv.inc(v.tensor_tensor(kvacc[:, 4:8, :], kvB[:, :, 0:65],
                                                   kvacc[:, 4:8, :], op=ALU.add))

                    def kp1_copy(v, j):
                        v.wait_ge(mm.h, mmk[("kpre", j)])
                        dv.inc(v.tensor_copy(kp1[:, j % 2, :], psC[:]))

                    def ktail2(v, j):
                        rr = j % 2
                        v.wait_ge(mm.h, mmk[("v", j)])
                        dv.inc(v.tensor_copy(
                            vsb[:, rr, :, 0:64],
                            psC[:].rearrange("p (h d) -> p h d", h=H)))
                        v.wait_ge(ac.h, m_kexp(j))
                        vs.inc(v.reciprocal(rk[:, rr, :], ksum[:, rr, :]))
                        c_rk = vs.n
                        if j >= 2:
                            v.wait_ge(mm.h, mmk[("kv", j - 2)])
                        v.wait_ge(vs.h, c_rk)
                        dv.inc(v.tensor_scalar_mul(khat[:, rr, :], kescr[:, rr, :],
                                                   rk[:, rr, :]))

                    def ln_chain(v, j):
                        rj = j % 2
                        v.wait_ge(smx[rj].h, xmark[j])
                        if j >= 2:
                            v.wait_ge(mm.h, mmk[("xnT", j - 2)])
                            v.wait_ge(mm.h, mmk[("x1T", j - 2)])
                        vs.inc(v.reduce_sum(s_[:, rj, :], xbuf[:, rj, :], axis=AX.X))
                        c_s = vs.n
                        v.wait_ge(ac.h, m_sq(j))
                        vs.inc(v.tensor_scalar_mul(t1_[:, rj, :], ss[:, rj, :], 1.0 / D))
                        v.wait_ge(vs.h, c_s)
                        vs.inc(v.tensor_scalar_mul(negm[:, rj, :], s_[:, rj, :],
                                                   -1.0 / D))
                        c_negm = vs.n
                        v.wait_ge(vs.h, c_negm)
                        vs.inc(v.tensor_mul(msq[:, rj, :], negm[:, rj, :],
                                            negm[:, rj, :]))
                        v.wait_ge(vs.h, vs.n)
                        dv.inc(v.tensor_sub(var[:, rj, :], t1_[:, rj, :],
                                            msq[:, rj, :]))
                        v.wait_ge(ac.h, m_rstd(j))
                        vs.inc(v.tensor_scalar(xn_tm[:, rj, :], xbuf[:, rj, :],
                                               negm[:, rj, :], rstd[:, rj, :],
                                               op0=ALU.add, op1=ALU.mult))
                        v.wait_ge(ac.h, m_r01(j))
                        vs.inc(v.tensor_scalar(x1pre[:, 0, :], xbuf[:, rj, :],
                                               negm[:, rj, :], r01[:, rj, :],
                                               op0=ALU.add, op1=ALU.mult))
                        v.wait_ge(vs.h, vs.n)
                        vs.inc(v.tensor_mul(x1b[:, 0, :], x1pre[:, 0, :], w1ln_bc[:]))
                        c_x1b = vs.n
                        dv.inc(v.tensor_copy(xn_bf[:, rj, :], xn_tm[:, rj, :]))
                        v.wait_ge(vs.h, c_x1b)
                        dv.inc(v.tensor_tensor(x1bf[:, rj, :], x1b[:, 0, :],
                                               b1ln_bc[:], op=ALU.add))

                    for c in range(NCF):
                        r = c % 2
                        if c >= 1:
                            kp1_copy(v, c - 1)
                        if c == 0:
                            ln_chain(v, 0)
                            ln_chain(v, 1)
                        elif c + 1 < NCF:
                            ln_chain(v, c + 1)
                        if c >= 2:
                            kvac_add(v, c - 2)
                        v.wait_ge(mm.h, mmk[("xnT", c)])
                        dv.inc(v.tensor_copy(xnfm[:, r, :], xnT[:]))
                        v.wait_ge(mm.h, mmk[("x1T", c)])
                        dv.inc(v.tensor_copy(
                            x1_fm[:, :, c * 128:(c + 1) * 128],
                            x1T[:].rearrange("p (a b) -> p a b", a=8)))
                        if c >= 1:
                            ktail2(v, c - 1)
                    kp1_copy(v, NCF - 1)
                    ktail2(v, NCF - 1)
                    kvac_add(v, NCF - 2)
                    kvac_add(v, NCF - 1)
                    v.wait_ge(dv.h, m_kvac(NCF - 1))
                    v.tensor_copy(kv_sb[0:64, :, 0:65], kvacc[0:64, :, :])
                    dv.inc(v.tensor_copy(kv_sb[64:128, :, 65:130], kvacc[64:128, :, :]))
                assert dv.n == dvm[("kvsb", 0)]
            p1_dv, p1_mm = dv.n, mm.n

            if until == 1:
                dbg_srcs = dict(xn_bf=xn_bf, khat=khat, kescr=kescr, ksum=ksum,
                                negmax=negmax, var=var, rstd=rstd, kp1=kp1, vsb=vsb,
                                kvacc=kvacc, xnfm=xnfm, x1bf=x1bf)

                @block.sync
                def _(sync):
                    sync.wait_ge(dv.h, dv.n)
                    sync.wait_ge(mm.h, mm.n)
                    sync.wait_ge(ac.h, ac.n)
                    sync.wait_ge(gp.h, gp.n)
                    sy.dinc(sync.dma_start(out=ddbg, in_=kv_sb[:]))
                    for nm, src in dbg_srcs.items():
                        sy.dinc(sync.dma_start(out=ddbg2[nm], in_=src[:]))
                    sync.wait_ge(sy.h, sy.n)
                return nc

            if "x1fm" in dtaps:
                @block.sync
                def _(sync):
                    sync.wait_ge(dv.h, p1_dv)
                    sy.dinc(sync.dma_start(out=dtaps["x1fm"], in_=x1_fm[:]))
                    sy.dinc(sync.dma_start(out=dtaps["kv"], in_=kv_sb[:]))

            ctxn = pctx.enter_context(nc.sbuf_tensor("ctxn", [128, 8, M], BF16, side="right"))

            with ExitStack() as p2:
                R2 = lambda n, sh, dt=F32: p2.enter_context(nc.sbuf_tensor(n, sh, dt, side="right"))
                wqp = R2("wqp", [128, 2, 8, 128], BF16)
                wkp = R2("wkp", [128, 2, 8, 128], BF16)
                wvp = R2("wvp", [128, 2, 8, 128], BF16)
                k1p = R2("k1p", [128, S], BF16)
                q1p = R2("q1p", [128, M], BF16)
                v1p = R2("v1p", [128, 32, 196], BF16)
                pab = R2("pab", [128, 3, 1024], BF16)
                rabf = R2("rabf", [65, 2048], BF16)
                rln = R2("rln", [65, 2048])
                rbc = R2("rbc", [64, 2, 1024])
                stgb = R2("stgb", [64, 2, 1024], BF16)

                ctxps_a = p2.enter_context(nc.psum_tensor("ctx_a", [128, 1024], F32))
                ctxps_b = p2.enter_context(nc.psum_tensor("ctx_b", [128, 1024], F32))
                sps0 = p2.enter_context(nc.psum_tensor("sps0", [128, 1024], F32))
                sps1 = p2.enter_context(nc.psum_tensor("sps1", [128, 1024], F32))
                spsl = [sps0, sps1]

                @block.vector
                def _(v):
                    v.wait_ge(dv.h, p1_dv)
                    v.memset(v1p[:], 0.0)
                    v.memset(v1p[:, :, 64:65], 1.0)
                    dv.inc(v.memset(v1p[:, :, 129:130], 1.0))

                mm2, ac2, dv2 = mm.n, ac.n, dv.n
                MMP, ACP, DVP = 304, 132, 50
                mk_k1 = lambda p, st: mmb(p) + st + 1
                mk_q1 = lambda p, st: mmb(p) + 8 + st + 1
                mk_v1 = lambda p, c: mmb(p) + 12 + c + 1
                att0 = lambda p, tqh: mmb(p) + 44 + tqh * 130
                mk_sc = lambda p, tqh, it: att0(p, tqh) + (it + 1 if it < 2 else 2 * it - 1)
                mk_pv = lambda p, tqh, j: att0(p, tqh) + (2 * j + 4 if j < 62 else 65 + j)
                mk_bpa = lambda p, tqh: att0(p, tqh) + 129
                mk_bpb = lambda p, tqh: att0(p, tqh) + 130
                mk_exp = lambda p, tqh, it: acb(p) + tqh * 66 + it + 1
                mk_rcp = lambda p, tqh: acb(p) + tqh * 66 + 66
                mmb = lambda p: mm2 + MMP * p
                acb = lambda p: ac2 + ACP * p
                dvb = lambda p: dv2 + DVP * p
                mk_k1e = lambda p, st: dvb(p) + st + 1
                mk_q1e = lambda p, st: dvb(p) + 8 + st + 1
                mk_v1e = lambda p, c: dvb(p) + 12 + c + 1
                mk_nr = lambda p, tqh, j: dvb(p) + 44 + 3 * tqh + j
                pslot = lambda p, tqh, j: ((p * 2 + tqh) * 64 + j) % 3
                pldmark = {}
                sgmark = {}

                @block.gpsimd
                def _(g):
                    g.wait_ge(mm.h, p1_mm)
                    g.wait_ge(dv.h, p1_dv)
                    for p in range(8):
                        if p >= 2:
                            g.wait_ge(mm.h, mk_v1(p - 2, 31))
                        r = p % 2
                        for fc in range(8):
                            fr = slice(fc * 128, (fc + 1) * 128)
                            gpp[r].dinc(g.dma_start(out=wqp[:, r, fc, :],
                                                    in_=dwqkv[fr, 128 * p:128 * p + 128]))
                            gpp[r].dinc(g.dma_start(out=wkp[:, r, fc, :],
                                                    in_=dwqkv[fr, D + 128 * p:D + 128 * p + 128]))
                            gpp[r].dinc(g.dma_start(
                                out=wvp[:, r, fc, :],
                                in_=dwqkv[fr, 2 * D + 128 * p:2 * D + 128 * p + 128]))
                        pldmark[p] = gpp[r].n

                @block.sync
                def _(sync):
                    for p in range(8):
                        for tqh in range(2):
                            qsl = slice(tqh * 1024, (tqh + 1) * 1024)
                            sync.wait_ge(dv.h, mk_nr(p, tqh, 3))
                            smg[tqh].dinc(sync.dma_start(out=ctxn[64:128, p, qsl],
                                                         in_=stgb[:, tqh, :]))
                            sgmark[(p, tqh)] = smg[tqh].n

                @block.tensor
                def _(t):
                    for p in range(8):
                        r = p % 2
                        t.wait_ge(gpp[p % 2].h, pldmark[p])
                        t.wait_ge(dv.h, dv2 if p == 0 else mk_nr(p - 1, 1, 1))
                        for st in range(8):
                            ps = sps0[:, 0:512] if st % 2 == 0 else sps1[:, 0:512]
                            if st >= 2:
                                t.wait_ge(dv.h, mk_k1e(p, st - 2))
                            for fc in range(8):
                                ins = t.matmul(ps, wkp[:, r, fc, :],
                                               x1_fm[:, fc, st * 512:(st + 1) * 512],
                                               start=(fc == 0), stop=(fc == 7))
                            mm.inc(ins)
                        for st in range(4):
                            ps = sps0[:, 0:512] if st % 2 == 0 else sps1[:, 0:512]
                            t.wait_ge(dv.h, mk_k1e(p, 6 + st) if st < 2
                                      else mk_q1e(p, st - 2))
                            for fc in range(8):
                                ins = t.matmul(ps, wqp[:, r, fc, :],
                                               x1_fm[:, fc, st * 512:(st + 1) * 512],
                                               start=(fc == 0), stop=(fc == 7))
                            mm.inc(ins)
                        for c in range(32):
                            ps = sps0[:, 0:128] if c % 2 == 0 else sps1[:, 0:128]
                            t.wait_ge(dv.h, mk_q1e(p, 2 + c) if c < 2
                                      else mk_v1e(p, c - 2))
                            for fc in range(8):
                                ins = t.matmul(ps,
                                               x1_fm[:, fc, c * 128:(c + 1) * 128],
                                               wvp[:, r, fc, :],
                                               start=(fc == 0), stop=(fc == 7))
                            mm.inc(ins)
                        t.wait_ge(dv.h, mk_v1e(p, 31))

                        def pv(t, p, tqh, j):
                            jt, jc = j % 2, j // 2
                            slot = pslot(p, tqh, j)
                            nsl = slice(jt * 512, (jt + 1) * 512)
                            ins = t.matmul(ctxps_a[:, nsl],
                                           v1p[:, jc, 0:128],
                                           pab[:, slot, 0:512],
                                           start=(jc == 0), stop=(jc == 31))
                            ins = t.matmul(ctxps_b[:, nsl],
                                           v1p[:, jc, 65:193],
                                           pab[:, slot, 512:1024],
                                           start=(jc == 0), stop=(jc == 31))
                            mm.inc(ins)

                        for tqh in range(2):
                            for it in range(64):
                                tile, c = it % 2, it // 2
                                ping = it % 2
                                if it >= 2:
                                    t.wait_ge(ac.h, mk_exp(p, tqh, it - 2))
                                elif tqh == 1:
                                    t.wait_ge(ac.h, mk_exp(p, 0, 62 + it))
                                    if it == 0:
                                        t.wait_ge(dv.h, mk_nr(p, 0, 1))
                                csl = slice(c * 128, (c + 1) * 128)
                                n0 = tqh * 1024 + tile * 512
                                t.matmul(spsl[ping][:, 0:512],
                                         k1p[0:64, csl], q1p[0:64, n0:n0 + 512],
                                         start=True, stop=True, tile_position=(0, 0))
                                ins = t.matmul(spsl[ping][:, 512:1024],
                                               k1p[64:128, csl],
                                               q1p[64:128, n0:n0 + 512],
                                               start=True, stop=True,
                                               tile_position=(64, 0))
                                mm.inc(ins)
                                if it >= 2:
                                    if it == 2 and tqh == 1:
                                        t.wait_ge(dv.h, mk_nr(p, 0, 3))
                                    pv(t, p, tqh, it - 2)
                            for j in (62, 63):
                                t.wait_ge(ac.h, mk_exp(p, tqh, j))
                                pv(t, p, tqh, j)
                            t.wait_ge(ac.h, mk_rcp(p, tqh))
                            for nt in range(2):
                                ins = t.matmul(sps0[0:64, nt * 512:(nt + 1) * 512],
                                               ones2[64:65, 0:64],
                                               rabf[64:65, nt * 512:(nt + 1) * 512],
                                               start=True, stop=True,
                                               tile_position=(64, 0))
                            mm.inc(ins)
                            for nt in range(2):
                                ins = t.matmul(sps1[0:64, nt * 512:(nt + 1) * 512],
                                               ones2[64:65, 0:64],
                                               rabf[64:65, 1024 + nt * 512:1024 + (nt + 1) * 512],
                                               start=True, stop=True,
                                               tile_position=(64, 0))
                            mm.inc(ins)
                assert mm.n == mm2 + 8 * MMP

                @block.scalar
                def _(sc):
                    for p in range(8):
                        for tqh in range(2):
                            for it in range(64):
                                g = (p * 2 + tqh) * 64 + it
                                ping = it % 2
                                sc.wait_ge(mm.h, mk_sc(p, tqh, it))
                                if g >= 3:
                                    gj = g - 3
                                    sc.wait_ge(mm.h, mk_pv(gj // 128,
                                                           (gj // 64) % 2, gj % 64))
                                ac.inc(sc.activation(out=pab[:, g % 3, :],
                                                     in_=spsl[ping][:],
                                                     func=AF.Exp, scale=MSA_SCALE,
                                                     bias=mnegb[:]))
                            sc.wait_ge(mm.h, mk_pv(p, tqh, 63))
                            sc.activation(out=rln[64:65, 0:1024],
                                          in_=ctxps_a[64:65, :], func=AF.Ln)
                            ac.inc(sc.activation(out=rabf[64:65, 0:1024],
                                                 in_=rln[64:65, 0:1024],
                                                 func=AF.Exp, scale=-1.0))
                            sc.activation(out=rln[64:65, 1024:2048],
                                          in_=ctxps_b[64:65, :], func=AF.Ln)
                            ac.inc(sc.activation(out=rabf[64:65, 1024:2048],
                                                 in_=rln[64:65, 1024:2048],
                                                 func=AF.Exp, scale=-1.0))
                assert ac.n == ac2 + 8 * ACP

                @block.vector
                def _(v):
                    for p in range(8):
                        for st in range(8):
                            ps = sps0[:, 0:512] if st % 2 == 0 else sps1[:, 0:512]
                            v.wait_ge(mm.h, mk_k1(p, st))
                            dv.inc(v.tensor_copy(k1p[:, st * 512:(st + 1) * 512], ps))
                        for st in range(4):
                            ps = sps0[:, 0:512] if st % 2 == 0 else sps1[:, 0:512]
                            v.wait_ge(mm.h, mk_q1(p, st))
                            dv.inc(v.tensor_copy(q1p[:, st * 512:(st + 1) * 512], ps))
                        for c in range(32):
                            ps = sps0[:, 0:128] if c % 2 == 0 else sps1[:, 0:128]
                            v.wait_ge(mm.h, mk_v1(p, c))
                            v.tensor_copy(v1p[:, c, 0:64], ps[:, 0:64])
                            dv.inc(v.tensor_copy(v1p[:, c, 65:129], ps[:, 64:128]))
                        for tqh in range(2):
                            qsl = slice(tqh * 1024, (tqh + 1) * 1024)
                            v.wait_ge(mm.h, mk_bpa(p, tqh))
                            v.tensor_copy(rbc[:, 0, :], sps0[0:64, :])
                            v.wait_ge(mm.h, mk_bpb(p, tqh))
                            if p >= 1:
                                v.wait_ge(smg[tqh].h, sgmark[(p - 1, tqh)])
                            dv.inc(v.tensor_copy(rbc[:, 1, :], sps1[0:64, :]))
                            v.wait_ge(dv.h, mk_nr(p, tqh, 1))
                            dv.inc(v.tensor_tensor(ctxn[0:64, p, qsl],
                                                   ctxps_a[0:64, :], rbc[:, 0, :], op=ALU.mult))
                            dv.inc(v.tensor_tensor(stgb[:, tqh, :],
                                                   ctxps_b[0:64, :], rbc[:, 1, :], op=ALU.mult))
                assert dv.n == dv2 + 8 * DVP
            p2_dv, p2_mm, p2_ac = dv.n, mm.n, ac.n

            if until == 2:
                @block.sync
                def _(sync):
                    sync.wait_ge(dv.h, dv.n)
                    sync.wait_ge(mm.h, mm.n)
                    sync.wait_ge(ac.h, ac.n)
                    sync.wait_ge(gp.h, gp.n)
                    sy.dinc(sync.dma_start(out=ddbg, in_=kv_sb[:]))
                    sync.wait_ge(sy.h, sy.n)
                return nc

        outacc = top.enter_context(nc.sbuf_tensor("outacc", [128, NCO, D], F32, side="left"))
        x2_fm = top.enter_context(nc.sbuf_tensor("x2fm", [128, 8, M], BF16, side="left"))

        if "ctxn" in dtaps:
            @block.sync
            def _(sync):
                sync.wait_ge(dv.h, p2_dv)
                sy.dinc(sync.dma_start(out=dtaps["ctxn"], in_=ctxn[:]))

        with ExitStack() as p34:
            R3 = lambda n, sh, dt=F32: p34.enter_context(nc.sbuf_tensor(n, sh, dt, side="right"))
            wp_sb = R3("wp_sb", [128, 8, D], BF16)
            gw_sb = R3("gw_sb", [128, 8, NE], BF16)
            xbuf2 = R3("xbuf2", [128, 2, D])
            sqd2 = R3("sqd2", [128, D], BF16)
            st2 = R3("st2", [128, 2, 16])
            xn2 = R3("xn2", [128, 2, D])
            x2bf = R3("x2bf", [128, 2, D], BF16)
            lgs = R3("lgs", [128, 2, NE])
            rmx2 = R3("rmx2", [128, 2, 1])

            msaps = p34.enter_context(nc.psum_tensor("msaps", [128, 2, D], F32))
            x2T = p34.enter_context(nc.psum_tensor("x2T", [128, D], BF16))
            lgps = p34.enter_context(nc.psum_tensor("lgps", [128, NE], F32))

            @block.gpsimd
            def _(g):
                g.wait_ge(mm.h, p2_mm)
                g.wait_ge(dv.h, p2_dv)
                g.wait_ge(ac.h, p2_ac)
                g.wait_ge(smg[0].h, sgmark[(7, 0)])
                g.wait_ge(smg[1].h, sgmark[(7, 1)])
                for p in range(8):
                    gp.dinc(g.dma_start(out=wp_sb[:, p, :], in_=dwproj[:, p, :]))
                for fc in range(8):
                    gp.dinc(g.dma_start(out=gw_sb[:, fc, :],
                                        in_=dgatew[fc * 128:(fc + 1) * 128, :]))
            p34_gp = gp.n

            mm3, ac3, dv3 = mm.n, ac.n, dv.n
            xmark3 = {}
            n_var2 = lambda c: dv3 + 6 * c + 1
            n_y1 = lambda c: dv3 + 6 * c + 2
            n_var3 = lambda c: dv3 + 6 * c + 3
            n_x2bf = lambda c: dv3 + 6 * c + 4
            n_x2fm = lambda c: dv3 + 6 * c + 5
            n_mask = lambda c: dv3 + 6 * c + 6
            n_sq2 = lambda c: ac3 + 6 * c + 1
            n_rstd2 = lambda c: ac3 + 6 * c + 3
            n_sqy = lambda c: ac3 + 6 * c + 4
            n_rstd3 = lambda c: ac3 + 6 * c + 6
            n_proj = lambda c: mm3 + 3 * c + 1
            n_x2T = lambda c: mm3 + 3 * c + 2
            n_lg = lambda c: mm3 + 3 * c + 3

            @block.sync
            def _(sync):
                sync.wait_ge(mm.h, p2_mm)
                sync.wait_ge(dv.h, p2_dv)
                sync.wait_ge(ac.h, p2_ac)
                for c in range(NCO):
                    if c >= 2:
                        sync.wait_ge(dv.h, n_y1(c - 2))
                        sync.wait_ge(ac.h, n_sq2(c - 2))
                    smx[c % 2].dinc(sync.dma_start(out=xbuf2[:, c % 2, :],
                                                   in_=dx[c * 128:(c + 1) * 128, :]))
                    xmark3[c] = smx[c % 2].n

            @block.scalar
            def _(sc):
                for c in range(NCO):
                    r = c % 2
                    sc.wait_ge(smx[r].h, xmark3[c])
                    ac.inc(sc.activation(out=sqd2[:], in_=xbuf2[:, r, :],
                                         func=AF.Square, accum_out=st2[:, r, 0:1]))
                    sc.wait_ge(dv.h, n_var2(c))
                    ac.inc(sc.activation(out=st2[:, r, 6:7], in_=st2[:, r, 5:6],
                                         func=AF.Ln, bias=epsc[:]))
                    ac.inc(sc.activation(out=st2[:, r, 7:8], in_=st2[:, r, 6:7],
                                         func=AF.Exp, scale=-0.5))
                    sc.wait_ge(dv.h, n_y1(c))
                    ac.inc(sc.activation(out=sqd2[:], in_=outacc[:, c, :],
                                         func=AF.Square, accum_out=st2[:, r, 8:9]))
                    sc.wait_ge(dv.h, n_var3(c))
                    ac.inc(sc.activation(out=st2[:, r, 14:15], in_=st2[:, r, 13:14],
                                         func=AF.Ln, bias=epsc[:]))
                    ac.inc(sc.activation(out=st2[:, r, 15:16], in_=st2[:, r, 14:15],
                                         func=AF.Exp, scale=-0.5))
            assert ac.n == ac3 + 6 * NCO

            @block.tensor
            def _(t):
                t.wait_ge(gp.h, p34_gp)
                t.wait_ge(dv.h, p2_dv)
                t.wait_ge(smg[0].h, sgmark[(7, 0)])
                t.wait_ge(smg[1].h, sgmark[(7, 1)])
                for c in range(NCO):
                    r = c % 2
                    if c >= 2:
                        t.wait_ge(dv.h, n_y1(c - 2))
                    csl = slice(c * 128, (c + 1) * 128)
                    for nt in range(2):
                        nsl = slice(nt * 512, (nt + 1) * 512)
                        for p in range(8):
                            ins = t.matmul(msaps[:, r, nsl],
                                           ctxn[:, p, csl],
                                           wp_sb[:, p, nsl],
                                           start=(p == 0), stop=(p == 7))
                    mm.inc(ins)
                    t.wait_ge(dv.h, n_x2bf(c))
                    if c >= 1:
                        t.wait_ge(dv.h, n_x2fm(c - 1))
                    for j in range(8):
                        ins = t.transpose(x2T[:, j * 128:(j + 1) * 128],
                                          x2bf[:, r, j * 128:(j + 1) * 128], ident[:])
                    mm.inc(ins)
                    t.wait_ge(dv.h, n_x2fm(c))
                    if c >= 1:
                        t.wait_ge(dv.h, n_mask(c - 1))
                    for fc in range(8):
                        ins = t.matmul(lgps[:], x2_fm[:, fc, csl],
                                       gw_sb[:, fc, :], start=(fc == 0), stop=(fc == 7))
                    mm.inc(ins)
            assert mm.n == mm3 + 3 * NCO

            @block.vector
            def _(v):
                v.wait_ge(dv.h, p2_dv)
                for c in range(NCO):
                    r = c % 2
                    v.wait_ge(smx[r].h, xmark3[c])
                    vs.inc(v.reduce_sum(st2[:, r, 1:2], xbuf2[:, r, :], axis=AX.X))
                    c_s = vs.n
                    v.wait_ge(ac.h, n_sq2(c))
                    vs.inc(v.tensor_scalar_mul(st2[:, r, 4:5], st2[:, r, 0:1], 1.0 / D))
                    v.wait_ge(vs.h, c_s)
                    vs.inc(v.tensor_scalar_mul(st2[:, r, 2:3], st2[:, r, 1:2], -1.0 / D))
                    v.wait_ge(vs.h, vs.n)
                    vs.inc(v.tensor_mul(st2[:, r, 3:4], st2[:, r, 2:3], st2[:, r, 2:3]))
                    v.wait_ge(vs.h, vs.n)
                    dv.inc(v.tensor_sub(st2[:, r, 5:6], st2[:, r, 4:5], st2[:, r, 3:4]))
                    v.wait_ge(ac.h, n_rstd2(c))
                    vs.inc(v.tensor_scalar(xn2[:, r, :], xbuf2[:, r, :], st2[:, r, 2:3],
                                           st2[:, r, 7:8], op0=ALU.add, op1=ALU.mult))
                    v.wait_ge(mm.h, n_proj(c))
                    v.wait_ge(vs.h, vs.n)
                    dv.inc(v.tensor_add(outacc[:, c, :], xn2[:, r, :], msaps[:, r, :]))
                    v.wait_ge(dv.h, n_y1(c))
                    vs.inc(v.reduce_sum(st2[:, r, 9:10], outacc[:, c, :], axis=AX.X))
                    c_s2 = vs.n
                    v.wait_ge(ac.h, n_sqy(c))
                    vs.inc(v.tensor_scalar_mul(st2[:, r, 12:13], st2[:, r, 8:9], 1.0 / D))
                    v.wait_ge(vs.h, c_s2)
                    vs.inc(v.tensor_scalar_mul(st2[:, r, 10:11], st2[:, r, 9:10], -1.0 / D))
                    v.wait_ge(vs.h, vs.n)
                    vs.inc(v.tensor_mul(st2[:, r, 11:12], st2[:, r, 10:11], st2[:, r, 10:11]))
                    v.wait_ge(vs.h, vs.n)
                    dv.inc(v.tensor_sub(st2[:, r, 13:14], st2[:, r, 12:13],
                                        st2[:, r, 11:12]))
                    v.wait_ge(ac.h, n_rstd3(c))
                    vs.inc(v.tensor_scalar(xn2[:, r, :], outacc[:, c, :], st2[:, r, 10:11],
                                           st2[:, r, 15:16], op0=ALU.add, op1=ALU.mult))
                    v.wait_ge(vs.h, vs.n)
                    vs.inc(v.tensor_mul(xn2[:, r, :], xn2[:, r, :], w2ln_bc[:]))
                    v.wait_ge(vs.h, vs.n)
                    dv.inc(v.tensor_tensor(x2bf[:, r, :], xn2[:, r, :], b2ln_bc[:],
                                           op=ALU.add))
                    v.wait_ge(mm.h, n_x2T(c))
                    dv.inc(v.tensor_copy(
                        x2_fm[:, :, c * 128:(c + 1) * 128],
                        x2T[:].rearrange("p (a b) -> p a b", a=8)))
                    v.wait_ge(mm.h, n_lg(c))
                    vs.inc(v.tensor_add(lgs[:, r, :], lgps[:], gb_bc[:]))
                    v.wait_ge(vs.h, vs.n)
                    vs.inc(v.reduce_max(rmx2[:, r, :], lgs[:, r, :], axis=AX.X))
                    v.wait_ge(vs.h, vs.n)
                    dv.inc(v.tensor_single_scalar(masks[:, c, :], lgs[:, r, :],
                                                  rmx2[:, r, :], op=ALU.is_equal))
            assert dv.n == dv3 + 6 * NCO
        p34_dv, p34_mm, p34_ac = dv.n, mm.n, ac.n

        if until == 3:
            @block.sync
            def _(sync):
                sync.wait_ge(dv.h, dv.n)
                sync.wait_ge(mm.h, mm.n)
                sync.wait_ge(ac.h, ac.n)
                sync.wait_ge(gp.h, gp.n)
                sy.dinc(sync.dma_start(out=ddbg, in_=kv_sb[:]))
                sync.wait_ge(sy.h, sy.n)
            return nc

        if "y1" in dtaps:
            @block.sync
            def _(sync):
                sync.wait_ge(dv.h, p34_dv)
                sy.dinc(sync.dma_start(out=dtaps["y1"], in_=outacc[:]))
                sy.dinc(sync.dma_start(out=dtaps["x2fm"], in_=x2_fm[:]))
                sy.dinc(sync.dma_start(out=dtaps["masks"], in_=masks[:]))

        pctx.close()
        p5_sy_taps = sy.n
        with ExitStack() as p5:
            w1sb = p5.enter_context(nc.sbuf_tensor("w1sb", [128, 8, 1024], BF16, side="right"))
            w2sb = p5.enter_context(nc.sbuf_tensor("w2sb", [128, 8, 1024], BF16, side="right"))
            h_sb = p5.enter_context(nc.sbuf_tensor("h_sb", [128, 8, M], BF16, side="right"))

            mm5, ac5, dv5 = mm.n, ac.n, dv.n
            MM5, AC5, DV5 = 8 + 16, 8, 16
            i_of = lambda e, q: e * NQ + q
            w1mark, w2mark = {}, {}
            w1mk = lambda i, hc: mm5 + MM5 * i + hc + 1
            w2mk = lambda i, tt: mm5 + MM5 * i + 8 + tt + 1
            gemk = lambda i, hc: ac5 + AC5 * i + hc + 1
            mamk = lambda i, tt: dv5 + DV5 * i + tt + 1

            @block.gpsimd
            def _(g):
                g.wait_ge(dv.h, p34_dv)
                for e in range(NE):
                    for q in range(NQ):
                        i = i_of(e, q)
                        if i >= 1:
                            g.wait_ge(mm.h, w1mk(i - 1, 7))
                        for fc in range(8):
                            gp1.dinc(g.dma_start(
                                out=w1sb[:, fc, :],
                                in_=dw1q[e, q, fc * 128:(fc + 1) * 128, :]))
                        w1mark[i] = gp1.n
                        if i >= 1:
                            g.wait_ge(mm.h, w2mk(i - 1, 15))
                        for hc in range(8):
                            gp2.dinc(g.dma_start(
                                out=w2sb[:, hc, :],
                                in_=dw2q[e, q, hc * 128:(hc + 1) * 128, :]))
                        w2mark[i] = gp2.n

            with ExitStack() as p5p:
                hps = p5p.enter_context(nc.psum_tensor("hps", [128, 2, M], F32))

                @block.tensor
                def _(t):
                    t.wait_ge(dv.h, p34_dv)
                    for e in range(NE):
                        for q in range(NQ):
                            i = i_of(e, q)
                            t.wait_ge(gp1.h, w1mark[i])
                            if i >= 1:
                                t.wait_ge(dv.h, mamk(i - 1, 15))
                            for hc in range(8):
                                if hc >= 2:
                                    t.wait_ge(ac.h, gemk(i, hc - 2))
                                for nt in range(4):
                                    nsl = slice(nt * 512, (nt + 1) * 512)
                                    for fc in range(8):
                                        ins = t.matmul(
                                            hps[:, hc % 2, nsl],
                                            w1sb[:, fc, hc * 128:(hc + 1) * 128],
                                            x2_fm[:, fc, nsl],
                                            start=(fc == 0), stop=(fc == 7))
                                mm.inc(ins)
                            t.wait_ge(gp2.h, w2mark[i])
                            t.wait_ge(ac.h, gemk(i, 7))
                            for tt in range(NCO):
                                ysl = (tt % 2)
                                if tt >= 2:
                                    t.wait_ge(dv.h, mamk(i, tt - 2))
                                for nt in range(2):
                                    nsl = slice(ysl * 1024 + nt * 512,
                                                ysl * 1024 + (nt + 1) * 512)
                                    for hc in range(8):
                                        ins = t.matmul(
                                            hps[:, 0, nsl],
                                            h_sb[:, hc, tt * 128:(tt + 1) * 128],
                                            w2sb[:, hc, nt * 512:(nt + 1) * 512],
                                            start=(hc == 0), stop=(hc == 7))
                                mm.inc(ins)
                assert mm.n == mm5 + MM5 * NE * NQ

                @block.scalar
                def _(sc):
                    for e in range(NE):
                        for q in range(NQ):
                            i = i_of(e, q)
                            for hc in range(8):
                                sc.wait_ge(mm.h, w1mk(i, hc))
                                if i >= 1:
                                    sc.wait_ge(mm.h, w2mk(i - 1, 15))
                                ac.inc(sc.activation(out=h_sb[:, hc, :],
                                                     in_=hps[:, hc % 2, :],
                                                     func=AF.Gelu_apprx_tanh))
                assert ac.n == ac5 + AC5 * NE * NQ

                @block.vector
                def _(v):
                    v.wait_ge(dv.h, p34_dv)
                    for e in range(NE):
                        for q in range(NQ):
                            i = i_of(e, q)
                            for tt in range(NCO):
                                ysl = (tt % 2)
                                v.wait_ge(mm.h, w2mk(i, tt))
                                if i >= 1:
                                    v.wait_ge(dv.h, mamk(i - 1, tt))
                                dv.inc(v.scalar_tensor_tensor(
                                    outacc[:, tt, :],
                                    hps[:, 0, ysl * 1024:(ysl + 1) * 1024],
                                    masks[:, tt, e:e + 1],
                                    outacc[:, tt, :],
                                    op0=ALU.mult, op1=ALU.add))
                assert dv.n == dv5 + DV5 * NE * NQ
        p5_dv, p5_mm, p5_ac = dv.n, mm.n, ac.n

        if until == 5:
            @block.sync
            def _(sync):
                sync.wait_ge(dv.h, dv.n)
                sync.wait_ge(mm.h, mm.n)
                sync.wait_ge(ac.h, ac.n)
                sync.wait_ge(gp.h, gp.n)
                sy.dinc(sync.dma_start(out=ddbg, in_=kv_sb[:]))
                sync.wait_ge(sy.h, sy.n)
            return nc

        if "acc5" in dtaps:
            @block.sync
            def _(sync):
                sync.wait_ge(dv.h, p5_dv)
                sy.dinc(sync.dma_start(out=dtaps["acc5"], in_=outacc[:]))

        with ExitStack() as p6:
            R6 = lambda n, sh, dt=F32: p6.enter_context(nc.sbuf_tensor(n, sh, dt, side="right"))
            wq6 = R6("wq6", [128, 8, D], BF16)
            wfq6 = R6("wfq6", [128, 8, D], BF16)
            xbuf3 = R6("xbuf3", [128, 3, D])
            sqd3 = R6("sqd3", [128, D], BF16)
            st3 = R6("st3", [128, 2, 8])
            xnbf3 = R6("xnbf3", [128, 2, D], BF16)
            xnfm3 = R6("xnfm3", [128, 2, D], BF16)
            qp1b = R6("qp1b", [128, 2, D], BF16)
            nmx6 = R6("nmx6", [128, 2, 1])
            qtbf = R6("qtbf", [128, 2, D], BF16)
            qfm = R6("qfm", [128, 2, D], BF16)
            rr6 = R6("rr6", [128, H])
            oc = R6("oc", [128, 2, D])

            xnT6 = p6.enter_context(nc.psum_tensor("xnT6", [128, D], BF16))
            qT6 = p6.enter_context(nc.psum_tensor("qT6", [128, D], BF16))
            psA = p6.enter_context(nc.psum_tensor("psA", [128, D], F32))
            ctx6 = [p6.enter_context(nc.psum_tensor(f"ctx6_{j}", [128, 2, 256], F32))
                    for j in range(4)]

            @block.gpsimd
            def _(g):
                g.wait_ge(mm.h, p5_mm)
                g.wait_ge(dv.h, p5_dv)
                for wsrc, wdst in ((dwqT, wq6), (dwfqT, wfq6)):
                    for fc in range(8):
                        gp.dinc(g.dma_start(out=wdst[:, fc, :],
                                            in_=wsrc[fc * 128:(fc + 1) * 128, :]))
            p6_gp = gp.n

            mm6, ac6, dv6 = mm.n, ac.n, dv.n
            dvm6 = {}
            _n6 = [dv6]

            def _rec6(*keys):
                for k in keys:
                    _n6[0] += 1
                    dvm6[k] = _n6[0]

            for _c in range(NCO):
                if _c == 0:
                    _rec6(("var", 0), ("var", 1), ("xnbf", 0))
                _rec6(("xnfm", _c))
                if _c >= 1:
                    _rec6(("qfme", _c - 1), ("asm", _c - 1))
                _rec6(("qp1e", _c))
                if _c + 2 < NCO:
                    _rec6(("var", _c + 2))
                if _c + 1 < NCO:
                    _rec6(("xnbf", _c + 1))
            _rec6(("qfme", NCO - 1), ("asm", NCO - 1))
            o_var = lambda c: dvm6[("var", c)]
            o_xnbf = lambda c: dvm6[("xnbf", c)]
            o_xnfm = lambda c: dvm6[("xnfm", c)]
            o_qp1e = lambda c: dvm6[("qp1e", c)]
            o_qfme = lambda c: dvm6[("qfme", c)]
            o_asm = lambda c: dvm6[("asm", c)]
            o_sq = lambda j: ac6 + (1 if j == 0 else 2 if j == 1 else 4 * j - 2)
            o_rstd = lambda c: ac6 + (4 if c == 0 else
                                      4 * c + 4 if c < NCO - 1 else 4 * NCO - 1)
            o_qexp = lambda c: ac6 + (5 if c == 0 else
                                      4 * c + 5 if c < NCO - 1 else 4 * NCO)
            o_xnT = lambda c: mm6 + (1 if c == 0 else 5 * c - 1)
            o_qp1 = lambda c: mm6 + (2 if c == 0 else 5 * c + 2)
            o_qp2 = lambda c: mm6 + (3 if c == 0 else 5 * c + 3)
            o_qT = lambda c: mm6 + (5 * c + 5 if c < NCO - 1 else 5 * NCO - 1)
            o_ctx = lambda c: mm6 + (5 * c + 6 if c < NCO - 1 else 5 * NCO)
            sy_marks = {}

            @block.gpsimd
            def _(g):
                for c in range(NCO):
                    g.wait_ge(dv.h, o_asm(c))
                    smy[c % 2].dinc(g.dma_start(out=dy[c * 128:(c + 1) * 128, :],
                                                in_=oc[:, c % 2, :]))
                    sy_marks[("y", c)] = smy[c % 2].n

            @block.sync
            def _(sync):
                sync.wait_ge(dv.h, p5_dv)
                for c in range(NCO):
                    if c >= 3:
                        sync.wait_ge(dv.h, o_asm(c - 3))
                        sync.wait_ge(ac.h, o_sq(c - 3))
                    smx[c % 2].dinc(sync.dma_start(out=xbuf3[:, c % 3, :],
                                                   in_=dx[c * 128:(c + 1) * 128, :]))
                    sy_marks[("x", c)] = smx[c % 2].n
                sync.wait_ge(smy[0].h, smy[0].n)
                sync.wait_ge(smy[1].h, smy[1].n)
                if sy.n:
                    sync.wait_ge(sy.h, sy.n)

            @block.scalar
            def _(sc):
                def sq6(sc, j):
                    rj = j % 2
                    sc.wait_ge(smx[rj].h, sy_marks[("x", j)])
                    ac.inc(sc.activation(out=sqd3[:], in_=xbuf3[:, j % 3, :],
                                         func=AF.Square, accum_out=st3[:, rj, 0:1]))

                for c in range(NCO):
                    r = c % 2
                    if c == 0:
                        sq6(sc, 0)
                        sq6(sc, 1)
                    elif c + 1 < NCO:
                        sq6(sc, c + 1)
                    sc.wait_ge(dv.h, o_var(c))
                    ac.inc(sc.activation(out=st3[:, r, 6:7], in_=st3[:, r, 5:6],
                                         func=AF.Ln, bias=epsc[:]))
                    ac.inc(sc.activation(out=st3[:, r, 7:8], in_=st3[:, r, 6:7],
                                         func=AF.Exp, scale=-0.5))
                    sc.wait_ge(mm.h, o_qp2(c))
                    ac.inc(sc.activation(out=qtbf[:, r, :], in_=psA[:],
                                         func=AF.Exp, bias=kqb[:]))
            assert ac.n == ac6 + 4 * NCO

            @block.tensor
            def _(t):
                t.wait_ge(gp.h, p6_gp)

                def qt_ctx(cc):
                    rr = cc % 2
                    t.wait_ge(ac.h, o_qexp(cc))
                    if cc >= 1:
                        t.wait_ge(dv.h, o_qfme(cc - 1))
                    for j in range(8):
                        ins = t.transpose(qT6[:, j * 128:(j + 1) * 128],
                                          qtbf[:, rr, j * 128:(j + 1) * 128], ident[:])
                    mm.inc(ins)
                    t.wait_ge(dv.h, o_qfme(cc))
                    if cc >= 1:
                        t.wait_ge(dv.h, o_asm(cc - 1))
                    for p in range(8):
                        ins = t.matmul(ctx6[p // 2][:, p % 2, 0:130],
                                       qfm[:, rr, p * 128:(p + 1) * 128],
                                       kv_sb[:, p, 0:130],
                                       start=True, stop=True)
                    mm.inc(ins)

                for c in range(NCO):
                    r = c % 2
                    t.wait_ge(dv.h, o_xnbf(c))
                    if c >= 1:
                        t.wait_ge(dv.h, o_xnfm(c - 1))
                    for j in range(8):
                        ins = t.transpose(xnT6[:, j * 128:(j + 1) * 128],
                                          xnbf3[:, r, j * 128:(j + 1) * 128], ident[:])
                    mm.inc(ins)
                    if c >= 1:
                        qt_ctx(c - 1)
                    t.wait_ge(dv.h, o_xnfm(c))
                    if c >= 1:
                        t.wait_ge(ac.h, o_qexp(c - 1))
                    for jc in range(8):
                        sl = slice(jc * 128, (jc + 1) * 128)
                        for fc in range(8):
                            t.matmul(psA[:, sl], wq6[:, fc, sl],
                                     xnfm3[:, r, fc * 128:(fc + 1) * 128],
                                     start=(fc == 0), stop=False)
                        ins = t.matmul(psA[:, sl], bq_bf[:, sl],
                                       ones2[0:1, 0:128], start=False, stop=True)
                    mm.inc(ins)
                    t.wait_ge(dv.h, o_qp1e(c))
                    for nt in range(2):
                        sl = slice(nt * 512, (nt + 1) * 512)
                        for jc in range(8):
                            t.matmul(psA[:, sl], qp1b[:, r, jc * 128:(jc + 1) * 128],
                                     wfq6[:, jc, sl], start=(jc == 0), stop=False)
                        ins = t.matmul(psA[:, sl], ones2[0:1, 0:128],
                                       bfq_bf[:, sl], start=False, stop=True)
                    mm.inc(ins)
                qt_ctx(NCO - 1)
            assert mm.n == mm6 + 5 * NCO

            @block.vector
            def _(v):
                v.wait_ge(dv.h, p5_dv)

                def qfm_asm(v, cc):
                    rr = cc % 2
                    v.wait_ge(mm.h, o_qT(cc))
                    dv.inc(v.tensor_copy(qfm[:, rr, :], qT6[:]))
                    if cc >= 2:
                        v.wait_ge(smy[rr].h, sy_marks[("y", cc - 2)])
                    vs.inc(v.tensor_add(oc[:, rr, :], xbuf3[:, cc % 3, :],
                                        outacc[:, cc, :]))
                    v.wait_ge(mm.h, o_ctx(cc))
                    for h in range(H):
                        p, side = h // 2, h % 2
                        cps = ctx6[p // 2]
                        ins = v.reciprocal(rr6[:, h:h + 1],
                                           cps[:, p % 2, 65 * side + 64:65 * side + 65])
                    vs.inc(ins)
                    v.wait_ge(vs.h, vs.n)
                    for h in range(H):
                        p, side = h // 2, h % 2
                        cps = ctx6[p // 2]
                        ins = v.scalar_tensor_tensor(
                            oc[:, rr, h * 64:(h + 1) * 64],
                            cps[:, p % 2, 65 * side:65 * side + 64],
                            rr6[:, h:h + 1], oc[:, rr, h * 64:(h + 1) * 64],
                            op0=ALU.mult, op1=ALU.add)
                    dv.inc(ins)

                def ln6a(v, j):
                    rj = j % 2
                    v.wait_ge(smx[rj].h, sy_marks[("x", j)])
                    vs.inc(v.reduce_sum(st3[:, rj, 1:2], xbuf3[:, j % 3, :],
                                        axis=AX.X))
                    c_s = vs.n
                    v.wait_ge(ac.h, o_sq(j))
                    vs.inc(v.tensor_scalar_mul(st3[:, rj, 4:5], st3[:, rj, 0:1],
                                               1.0 / D))
                    v.wait_ge(vs.h, c_s)
                    vs.inc(v.tensor_scalar_mul(st3[:, rj, 2:3], st3[:, rj, 1:2],
                                               -1.0 / D))
                    v.wait_ge(vs.h, vs.n)
                    vs.inc(v.tensor_mul(st3[:, rj, 3:4], st3[:, rj, 2:3],
                                        st3[:, rj, 2:3]))
                    v.wait_ge(vs.h, vs.n)
                    if j >= 2:
                        v.wait_ge(ac.h, o_rstd(j - 2))
                    dv.inc(v.tensor_sub(st3[:, rj, 5:6], st3[:, rj, 4:5],
                                        st3[:, rj, 3:4]))

                def ln6b(v, j):
                    rj = j % 2
                    v.wait_ge(ac.h, o_rstd(j))
                    if j >= 2:
                        v.wait_ge(mm.h, o_xnT(j - 2))
                    dv.inc(v.tensor_scalar(xnbf3[:, rj, :], xbuf3[:, j % 3, :],
                                           st3[:, rj, 2:3], st3[:, rj, 7:8],
                                           op0=ALU.add, op1=ALU.mult))

                for c in range(NCO):
                    r = c % 2
                    if c == 0:
                        ln6a(v, 0)
                        ln6a(v, 1)
                        ln6b(v, 0)
                    v.wait_ge(mm.h, o_xnT(c))
                    dv.inc(v.tensor_copy(xnfm3[:, r, :], xnT6[:]))
                    if c >= 1:
                        qfm_asm(v, c - 1)
                    v.wait_ge(mm.h, o_qp1(c))
                    dv.inc(v.tensor_copy(qp1b[:, r, :], psA[:]))
                    if c + 2 < NCO:
                        ln6a(v, c + 2)
                    if c + 1 < NCO:
                        ln6b(v, c + 1)
                qfm_asm(v, NCO - 1)
            assert dv.n == dv6 + 6 * NCO

    return nc


def _stage(inputs):
    g = lambda k: np.asarray(inputs["L0_" + k])
    x = g("attn_hidden_states")
    shared = {
        "wqT": np.ascontiguousarray(g("attn_Wq").T),
        "wkT": np.ascontiguousarray(g("attn_Wk").T),
        "wvT": np.ascontiguousarray(g("attn_Wv").T),
        "wfqT": np.ascontiguousarray(g("attn_Wfq").T),
        "wfkT": np.ascontiguousarray(g("attn_Wfk").T),
        "wqkv": np.ascontiguousarray(g("ffn_w_qkv")),
        "wprojp": np.ascontiguousarray(g("ffn_w_proj").reshape(8, 128, D).transpose(1, 0, 2)),
        "gatew": np.ascontiguousarray(g("ffn_gate_w")),
        "w1q": np.ascontiguousarray(g("ffn_w1").reshape(NE, D, NQ, 1024).transpose(0, 2, 1, 3)),
        "w2q": np.ascontiguousarray(g("ffn_w2").reshape(NE, NQ, 1024, D)),
        "bq": g("attn_bq").reshape(1, D), "bk": g("attn_bk").reshape(1, D),
        "bv": g("attn_bv").reshape(1, D), "bfq": g("attn_bfq").reshape(1, D),
        "bfk": g("attn_bfk").reshape(1, D),
        "ln1w": g("ffn_ln1_w").reshape(1, D), "ln1b": g("ffn_ln1_b").reshape(1, D),
        "ln2w": g("ffn_ln2_w").reshape(1, D), "ln2b": g("ffn_ln2_b").reshape(1, D),
        "gateb": g("ffn_gate_b").reshape(1, NE),
    }
    shared = {k: np.ascontiguousarray(v, dtype=np.float32) for k, v in shared.items()}
    in_maps = []
    for c in range(8):
        b, hf = c // 2, c % 2
        xb = x[b]
        xc = np.ascontiguousarray(np.concatenate(
            [xb[hf * M:(hf + 1) * M], xb[(1 - hf) * M:(2 - hf) * M]], axis=0),
            dtype=np.float32)
        im = dict(shared)
        im["x"] = xc
        in_maps.append(im)
    return in_maps


def kernel(**inputs):
    in_maps = _stage(inputs)
    nc = build_program()
    res = run_bass_kernel_spmd(nc, in_maps, core_ids=list(range(8)))
    out = np.empty((4, S, D), dtype=np.float32)
    for c in range(8):
        b, hf = c // 2, c % 2
        out[b, hf * M:(hf + 1) * M, :] = res.results[c]["y"]
    return out
